# version 02
# baseline (speedup 1.0000x reference)
import numpy as np
import ml_dtypes
from contextlib import ExitStack

import concourse.bass as bass
import concourse.mybir as mybir
from concourse.bass_utils import run_bass_kernel_spmd

F32 = mybir.dt.float32
BF16 = mybir.dt.bfloat16
U8 = mybir.dt.uint8
AF = mybir.ActivationFunctionType
ALU = mybir.AluOpType

H, P, N, G = 64, 64, 128, 8
I = H * P
GN = G * N
CONV_DIM = I + 2 * GN
PROJ = I + CONV_DIM + H
DM = 4096
EPS = 1e-5
T = 1024
TH = T + 3
THp = 1028
Q = 128
NCH = T // Q
DT_D = DM // 128


def build_program(phases=99, debug=False):
    nc = bass.Bass("TRN2", target_bir_lowering=False)
    ap = lambda t: t.ap()
    xT = ap(nc.dram_tensor("xT", [DM, TH], F32, kind="ExternalInput"))
    m1T = ap(nc.dram_tensor("m1T", [DM, TH], U8, kind="ExternalInput"))
    m2T = ap(nc.dram_tensor("m2T", [DM, TH], U8, kind="ExternalInput"))
    m3T = ap(nc.dram_tensor("m3T", [DM, TH], U8, kind="ExternalInput"))
    w0t = ap(nc.dram_tensor("w0t", [32, 128, DM], F32, kind="ExternalInput"))
    wpt = ap(nc.dram_tensor("wpt", [81, 128, DM], F32, kind="ExternalInput"))
    wot = ap(nc.dram_tensor("wot", [32, 128, DM], F32, kind="ExternalInput"))
    wat = ap(nc.dram_tensor("wat", [128, 2048], F32, kind="ExternalInput"))
    wbts = ap(nc.dram_tensor("wbts", [32, 16, 512], F32, kind="ExternalInput"))
    b0c = ap(nc.dram_tensor("b0c", [128, 32], F32, kind="ExternalInput"))
    convw = ap(nc.dram_tensor("convw", [128, 192], F32, kind="ExternalInput"))
    convb = ap(nc.dram_tensor("convb", [128, 48], F32, kind="ExternalInput"))
    dtbias = ap(nc.dram_tensor("dtbias", [128, 64], F32, kind="ExternalInput"))
    alogb = ap(nc.dram_tensor("alogb", [128, 64], F32, kind="ExternalInput"))
    dcolh = ap(nc.dram_tensor("dcolh", [128, 32], F32, kind="ExternalInput"))
    nwcolh = ap(nc.dram_tensor("nwcolh", [128, 32], F32, kind="ExternalInput"))
    identb = ap(nc.dram_tensor("identb", [128, 128], BF16, kind="ExternalInput"))
    identf = ap(nc.dram_tensor("identf", [128, 128], F32, kind="ExternalInput"))
    ones1f = ap(nc.dram_tensor("ones1f", [1, 128], F32, kind="ExternalInput"))
    onescolb = ap(nc.dram_tensor("onescolb", [128, 1], BF16, kind="ExternalInput"))
    onesrowb = ap(nc.dram_tensor("onesrowb", [1, 512], BF16, kind="ExternalInput"))
    ctrif = ap(nc.dram_tensor("ctrif", [128, 128], F32, kind="ExternalInput"))
    k01f = ap(nc.dram_tensor("k01f", [128, 128], F32, kind="ExternalInput"))
    halom = ap(nc.dram_tensor("halom", [128, 1], F32, kind="ExternalInput"))
    actrow = ap(nc.dram_tensor("actrow", [1, 256], F32, kind="ExternalInput"))
    epsr = ap(nc.dram_tensor("epsr", [1, 1], F32, kind="ExternalInput"))
    mselrow = ap(nc.dram_tensor("mselrow", [1, 256], F32, kind="ExternalInput"))

    outT = ap(nc.dram_tensor("outT", [DM, T], F32, kind="ExternalOutput"))

    KD = "ExternalOutput" if debug else "Internal"
    x1_scr = ap(nc.dram_tensor("x1_scr", [DM, THp], F32, kind=KD))
    x2_scr = ap(nc.dram_tensor("x2_scr", [DM, T], F32, kind=KD))
    gate_scr = ap(nc.dram_tensor("gate_scr", [DM, THp], BF16, kind=KD))
    xssm_scr = ap(nc.dram_tensor("xssm_scr", [DM, T], BF16, kind=KD))
    B_scr = ap(nc.dram_tensor("B_scr", [GN, T], BF16, kind=KD))
    C_scr = ap(nc.dram_tensor("C_scr", [GN, T], BF16, kind=KD))
    ca_scr = ap(nc.dram_tensor("ca_scr", [NCH, 8192], F32, kind=KD))
    cae_scr = ap(nc.dram_tensor("cae_scr", [NCH, 64], F32, kind="Internal"))
    coll_in = nc.dram_tensor("coll_in", [129, 4096], F32, kind="Internal")
    coll_out = nc.dram_tensor("coll_out", [516, 4096], F32, kind="Internal")

    es = ExitStack()
    with es:
        sbt = lambda name, shape, dt: es.enter_context(nc.sbuf_tensor(name, shape, dt))
        xn_a = sbt("xn_a", [128, DT_D * THp], BF16)
        wbuf = sbt("wbuf", [128, 2 * DM], BF16)
        wat_sb = sbt("wat_sb", [128, 2048], BF16)
        wbt_r = sbt("wbt_r", [16, 2 * 512], BF16)
        b0h_sb = sbt("b0h_sb", [128, 32], F32)
        identb_sb = sbt("identb_sb", [128, 128], BF16)
        identf_sb = sbt("identf_sb", [128, 128], F32)
        ones1f_sb = sbt("ones1f_sb", [1, 128], F32)
        onescolb_sb = sbt("onescolb_sb", [128, 1], BF16)
        onesrowb_sb = sbt("onesrowb_sb", [1, 512], BF16)
        ctrif_sb = sbt("ctrif_sb", [128, 128], F32)
        k01f_sb = sbt("k01f_sb", [128, 128], F32)
        convw_sb = sbt("convw_sb", [128, 192], F32)
        convb_sb = sbt("convb_sb", [128, 48], F32)
        dtbias_sb = sbt("dtbias_sb", [128, 64], F32)
        expA_sb = sbt("expA_sb", [128, 64], F32)
        dcol_sb = sbt("dcol_sb", [128, 32], F32)
        nwcol_sb = sbt("nwcol_sb", [128, 32], F32)
        halo_sb = sbt("halo_sb", [128, 1], F32)
        actrow_sb = sbt("actrow_sb", [1, 256], F32)
        mselrow_sb = sbt("mselrow_sb", [1, 256], F32)
        eps_sb = sbt("eps_sb", [1, 1], F32)
        dtp_all = sbt("dtp_all", [128, NCH * 64], F32)
        ca_all = sbt("ca_all", [128, NCH * 64], F32)
        caendb_all = sbt("caendb_all", [128, NCH * 64], F32)
        dAtot_all = sbt("dAtot_all", [128, NCH * 64], F32)
        aprod_sb = sbt("aprod_sb", [128, 64], F32)
        rows_t = sbt("rows_t", [128, THp], F32)
        a_bc = sbt("a_bc", [128, THp], F32)
        b_bc = sbt("b_bc", [128, THp], F32)
        t_scr = sbt("t_scr", [128, THp], F32)
        xsqtl = sbt("xsqtl", [128, 2 * THp], BF16)

        r_mu = rows_t[0:1, 0:THp]
        r_v = t_scr[0:1, 0:THp]

        class Sem:
            def __init__(self, name):
                self.h = es.enter_context(nc.semaphore(name))
                self.n = 0
        S_bar = Sem("bar"); S_wld = Sem("wld"); S_xld = Sem("xld")
        S_pe = Sem("pe"); S_dv = Sem("dv"); S_ac = Sem("ac")
        S_sto = Sem("sto"); S_col = Sem("col")
        rings = {}

        def ring(name, n=2):
            if name not in rings:
                rings[name] = [Sem(f"{name}{i}") for i in range(n)]
            return rings[name]

        def rsem(name, u, n=2):
            return ring(name, n)[u % n]


        blk = es.enter_context(nc.Block())
        nbar = [0]

        def barrier():
            nbar[0] += 1
            sems = [S_pe, S_dv, S_ac, S_col, S_xld, S_wld, S_sto]
            for dec in (blk.sync, blk.gpsimd, blk.tensor, blk.vector, blk.scalar):
                @dec
                def _(eng):
                    for q in sems:
                        if q.n:
                            eng.wait_ge(q.h, q.n)
                    for rl in rings.values():
                        for q in rl:
                            if q.n:
                                eng.wait_ge(q.h, q.n)


        def finish():
            @blk.sync
            def _(s):
                s.wait_ge(S_sto.h, S_sto.n)
                s.wait_ge(S_xld.h, S_xld.n)
                for sems in rings.values():
                    for q in sems:
                        if q.n:
                            s.wait_ge(q.h, q.n)

        @blk.sync
        def _(s):
            for dst, src in [
                (identb_sb, identb), (identf_sb, identf), (ones1f_sb, ones1f),
                (onescolb_sb, onescolb), (onesrowb_sb, onesrowb),
                (ctrif_sb, ctrif), (k01f_sb, k01f), (convw_sb, convw),
                (convb_sb, convb), (dtbias_sb, dtbias), (dcol_sb, dcolh),
                (nwcol_sb, nwcolh), (halo_sb, halom),
                (actrow_sb, actrow), (mselrow_sb, mselrow), (eps_sb, epsr),
            ]:
                s.dma_start(out=dst[:], in_=src[:]).then_inc(S_xld.h, 16)
                S_xld.n += 16
            s.dma_start(out=expA_sb[:], in_=alogb[:]).then_inc(S_xld.h, 16)
            s.dma_start(out=b0h_sb[:], in_=b0c[:]).then_inc(S_xld.h, 16)
            S_xld.n += 32

        @blk.gpsimd
        def _(g):
            g.dma_start(out=wat_sb[:], in_=wat[:]).then_inc(S_wld.h, 16)
            S_wld.n += 16

        @blk.scalar
        def _(sc):
            sc.wait_ge(S_xld.h, S_xld.n)
            sc.wait_ge(S_wld.h, S_wld.n)
            sc.activation(expA_sb[:], expA_sb[:], AF.Exp).then_inc(S_ac.h, 1)
            S_ac.n += 1
        @blk.vector
        def _(v):
            v.wait_ge(S_ac.h, S_ac.n)
            _li = v.tensor_scalar(b0h_sb[:], b0h_sb[:], 0.5, None, op0=ALU.mult)
            _li.then_inc(S_dv.h, 1)
            S_dv.n += 1
        barrier()

        _uid = [0]

        def ln_stats(buf_bf, subw):
            _uid[0] += 1
            uid = _uid[0]
            ps = ExitStack()
            with ps:
                psl = ps.enter_context(nc.psum_tensor(f"psl_ln{uid}", [128, 512], F32))
                psl2 = ps.enter_context(nc.psum_tensor(f"psl2_ln{uid}", [128, 512], F32))
                for (off, w) in subw:
                    sq_marks = []
                    pe_base = S_pe.n
                    @blk.scalar
                    def _(sc, off=off, w=w, sq_marks=sq_marks, pe_base=pe_base):
                        for dt in range(DT_D):
                            sc.wait_ge(S_pe.h, pe_base + max(dt - 1, 0))
                            sc.activation(xsqtl[:, (dt % 2) * THp:(dt % 2) * THp + w],
                                          buf_bf[:, dt * THp + off: dt * THp + off + w],
                                          AF.Square).then_inc(S_ac.h, 1)
                            S_ac.n += 1
                            sq_marks.append(S_ac.n)
                    @blk.tensor
                    def _(t, off=off, w=w, sq_marks=sq_marks):
                        for dt in range(DT_D):
                            t.wait_ge(S_ac.h, sq_marks[dt])
                            t.matmul(psl[0:1, 0:w], onescolb_sb[:],
                                     buf_bf[:, dt * THp + off: dt * THp + off + w],
                                     start=(dt == 0), stop=(dt == DT_D - 1))
                            _li = t.matmul(psl2[0:1, 0:w], onescolb_sb[:],
                                     xsqtl[:, (dt % 2) * THp:(dt % 2) * THp + w],
                                     start=(dt == 0), stop=(dt == DT_D - 1))
                            _li.then_inc(S_pe.h, 1)
                            S_pe.n += 1
                    @blk.scalar
                    def _(sc, off=off, w=w):
                        sc.wait_ge(S_pe.h, S_pe.n)
                        sc.activation(r_mu[0:1, off:off + w], psl[0:1, 0:w], AF.Copy)
                        _li = sc.activation(r_v[0:1, off:off + w], psl2[0:1, 0:w], AF.Copy)
                        _li.then_inc(S_ac.h, 1)
                        S_ac.n += 1
                    @blk.tensor
                    def _(t):
                        t.wait_ge(S_ac.h, S_ac.n)

        def ln_fin_bcast_apply(buf_bf, W, subw):
            @blk.vector
            def _(v, W=W):
                v.wait_ge(S_ac.h, S_ac.n)
                v.tensor_scalar(r_mu[0:1, 0:W], r_mu[0:1, 0:W], -1.0 / DM, None, op0=ALU.mult)
                v.tensor_scalar(r_v[0:1, 0:W], r_v[0:1, 0:W], 1.0 / DM, None, op0=ALU.mult)
                v.tensor_tensor(a_bc[0:1, 0:W], r_mu[0:1, 0:W], r_mu[0:1, 0:W], op=ALU.mult)
                _li = v.tensor_tensor(r_v[0:1, 0:W], r_v[0:1, 0:W], a_bc[0:1, 0:W], op=ALU.subtract)
                _li.then_inc(S_dv.h, 1)
                S_dv.n += 1
            @blk.scalar
            def _(sc, W=W):
                sc.wait_ge(S_dv.h, S_dv.n)
                sc.activation(r_v[0:1, 0:W], r_v[0:1, 0:W], AF.Ln, bias=eps_sb[0:1, 0:1])
                _li = sc.activation(r_v[0:1, 0:W], r_v[0:1, 0:W], AF.Exp, scale=-0.5)
                _li.then_inc(S_ac.h, 1)
                S_ac.n += 1
            @blk.vector
            def _(v, W=W):
                v.wait_ge(S_ac.h, S_ac.n)
                _li = v.tensor_tensor(r_mu[0:1, 0:W], r_mu[0:1, 0:W], r_v[0:1, 0:W], op=ALU.mult)
                _li.then_inc(S_dv.h, 1)
                S_dv.n += 1
            ps = ExitStack()
            with ps:
                _uid[0] += 1
                uid = _uid[0]
                pba = ps.enter_context(nc.psum_tensor(f"pbca{uid}", [128, 512], F32))
                pbb = ps.enter_context(nc.psum_tensor(f"pbcb{uid}", [128, 512], F32))
                for (off, w) in subw:
                    @blk.tensor
                    def _(t, off=off, w=w):
                        t.wait_ge(S_dv.h, S_dv.n)
                        t.wait_ge(S_ac.h, S_ac.n)
                        t.matmul(pba[:, 0:w], ones1f_sb[:], r_v[0:1, off:off + w],
                                 start=True, stop=True)
                        _li = t.matmul(pbb[:, 0:w], ones1f_sb[:], r_mu[0:1, off:off + w],
                                 start=True, stop=True)
                        _li.then_inc(S_pe.h, 1)
                        S_pe.n += 1
                    @blk.scalar
                    def _(sc, off=off, w=w):
                        sc.wait_ge(S_pe.h, S_pe.n)
                        sc.activation(a_bc[:, off:off + w], pba[:, 0:w], AF.Copy)
                        _li = sc.activation(b_bc[:, off:off + w], pbb[:, 0:w], AF.Copy)
                        _li.then_inc(S_ac.h, 1)
                        S_ac.n += 1
            @blk.vector
            def _(v, W=W):
                v.wait_ge(S_ac.h, S_ac.n)
                for dt in range(DT_D):
                    c0 = dt * THp
                    v.tensor_tensor(t_scr[:, 0:W], buf_bf[:, c0:c0 + W], a_bc[:, 0:W], op=ALU.mult)
                    _li = v.tensor_tensor(buf_bf[:, c0:c0 + W], t_scr[:, 0:W], b_bc[:, 0:W], op=ALU.add)
                _li.then_inc(S_dv.h, 1)
                S_dv.n += 1

        subw1 = [(0, 512), (512, 512), (1024, TH - 1024)]
        ld_marks = []
        ac_base1 = S_ac.n
        xring = [a_bc, b_bc]
        @blk.sync
        def _(s):
            for dt in range(DT_D):
                if dt >= 2:
                    s.wait_ge(S_ac.h, ac_base1 + dt - 1)
                q = rsem("p1x", dt)
                s.dma_start(out=xring[dt % 2][:, 0:TH],
                            in_=xT[dt * 128:(dt + 1) * 128, :]).then_inc(q.h, 16)
                q.n += 16
                ld_marks.append((q.h, q.n))
        @blk.scalar
        def _(sc):
            for dt in range(DT_D):
                sc.wait_ge(*ld_marks[dt])
                sc.activation(xn_a[:, dt * THp: dt * THp + TH], xring[dt % 2][:, 0:TH],
                              AF.Copy).then_inc(S_ac.h, 1)
                S_ac.n += 1
        barrier()
        ln_stats(xn_a, subw1)
        ln_fin_bcast_apply(xn_a, TH, subw1)
        barrier()

        def emit_ffn(inbuf, res_src, out_dst, W, nxt_buf, tag):
            subw = [(0, 512), (512, 512)] + ([(1024, W - 1024)] if W > 1024 else [])
            nsub = len(subw)
            moff0 = 3 if W == T else 0
            ps = ExitStack()
            with ps:
                psq = [ps.enter_context(nc.psum_tensor(f"psq{tag}{l}", [128, 512], F32))
                       for l in range(4)]
                psb = [ps.enter_context(nc.psum_tensor(f"psb{tag}{j}", [128, 512], F32))
                       for j in range(2)]
                psacc = ps.enter_context(nc.psum_tensor(f"psacc{tag}", [128, 512], F32))
                hbf = ps.enter_context(nc.sbuf_tensor(f"hbf{tag}", [16, 4 * THp], BF16))
                xtl = ps.enter_context(nc.sbuf_tensor(f"xtl{tag}", [128, 2 * 512], F32))
                mtl = ps.enter_context(nc.sbuf_tensor(f"mtl{tag}", [128, 6 * 512], U8))
                x1tl = ps.enter_context(nc.sbuf_tensor(f"x1tl{tag}", [128, 2 * 512], F32))
                for si, (off, w) in enumerate(subw):
                    @blk.tensor
                    def _(t, si=si, off=off, w=w):
                        if si > 0:
                            t.wait_ge(S_ac.h, S_ac.n)
                        for l in range(4):
                            for dt in range(DT_D):
                                _li = t.matmul(psq[l][0:16, 0:w],
                                         wat_sb[:, l * 512 + dt * 16: l * 512 + (dt + 1) * 16],
                                         inbuf[:, dt * THp + off: dt * THp + off + w],
                                         start=(dt == 0), stop=(dt == DT_D - 1))
                        _li.then_inc(S_pe.h, 1)
                        S_pe.n += 1
                    @blk.scalar
                    def _(sc, si=si, off=off, w=w):
                        sc.wait_ge(S_pe.h, S_pe.n)
                        for l in range(4):
                            _li = sc.activation(hbf[:, l * THp + off: l * THp + off + w],
                                          psq[l][0:16, 0:w], AF.Copy)
                        _li.then_inc(S_ac.h, 1)
                        S_ac.n += 1
                barrier()
                wldm = {}; pem = {}; dvm = {}; accm = {}; xldm = {}
                stom = {}; x1m = {}; wbm = {}
                for ct in range(32):
                    @blk.gpsimd
                    def _(g, ct=ct):
                        if ct >= 2:
                            g.wait_ge(S_pe.h, pem[(ct - 2, nsub - 1)])
                        q = rsem("rw", ct)
                        g.dma_start(out=wbuf[:, (ct % 2) * DM:(ct % 2 + 1) * DM],
                                    in_=w0t[ct]).then_inc(q.h, 16)
                        g.dma_start(out=wbt_r[:, (ct % 2) * 512:(ct % 2 + 1) * 512],
                                    in_=wbts[ct]).then_inc(q.h, 16)
                        q.n += 32
                        wldm[ct] = (q.h, q.n)
                    for si, (off, w) in enumerate(subw):
                        u = ct * nsub + si
                        slot = u % 2
                        @blk.sync
                        def _(s, ct=ct, off=off, w=w, u=u, slot=slot):
                            if u >= 2:
                                s.wait_ge(S_dv.h, dvm[u - 2])
                            q = rsem("rm", u)
                            s.dma_start(out=xtl[:, slot * 512: slot * 512 + w],
                                        in_=res_src[ct * 128:(ct + 1) * 128, off:off + w]
                                        ).then_inc(q.h, 16)
                            mo = off + moff0
                            for mi, msrc in enumerate((m1T, m2T, m3T)):
                                s.dma_start(out=mtl[:, (slot * 3 + mi) * 512:(slot * 3 + mi) * 512 + w],
                                            in_=msrc[ct * 128:(ct + 1) * 128, mo:mo + w]
                                            ).then_inc(q.h, 16)
                            q.n += 64
                            xldm[u] = (q.h, q.n)
                        @blk.tensor
                        def _(t, ct=ct, si=si, off=off, w=w, u=u):
                            t.wait_ge(*wldm[ct])
                            if u >= 2:
                                t.wait_ge(S_dv.h, dvm[u - 2])
                            bb = psb[u % 2]
                            for dt in range(DT_D):
                                t.matmul(bb[:, 0:w],
                                         wbuf[:, (ct % 2) * DM + dt * 128:(ct % 2) * DM + (dt + 1) * 128],
                                         inbuf[:, dt * THp + off: dt * THp + off + w],
                                         start=(dt == 0), stop=(dt == DT_D - 1))
                            if u >= 1:
                                t.wait_ge(S_dv.h, dvm[u - 1])
                                t.wait_ge(S_ac.h, accm[u - 1])
                            for l in range(4):
                                _li = t.matmul(psq[l][:, 0:w],
                                         wbt_r[:, (ct % 2) * 512 + l * 128:(ct % 2) * 512 + (l + 1) * 128],
                                         hbf[:, l * THp + off: l * THp + off + w],
                                         start=True, stop=True)
                            _li.then_inc(S_pe.h, 1)
                            S_pe.n += 1
                            pem[(ct, si)] = S_pe.n
                        @blk.scalar
                        def _(sc, ct=ct, si=si, w=w, u=u):
                            sc.wait_ge(S_pe.h, pem[(ct, si)])
                            if u >= 1:
                                sc.wait_ge(S_dv.h, dvm[u - 1])
                            sc.activation(psacc[:, 0:w], psq[0][:, 0:w], AF.Copy).then_inc(S_ac.h, 1)
                            S_ac.n += 1
                            accm[u] = S_ac.n
                        @blk.vector
                        def _(v, ct=ct, w=w, u=u, slot=slot):
                            v.wait_ge(S_ac.h, accm[u])
                            v.wait_ge(*xldm[u])
                            if u >= 2:
                                v.wait_ge(*stom[u - 2])
                                if nxt_buf is not None:
                                    v.wait_ge(S_ac.h, x1m[u - 2])
                            mb = slot * 3 * 512
                            v.copy_predicated(psacc[:, 0:w], mtl[:, mb:mb + w], psq[1][:, 0:w])
                            v.copy_predicated(psacc[:, 0:w], mtl[:, mb + 512:mb + 512 + w], psq[2][:, 0:w])
                            v.copy_predicated(psacc[:, 0:w], mtl[:, mb + 1024:mb + 1024 + w], psq[3][:, 0:w])
                            v.scalar_tensor_tensor(t_scr[:, 0:w], psb[u % 2][:, 0:w], 0.5,
                                                   xtl[:, slot * 512: slot * 512 + w],
                                                   op0=ALU.mult, op1=ALU.add)
                            v.tensor_scalar(t_scr[:, 0:w], t_scr[:, 0:w], b0h_sb[:, ct:ct + 1],
                                            None, op0=ALU.add)
                            _li = v.scalar_tensor_tensor(x1tl[:, slot * 512: slot * 512 + w],
                                                   psacc[:, 0:w], 4.0, t_scr[:, 0:w],
                                                   op0=ALU.mult, op1=ALU.add)
                            _li.then_inc(S_dv.h, 1)
                            S_dv.n += 1
                            dvm[u] = S_dv.n
                        @blk.sync
                        def _(s, ct=ct, off=off, w=w, u=u, slot=slot):
                            s.wait_ge(S_dv.h, dvm[u])
                            q = rsem("rs1", u)
                            s.dma_start(out=out_dst[ct * 128:(ct + 1) * 128, off:off + w],
                                        in_=x1tl[:, slot * 512: slot * 512 + w]).then_inc(q.h, 16)
                            q.n += 16
                            stom[u] = (q.h, q.n)
                        if nxt_buf is not None:
                            @blk.scalar
                            def _(sc, ct=ct, off=off, w=w, u=u, slot=slot):
                                sc.wait_ge(S_dv.h, dvm[u])
                                sc.activation(nxt_buf[:, ct * THp + off: ct * THp + off + w],
                                              x1tl[:, slot * 512: slot * 512 + w],
                                              AF.Copy).then_inc(S_ac.h, 1)
                                S_ac.n += 1
                                x1m[u] = S_ac.n
            barrier()

        if phases < 2:
            finish()
            return nc
        pscope = ExitStack()
        with pscope:
            xn_b = pscope.enter_context(nc.sbuf_tensor("xn_b", [128, DT_D * THp], BF16))
            emit_ffn(xn_a, xT, x1_scr, TH, xn_b, "f1")
            if phases < 3:
                finish()
                return nc
            ln_stats(xn_b, subw1)
            ln_fin_bcast_apply(xn_b, TH, subw1)
            barrier()
            if phases < 4:
                finish()
                return nc

            p2 = ExitStack()
            with p2:
                psip = [p2.enter_context(nc.psum_tensor(f"psip{j}", [128, 512], F32)) for j in range(2)]
                hbcT = p2.enter_context(nc.sbuf_tensor("hbcT", [128, THp], F32))
                cacc = p2.enter_context(nc.sbuf_tensor("cacc", [128, T], F32))
                gtt = p2.enter_context(nc.sbuf_tensor("gtt", [128, 2 * 512], BF16))
                xbcb = p2.enter_context(nc.sbuf_tensor("xbcb", [128, 2 * T], BF16))
                wldm = {}; pem = {}; evm = {}; cvm = {}; stom = {}; silm = {}
                for ct in range(80):
                    @blk.gpsimd
                    def _(g, ct=ct):
                        if ct >= 2:
                            g.wait_ge(S_pe.h, pem[(ct - 2, 2)])
                        q = rsem("rw", ct)
                        g.dma_start(out=wbuf[:, (ct % 2) * DM:(ct % 2 + 1) * DM],
                                    in_=wpt[ct]).then_inc(q.h, 16)
                        q.n += 16
                        wldm[ct] = (q.h, q.n)
                    for si, (off, w) in enumerate(subw1):
                        u = ct * 3 + si
                        @blk.tensor
                        def _(t, ct=ct, si=si, off=off, w=w, u=u):
                            t.wait_ge(*wldm[ct])
                            if u >= 2:
                                t.wait_ge(S_ac.h, evm[u - 2])
                            bb = psip[u % 2]
                            for dt in range(DT_D):
                                _li = t.matmul(bb[:, 0:w],
                                         wbuf[:, (ct % 2) * DM + dt * 128:(ct % 2) * DM + (dt + 1) * 128],
                                         xn_b[:, dt * THp + off: dt * THp + off + w],
                                         start=(dt == 0), stop=(dt == DT_D - 1))
                            _li.then_inc(S_pe.h, 1)
                            S_pe.n += 1
                            pem[(ct, si)] = S_pe.n
                        @blk.scalar
                        def _(sc, ct=ct, si=si, off=off, w=w, u=u):
                            sc.wait_ge(S_pe.h, pem[(ct, si)])
                            if ct < 32:
                                if u >= 2 and ("g", u - 2) in stom:
                                    sc.wait_ge(*stom[("g", u - 2)])
                                sc.activation(gtt[:, (u % 2) * 512:(u % 2) * 512 + w],
                                              psip[u % 2][:, 0:w], AF.Copy).then_inc(S_ac.h, 1)
                            else:
                                if si == 0 and ct >= 33:
                                    sc.wait_ge(S_dv.h, cvm.get(ct - 1, 0))
                                sc.activation(hbcT[:, off:off + w],
                                              psip[u % 2][:, 0:w], AF.Copy).then_inc(S_ac.h, 1)
                            S_ac.n += 1
                            evm[u] = S_ac.n
                        if ct < 32:
                            @blk.sync
                            def _(s, ct=ct, si=si, off=off, w=w, u=u):
                                s.wait_ge(S_ac.h, evm[u])
                                q = rsem("rg", u)
                                s.dma_start(out=gate_scr[ct * 128:(ct + 1) * 128, off:off + w],
                                            in_=gtt[:, (u % 2) * 512:(u % 2) * 512 + w]
                                            ).then_inc(q.h, 16)
                                q.n += 16
                                stom[("g", u)] = (q.h, q.n)
                    if ct >= 32:
                        hc = ct - 32
                        @blk.vector
                        def _(v, ct=ct, hc=hc):
                            v.wait_ge(S_ac.h, evm[ct * 3 + 2])
                            if hc >= 1:
                                v.wait_ge(S_ac.h, silm.get(hc - 1, 0))
                            o0 = (hc % 2) * T
                            v.tensor_scalar(hbcT[:, 0:3], hbcT[:, 0:3],
                                            halo_sb[:, 0:1], None, op0=ALU.mult)
                            v.tensor_scalar(cacc[:, 0:T], hbcT[:, 0:T],
                                            convw_sb[:, hc * 4: hc * 4 + 1],
                                            convb_sb[:, hc:hc + 1], op0=ALU.mult, op1=ALU.add)
                            for k in range(1, 4):
                                _li = v.scalar_tensor_tensor(cacc[:, 0:T], hbcT[:, k:k + T],
                                                       convw_sb[:, hc * 4 + k: hc * 4 + k + 1],
                                                       cacc[:, 0:T], op0=ALU.mult, op1=ALU.add)
                            _li.then_inc(S_dv.h, 1)
                            S_dv.n += 1
                            cvm[ct] = S_dv.n
                        @blk.scalar
                        def _(sc, ct=ct, hc=hc):
                            sc.wait_ge(S_dv.h, cvm[ct])
                            o0 = (hc % 2) * T
                            if hc >= 2 and ("x", hc - 2) in stom:
                                sc.wait_ge(*stom[("x", hc - 2)])
                            sc.activation(xbcb[:, o0:o0 + T], cacc[:, 0:T],
                                          AF.Silu).then_inc(S_ac.h, 1)
                            S_ac.n += 1
                            silm[hc] = S_ac.n
                        @blk.sync
                        def _(s, hc=hc):
                            s.wait_ge(S_ac.h, silm[hc])
                            if hc < 32:
                                dst = xssm_scr[hc * 128:(hc + 1) * 128, :]
                            elif hc < 40:
                                dst = B_scr[(hc - 32) * 128:(hc - 31) * 128, :]
                            else:
                                dst = C_scr[(hc - 40) * 128:(hc - 39) * 128, :]
                            q = rsem("rxs", hc)
                            s.dma_start(out=dst, in_=xbcb[:, (hc % 2) * T:(hc % 2) * T + T]
                                        ).then_inc(q.h, 16)
                            q.n += 16
                            stom[("x", hc)] = (q.h, q.n)
                @blk.gpsimd
                def _(g):
                    g.wait_ge(S_pe.h, pem[(78, 2)])
                    q = rsem("rw", 80)
                    g.dma_start(out=wbuf[:, 0:DM], in_=wpt[80]).then_inc(q.h, 16)
                    q.n += 16
                wld80 = (rsem("rw", 80).h, rsem("rw", 80).n)
                ps_dt = p2.enter_context(nc.psum_tensor("ps_dt", [128, 512], F32))
                ps_ca = p2.enter_context(nc.psum_tensor("ps_ca", [128, 512], F32))
                ps_tr_full = p2.enter_context(nc.psum_tensor("ps_tr", [128, 512], F32))
                ps_bc = p2.enter_context(nc.psum_tensor("ps_bc", [128, 512], F32))
                dt_sc = p2.enter_context(nc.sbuf_tensor("dt_sc", [128, 64], F32))
                dt_s2 = p2.enter_context(nc.sbuf_tensor("dt_s2", [128, 64], F32))
                dt_s3 = p2.enter_context(nc.sbuf_tensor("dt_s3", [128, 64], F32))
                caf_sc = p2.enter_context(nc.sbuf_tensor("caf_sc", [64, 128], F32))
                cer_sc = p2.enter_context(nc.sbuf_tensor("cer_sc", [1, 64], F32))
                for ch in range(NCH):
                    @blk.tensor
                    def _(t, ch=ch):
                        t.wait_ge(*wld80)
                        if ch > 0:
                            t.wait_ge(S_ac.h, S_ac.n)
                        for dt in range(DT_D):
                            _li = t.matmul(ps_dt[:, 0:64],
                                     xn_b[:, dt * THp + 3 + ch * Q: dt * THp + 3 + (ch + 1) * Q],
                                     wbuf[:, dt * 128: dt * 128 + 64],
                                     start=(dt == 0), stop=(dt == DT_D - 1))
                        _li.then_inc(S_pe.h, 1)
                        S_pe.n += 1
                    pe1 = S_pe.n
                    @blk.vector
                    def _(v, pe1=pe1):
                        v.wait_ge(S_pe.h, pe1)
                        _li = v.tensor_tensor(dt_sc[:], ps_dt[:, 0:64], dtbias_sb[:], op=ALU.add)
                        _li.then_inc(S_dv.h, 1)
                        S_dv.n += 1
                    dv1 = S_dv.n
                    @blk.scalar
                    def _(sc, dv1=dv1):
                        sc.wait_ge(S_dv.h, dv1)
                        sc.activation(dt_s2[:], dt_sc[:], AF.Abs)
                        sc.activation(dt_s2[:], dt_s2[:], AF.Exp, scale=-1.0)
                        _li = sc.activation(dt_s2[:], dt_s2[:], AF.Ln, bias=1.0)
                        _li.then_inc(S_ac.h, 1)
                        S_ac.n += 1
                    ac1 = S_ac.n
                    @blk.vector
                    def _(v, ch=ch, ac1=ac1):
                        v.wait_ge(S_ac.h, ac1)
                        v.tensor_scalar(dt_s3[:], dt_sc[:], 0.0, None, op0=ALU.max)
                        v.tensor_tensor(dtp_all[:, ch * 64:(ch + 1) * 64], dt_s3[:], dt_s2[:], op=ALU.add)
                        _li = v.tensor_tensor(dt_s3[:], dtp_all[:, ch * 64:(ch + 1) * 64], expA_sb[:], op=ALU.mult)
                        _li.then_inc(S_dv.h, 1)
                        S_dv.n += 1
                    dv2 = S_dv.n
                    @blk.tensor
                    def _(t, dv2=dv2):
                        t.wait_ge(S_dv.h, dv2)
                        _li = t.matmul(ps_ca[:, 0:64], ctrif_sb[:], dt_s3[:], start=True, stop=True)
                        _li.then_inc(S_pe.h, 1)
                        S_pe.n += 1
                    pe2 = S_pe.n
                    @blk.scalar
                    def _(sc, ch=ch, pe2=pe2):
                        sc.wait_ge(S_pe.h, pe2)
                        _li = sc.activation(ca_all[:, ch * 64:(ch + 1) * 64], ps_ca[:, 0:64], AF.Copy)
                        _li.then_inc(S_ac.h, 1)
                        S_ac.n += 1
                    ac2 = S_ac.n
                    @blk.tensor
                    def _(t, ch=ch, ac2=ac2):
                        t.wait_ge(S_ac.h, ac2)
                        _li = t.transpose(ps_tr_full[0:64, 0:128], ca_all[:, ch * 64:(ch + 1) * 64], identf_sb[:])
                        _li.then_inc(S_pe.h, 1)
                        S_pe.n += 1
                    pe3 = S_pe.n
                    @blk.scalar
                    def _(sc, pe3=pe3):
                        sc.wait_ge(S_pe.h, pe3)
                        sc.activation(caf_sc[:], ps_tr_full[0:64, 0:128], AF.Copy).then_inc(S_ac.h, 1)
                        S_ac.n += 1
                    ac3 = S_ac.n
                    @blk.sync
                    def _(s, ch=ch, ac3=ac3):
                        s.wait_ge(S_ac.h, ac3)
                        q = rsem("rcs", 0, n=1)
                        s.dma_start(out=ca_scr[ch:ch + 1, :].rearrange("o (h q) -> h (o q)", h=64),
                                    in_=caf_sc[:]).then_inc(q.h, 16)
                        s.dma_start(out=cae_scr[ch:ch + 1, :].rearrange("o h -> h o"),
                                    in_=caf_sc[:, 127:128]).then_inc(q.h, 16)
                        q.n += 32
                        s.wait_ge(q.h, q.n)
                        q2 = rsem("rcb", 0, n=1)
                        s.dma_start(out=cer_sc[:], in_=cae_scr[ch:ch + 1, :]).then_inc(q2.h, 16)
                        q2.n += 16
                    xl1 = (rsem("rcb", 0, n=1).h, rsem("rcb", 0, n=1).n)
                    @blk.tensor
                    def _(t, xl1=xl1):
                        t.wait_ge(*xl1)
                        _li = t.matmul(ps_bc[:, 0:64], ones1f_sb[:], cer_sc[:], start=True, stop=True)
                        _li.then_inc(S_pe.h, 1)
                        S_pe.n += 1
                    pe4 = S_pe.n
                    @blk.scalar
                    def _(sc, ch=ch, pe4=pe4):
                        sc.wait_ge(S_pe.h, pe4)
                        sc.activation(caendb_all[:, ch * 64:(ch + 1) * 64], ps_bc[:, 0:64], AF.Copy)
                        _li = sc.activation(dAtot_all[:, ch * 64:(ch + 1) * 64], ps_bc[:, 0:64], AF.Exp)
                        _li.then_inc(S_ac.h, 1)
                        S_ac.n += 1
                @blk.vector
                def _(v):
                    v.wait_ge(S_ac.h, S_ac.n)
                    v.tensor_copy(aprod_sb[:], dAtot_all[:, 0:64])
                    for ch in range(1, NCH):
                        _li = v.tensor_tensor(aprod_sb[:], aprod_sb[:],
                                        dAtot_all[:, ch * 64:(ch + 1) * 64], op=ALU.mult)
                    _li.then_inc(S_dv.h, 1)
                    S_dv.n += 1
            barrier()
            if phases < 5:
                finish()
                return nc
        barrier()

        p3 = ExitStack()
        with p3:
            S_cur = p3.enter_context(nc.sbuf_tensor("S_cur", [128, 4096], F32))
            S_bfb = p3.enter_context(nc.sbuf_tensor("S_bfb", [128, 4096], BF16))
            xsl = p3.enter_context(nc.sbuf_tensor("xsl", [128, 2 * 4096], BF16))
            dtx = p3.enter_context(nc.sbuf_tensor("dtx", [128, 2 * 4096], BF16))
            bsl = p3.enter_context(nc.sbuf_tensor("bsl", [128, 2 * 1024], BF16))
            csl = p3.enter_context(nc.sbuf_tensor("csl", [128, 2 * 1024], BF16))
            btok = p3.enter_context(nc.sbuf_tensor("btok", [128, 8 * 128], BF16))
            dtxw = p3.enter_context(nc.sbuf_tensor("dtxw", [128, 512], BF16))
            wcol = p3.enter_context(nc.sbuf_tensor("wcol", [128, 64], F32))
            ps_tx_full = p3.enter_context(nc.psum_tensor("ps_tx", [128, 1024], BF16))
            ps_S = p3.enter_context(nc.psum_tensor("ps_S", [128, 512], F32))

            def slab_rearr(scr):
                return scr.rearrange("(a p) t -> p a t", p=128)

            def emit_slabs(ch, mk, want_bc):
                @blk.sync
                def _(s, ch=ch):
                    if ch >= 2:
                        s.wait_ge(S_dv.h, mk[("xslfree", ch - 2)])
                        s.wait_ge(S_dv.h, mk[("bcfree", ch - 2)])
                    q = rsem("rsl", ch)
                    s.dma_start(
                        out=xsl[:, (ch % 2) * 4096:(ch % 2 + 1) * 4096].rearrange(
                            "p (a t) -> p a t", a=32),
                        in_=slab_rearr(xssm_scr)[:, :, ch * Q:(ch + 1) * Q]).then_inc(q.h, 16)
                    q.n += 16
                    if want_bc:
                        s.dma_start(
                            out=csl[:, (ch % 2) * 1024:(ch % 2 + 1) * 1024].rearrange(
                                "p (a t) -> p a t", a=8),
                            in_=slab_rearr(C_scr)[:, :, ch * Q:(ch + 1) * Q]).then_inc(q.h, 16)
                        q.n += 16
                    s.dma_start(
                        out=bsl[:, (ch % 2) * 1024:(ch % 2 + 1) * 1024].rearrange(
                            "p (a t) -> p a t", a=8),
                        in_=slab_rearr(B_scr)[:, :, ch * Q:(ch + 1) * Q]).then_inc(q.h, 16)
                    q.n += 16
                    mk[("xld", ch)] = (q.h, q.n)

            def emit_dtx(ch, mk):
                for m in range(32):
                    @blk.tensor
                    def _(t, ch=ch, m=m):
                        if m == 0:
                            t.wait_ge(*mk[("xld", ch)])
                            if ch >= 2:
                                t.wait_ge(S_pe.h, mk.get(("dtxfree", ch - 2), 0))
                        if m > 0:
                            t.wait_ge(S_dv.h, mk[("dtx", ch, m - 1)])
                        _li = t.transpose(ps_tx_full[:, 0:128],
                                    xsl[:, (ch % 2) * 4096 + m * 128:(ch % 2) * 4096 + (m + 1) * 128],
                                    identb_sb[:])
                        _li.then_inc(S_pe.h, 1)
                        S_pe.n += 1
                        mk[("tx", ch, m)] = S_pe.n
                    @blk.vector
                    def _(v, ch=ch, m=m):
                        v.wait_ge(S_pe.h, mk[("tx", ch, m)])
                        d0 = (ch % 2) * 4096 + m * 128
                        for hh in range(2):
                            h = 2 * m + hh
                            _li = v.tensor_scalar(dtx[:, d0 + hh * 64: d0 + (hh + 1) * 64],
                                            ps_tx_full[:, hh * 64:(hh + 1) * 64],
                                            dtp_all[:, ch * 64 + h: ch * 64 + h + 1], None,
                                            op0=ALU.mult)
                        _li.then_inc(S_dv.h, 1)
                        S_dv.n += 1
                        mk[("dtx", ch, m)] = S_dv.n
                mk[("xslfree", ch)] = mk[("dtx", ch, 31)]

            def emit_contrib(ch, mk):
                @blk.vector
                def _(v, ch=ch):
                    v.wait_ge(S_dv.h, mk[("dtx", ch, 31)])
                    _li = v.tensor_tensor(wcol[:], caendb_all[:, ch * 64:(ch + 1) * 64],
                                    ca_all[:, ch * 64:(ch + 1) * 64], op=ALU.subtract)
                    _li.then_inc(S_dv.h, 1)
                    S_dv.n += 1
                dvw = S_dv.n
                @blk.scalar
                def _(sc, dvw=dvw):
                    sc.wait_ge(S_dv.h, dvw)
                    sc.activation(wcol[:], wcol[:], AF.Exp).then_inc(S_ac.h, 1)
                    S_ac.n += 1
                acw = S_ac.n
                for g in range(8):
                    @blk.tensor
                    def _(t, ch=ch, g=g, acw=acw):
                        if g == 0:
                            t.wait_ge(S_ac.h, acw)
                        else:
                            t.wait_ge(S_ac.h, mk[("btk_ev", ch, g - 1)])
                        _li = t.transpose(ps_tx_full[:, 0:128],
                                    bsl[:, (ch % 2) * 1024 + g * 128:(ch % 2) * 1024 + (g + 1) * 128],
                                    identb_sb[:])
                        _li.then_inc(S_pe.h, 1)
                        S_pe.n += 1
                        mk[("btk", ch, g)] = S_pe.n
                    @blk.scalar
                    def _(sc, ch=ch, g=g):
                        sc.wait_ge(S_pe.h, mk[("btk", ch, g)])
                        sc.activation(btok[:, g * 128:(g + 1) * 128], ps_tx_full[:, 0:128], AF.Copy).then_inc(S_ac.h, 1)
                        S_ac.n += 1
                        mk[("btk_ev", ch, g)] = S_ac.n
                    @blk.vector
                    def _(v, ch=ch, g=g, acw=acw):
                        v.wait_ge(S_ac.h, acw)
                        if g > 0:
                            v.wait_ge(S_pe.h, mk[("ctr", ch, g - 1)])
                        d0 = (ch % 2) * 4096 + g * 512
                        for hh in range(8):
                            h = g * 8 + hh
                            _li = v.tensor_scalar(dtxw[:, hh * 64:(hh + 1) * 64],
                                            dtx[:, d0 + hh * 64: d0 + (hh + 1) * 64],
                                            wcol[:, h:h + 1], None, op0=ALU.mult)
                        _li.then_inc(S_dv.h, 1)
                        S_dv.n += 1
                        mk[("dtxw", ch, g)] = S_dv.n
                    @blk.tensor
                    def _(t, ch=ch, g=g):
                        t.wait_ge(S_ac.h, mk[("btk_ev", ch, g)])
                        t.wait_ge(S_dv.h, mk[("dtxw", ch, g)])
                        prev = mk[("supd", ch, g - 1)] if g > 0 else mk.get(("supd_last", ch), 0)
                        t.wait_ge(S_dv.h, prev)
                        _li = t.matmul(ps_S[:], btok[:, g * 128:(g + 1) * 128], dtxw[:], start=True, stop=True)
                        _li.then_inc(S_pe.h, 1)
                        S_pe.n += 1
                        mk[("ctr", ch, g)] = S_pe.n
                    @blk.vector
                    def _(v, ch=ch, g=g):
                        v.wait_ge(S_pe.h, mk[("ctr", ch, g)])
                        for hh in range(8):
                            h = g * 8 + hh
                            _li = v.scalar_tensor_tensor(
                                S_cur[:, h * 64:(h + 1) * 64], S_cur[:, h * 64:(h + 1) * 64],
                                dAtot_all[:, ch * 64 + h: ch * 64 + h + 1],
                                ps_S[:, hh * 64:(hh + 1) * 64], op0=ALU.mult, op1=ALU.add)
                        _li.then_inc(S_dv.h, 1)
                        S_dv.n += 1
                        mk[("supd", ch, g)] = S_dv.n
                mk[("supd_last", ch + 1)] = mk[("supd", ch, 7)]
                mk[("bcfree", ch)] = mk[("supd", ch, 7)]

            m1 = {("supd_last", 0): 0}
            @blk.vector
            def _(v):
                _li = v.memset(S_cur[:], 0.0)
                _li.then_inc(S_dv.h, 1)
                S_dv.n += 1
            for ch in range(NCH):
                emit_slabs(ch, m1, want_bc=False)
                emit_dtx(ch, m1)
                emit_contrib(ch, m1)
            @blk.sync
            def _(s):
                s.wait_ge(S_dv.h, m1[("supd", NCH - 1, 7)])
                q = rsem("rcp", 0, n=1)
                s.dma_start(out=coll_in.ap()[0:128, :], in_=S_cur[:]).then_inc(q.h, 16)
                s.dma_start(out=coll_in.ap()[128:129, :].rearrange("p (a f) -> p a f", a=64),
                            in_=aprod_sb[0:1, :].rearrange("p (o f) -> p o f", o=1
                                                           ).broadcast_to([1, 64, 64])
                            ).then_inc(q.h, 16)
                q.n += 32
            sto_cl = (rsem("rcp", 0, n=1).h, rsem("rcp", 0, n=1).n)
            @blk.gpsimd
            def _(g):
                g.wait_ge(*sto_cl)
                g.collective_compute("AllGather", ALU.bypass,
                                     replica_groups=[[0, 1, 2, 3], [4, 5, 6, 7]],
                                     ins=[coll_in.ap().opt()],
                                     outs=[coll_out.ap().opt()]).then_inc(S_col.h, 1)
                S_col.n += 1
            pcmb = ExitStack()
            pcmb.__enter__()
            sg = pcmb.enter_context(nc.sbuf_tensor("sg", [128, 4096], F32))
            arow = pcmb.enter_context(nc.sbuf_tensor("arow", [1, 256], F32))
            wrow = pcmb.enter_context(nc.sbuf_tensor("wrow", [1, 320], F32))
            wbc = pcmb.enter_context(nc.sbuf_tensor("wbc", [128, 64], F32))
            ps_wb = pcmb.enter_context(nc.psum_tensor("ps_wb", [128, 512], F32))
            @blk.sync
            def _(s):
                s.wait_ge(S_col.h, S_col.n)
                q = rsem("rar", 0, n=1)
                for j in range(4):
                    s.dma_start(out=arow[0:1, j * 64:(j + 1) * 64],
                                in_=coll_out.ap()[129 * j + 128: 129 * j + 129, 0:64]
                                ).then_inc(q.h, 16)
                q.n += 64
            xl_ar = (rsem("rar", 0, n=1).h, rsem("rar", 0, n=1).n)
            @blk.vector
            def _(v, xl_ar=xl_ar):
                v.wait_ge(*xl_ar)
                for k2 in range(4):
                    v.tensor_scalar(wrow[0:1, k2 * 64:(k2 + 1) * 64],
                                    arow[0:1, k2 * 64:(k2 + 1) * 64], -1.0, None, op0=ALU.add)
                    v.tensor_tensor(wrow[0:1, k2 * 64:(k2 + 1) * 64],
                                    wrow[0:1, k2 * 64:(k2 + 1) * 64],
                                    actrow_sb[0:1, k2 * 64:(k2 + 1) * 64], op=ALU.mult)
                    v.tensor_scalar(wrow[0:1, k2 * 64:(k2 + 1) * 64],
                                    wrow[0:1, k2 * 64:(k2 + 1) * 64], 1.0, None, op0=ALU.add)
                v.tensor_copy(wrow[0:1, 256:320], wrow[0:1, 192:256])
                v.tensor_tensor(wrow[0:1, 192:256], mselrow_sb[0:1, 192:256],
                                mselrow_sb[0:1, 192:256], op=ALU.mult)
                v.tensor_tensor(wrow[0:1, 128:192], wrow[0:1, 256:320],
                                mselrow_sb[0:1, 128:192], op=ALU.mult)
                v.tensor_tensor(wrow[0:1, 256:320], wrow[0:1, 64:128], wrow[0:1, 256:320],
                                op=ALU.mult)
                v.tensor_tensor(wrow[0:1, 64:128], wrow[0:1, 256:320],
                                mselrow_sb[0:1, 64:128], op=ALU.mult)
                v.tensor_tensor(wrow[0:1, 256:320], wrow[0:1, 0:64], wrow[0:1, 256:320],
                                op=ALU.mult)
                _li = v.tensor_tensor(wrow[0:1, 0:64], wrow[0:1, 256:320],
                                mselrow_sb[0:1, 0:64], op=ALU.mult)
                _li.then_inc(S_dv.h, 1)
                S_dv.n += 1
            dv_w = S_dv.n
            sinm = {}
            for j in range(4):
                @blk.sync
                def _(s, j=j):
                    if j >= 1:
                        s.wait_ge(S_dv.h, sinm[j - 1])
                    q = rsem("rsg", 0, n=1)
                    s.dma_start(out=sg[:], in_=coll_out.ap()[129 * j: 129 * j + 128, :]
                                ).then_inc(q.h, 16)
                    q.n += 16
                xl_j = (rsem("rsg", 0, n=1).h, rsem("rsg", 0, n=1).n)
                @blk.tensor
                def _(t, j=j, dv_w=dv_w):
                    t.wait_ge(S_dv.h, dv_w)
                    if j > 0:
                        t.wait_ge(S_dv.h, sinm[j - 1])
                    _li = t.matmul(ps_wb[:, 0:64], ones1f_sb[:], wrow[0:1, j * 64:(j + 1) * 64],
                             start=True, stop=True)
                    _li.then_inc(S_pe.h, 1)
                    S_pe.n += 1
                pe_j = S_pe.n
                @blk.scalar
                def _(sc, pe_j=pe_j):
                    sc.wait_ge(S_pe.h, pe_j)
                    sc.activation(wbc[:], ps_wb[:, 0:64], AF.Copy).then_inc(S_ac.h, 1)
                    S_ac.n += 1
                ac_j = S_ac.n
                @blk.vector
                def _(v, j=j, ac_j=ac_j, xl_j=xl_j):
                    v.wait_ge(S_ac.h, ac_j)
                    v.wait_ge(*xl_j)
                    for h in range(64):
                        src = sg[:, h * 64:(h + 1) * 64]
                        if j == 0:
                            _li = v.tensor_scalar(S_cur[:, h * 64:(h + 1) * 64], src,
                                                  wbc[:, h:h + 1], None, op0=ALU.mult)
                        else:
                            _li = v.scalar_tensor_tensor(S_cur[:, h * 64:(h + 1) * 64], src,
                                                   wbc[:, h:h + 1],
                                                   S_cur[:, h * 64:(h + 1) * 64],
                                                   op0=ALU.mult, op1=ALU.add)
                    _li.then_inc(S_dv.h, 1)
                    S_dv.n += 1
                    sinm[j] = S_dv.n
            barrier()
            pcmb.__exit__(None, None, None)
            if phases < 6:
                finish()
                return nc

            gm = p3.enter_context(nc.sbuf_tensor("gm", [128, 128], F32))
            dcl = p3.enter_context(nc.sbuf_tensor("dcl", [128, 512], F32))
            E_sb = p3.enter_context(nc.sbuf_tensor("E_sb", [128, 512], F32))
            E2_sb = p3.enter_context(nc.sbuf_tensor("E2_sb", [128, 512], F32))
            mt4 = p3.enter_context(nc.sbuf_tensor("mt4", [128, 512], BF16))
            c4 = p3.enter_context(nc.sbuf_tensor("c4", [128, 512], BF16))
            crow = p3.enter_context(nc.sbuf_tensor("crow", [1, 2 * 512], F32))
            ps_G_full = p3.enter_context(nc.psum_tensor("ps_G", [128, 512], F32))
            ps_da = p3.enter_context(nc.psum_tensor("ps_da", [128, 512], F32))
            ps_y = [p3.enter_context(nc.psum_tensor(f"ps_y{j}", [128, 512], F32)) for j in range(2)]
            m2 = {("supd_last", 0): 0}
            for ch in range(NCH):
                @blk.scalar
                def _(sc, ch=ch):
                    if ch == 0:
                        sc.wait_ge(S_dv.h, sinm[3])
                    else:
                        sc.wait_ge(S_dv.h, m2[("supd", ch - 1, 7)])
                        sc.wait_ge(S_pe.h, m2[("y", ch - 1, 31)])
                    sc.activation(S_bfb[:], S_cur[:], AF.Copy).then_inc(S_ac.h, 1)
                    S_ac.n += 1
                    m2[("sbf", ch)] = S_ac.n
                emit_slabs(ch, m2, want_bc=True)
                emit_dtx(ch, m2)
                for gq in range(16):
                    g, half = gq // 2, gq % 2
                    hb = g * 8 + half * 4
                    @blk.sync
                    def _(s, ch=ch, gq=gq, hb=hb):
                        prev = m2.get(("E", ch, gq - 2), m2.get(("E", ch - 1, 15), 0))
                        s.wait_ge(S_ac.h, prev)
                        q = rsem("rcr", gq)
                        s.dma_start(out=crow[0:1, (gq % 2) * 512:(gq % 2) * 512 + 512],
                                    in_=ca_scr[ch:ch + 1, hb * 128:(hb + 4) * 128]
                                    ).then_inc(q.h, 16)
                        q.n += 16
                        m2[("crow", ch, gq)] = (q.h, q.n)
                    if half == 0:
                        @blk.tensor
                        def _(t, ch=ch, g=g):
                            prev = m2.get(("gm", ch, g - 1), m2.get(("gm", ch - 1, 7), 0))
                            t.wait_ge(S_dv.h, prev)
                            t.wait_ge(*m2[("xld", ch)])
                            _li = t.matmul(ps_G_full[:, 0:128],
                                     bsl[:, (ch % 2) * 1024 + g * 128:(ch % 2) * 1024 + (g + 1) * 128],
                                     csl[:, (ch % 2) * 1024 + g * 128:(ch % 2) * 1024 + (g + 1) * 128],
                                     start=True, stop=True)
                            _li.then_inc(S_pe.h, 1)
                            S_pe.n += 1
                            m2[("G", ch, g)] = S_pe.n
                        @blk.vector
                        def _(v, ch=ch, g=g):
                            v.wait_ge(S_pe.h, m2[("G", ch, g)])
                            _li = v.tensor_tensor(gm[:], ps_G_full[:, 0:128], k01f_sb[:], op=ALU.mult)
                            _li.then_inc(S_dv.h, 1)
                            S_dv.n += 1
                            m2[("gm", ch, g)] = S_dv.n
                    @blk.tensor
                    def _(t, ch=ch, gq=gq):
                        t.wait_ge(*m2[("crow", ch, gq)])
                        prevE = m2.get(("E", ch, gq - 1), m2.get(("E", ch - 1, 15), 0))
                        t.wait_ge(S_ac.h, prevE)
                        _li = t.matmul(ps_da[:], ones1f_sb[:],
                                 crow[0:1, (gq % 2) * 512:(gq % 2) * 512 + 512],
                                 start=True, stop=True)
                        _li.then_inc(S_pe.h, 1)
                        S_pe.n += 1
                        m2[("da", ch, gq)] = S_pe.n
                    @blk.vector
                    def _(v, ch=ch, gq=gq, hb=hb):
                        v.wait_ge(S_pe.h, m2[("da", ch, gq)])
                        for hq in range(4):
                            h = hb + hq
                            _li = v.tensor_scalar(dcl[:, hq * 128:(hq + 1) * 128],
                                            ps_da[:, hq * 128:(hq + 1) * 128],
                                            ca_all[:, ch * 64 + h: ch * 64 + h + 1],
                                            0.0, op0=ALU.subtract, op1=ALU.min)
                        _li.then_inc(S_dv.h, 1)
                        S_dv.n += 1
                        m2[("dcl", ch, gq)] = S_dv.n
                    @blk.scalar
                    def _(sc, ch=ch, gq=gq):
                        sc.wait_ge(S_dv.h, m2[("dcl", ch, gq)])
                        sc.activation(E_sb[:], dcl[:], AF.Exp)
                        _li = sc.activation(E2_sb[:], ps_da[:], AF.Exp)
                        _li.then_inc(S_ac.h, 1)
                        S_ac.n += 1
                        m2[("E", ch, gq)] = S_ac.n
                    @blk.vector
                    def _(v, ch=ch, gq=gq, g=g, half=half):
                        v.wait_ge(S_ac.h, m2[("E", ch, gq)])
                        v.wait_ge(S_dv.h, m2[("gm", ch, g)])
                        pxm = g * 4 + half * 2 - 1
                        prevy = m2.get(("y", ch, pxm), m2.get(("y", ch - 1, 31), 0))
                        v.wait_ge(S_pe.h, prevy)
                        gmr = gm[:].rearrange("p (o q) -> p o q", o=1).broadcast_to([128, 4, 128])
                        v.tensor_tensor(mt4[:].rearrange("p (a q) -> p a q", a=4),
                                        E_sb[:].rearrange("p (a q) -> p a q", a=4), gmr,
                                        op=ALU.mult)
                        cslr = csl[:, (ch % 2) * 1024 + g * 128:(ch % 2) * 1024 + (g + 1) * 128]
                        cfr = cslr.rearrange("p (o q) -> p o q", o=1).broadcast_to([128, 4, 128])
                        _li = v.tensor_tensor(c4[:].rearrange("p (a q) -> p a q", a=4),
                                        E2_sb[:].rearrange("p (a q) -> p a q", a=4), cfr,
                                        op=ALU.mult)
                        _li.then_inc(S_dv.h, 1)
                        S_dv.n += 1
                        m2[("mt", ch, gq)] = S_dv.n
                    for pr in range(2):
                        h0 = hb + pr * 2
                        xm = h0 // 2
                        pyb = ps_y[xm % 2]
                        @blk.tensor
                        def _(t, ch=ch, gq=gq, hb=hb, h0=h0, xm=xm, pyb=pyb):
                            t.wait_ge(S_dv.h, m2[("mt", ch, gq)])
                            t.wait_ge(S_ac.h, m2[("sbf", ch)])
                            t.wait_ge(S_dv.h, m2[("dtx", ch, 31)])
                            prev = m2.get(("yev", ch, xm - 2), m2.get(("yev", ch - 1, 31), 0))
                            t.wait_ge(S_dv.h, prev)
                            d0 = (ch % 2) * 4096
                            for hh in range(2):
                                h = h0 + hh
                                lq = (h - hb) * 128
                                t.matmul(pyb[hh * 64:(hh + 1) * 64, 0:128],
                                         dtx[:, d0 + h * 64: d0 + (h + 1) * 64],
                                         mt4[:, lq:lq + 128],
                                         start=True, stop=False, tile_position=(0, hh * 64))
                                _li = t.matmul(pyb[hh * 64:(hh + 1) * 64, 0:128],
                                         S_bfb[:, h * 64:(h + 1) * 64],
                                         c4[:, lq:lq + 128],
                                         start=False, stop=True, tile_position=(0, hh * 64))
                            _li.then_inc(S_pe.h, 1)
                            S_pe.n += 1
                            m2[("y", ch, xm)] = S_pe.n
                        @blk.vector
                        def _(v, ch=ch, xm=xm, pyb=pyb):
                            v.wait_ge(S_pe.h, m2[("y", ch, xm)])
                            _li = v.scalar_tensor_tensor(
                                xn_a[:, xm * THp + ch * Q: xm * THp + (ch + 1) * Q],
                                xsl[:, (ch % 2) * 4096 + xm * 128:(ch % 2) * 4096 + (xm + 1) * 128],
                                dcol_sb[:, xm:xm + 1], pyb[:, 0:128],
                                op0=ALU.mult, op1=ALU.add)
                            _li.then_inc(S_dv.h, 1)
                            S_dv.n += 1
                            m2[("yev", ch, xm)] = S_dv.n
                m2[("xslfree", ch)] = m2[("yev", ch, 31)]
                m2[("dtxfree", ch)] = m2[("y", ch, 31)]
                if ch < NCH - 1:
                    emit_contrib(ch, m2)
                else:
                    m2[("bcfree", ch)] = m2[("y", ch, 31)]
        barrier()
        if phases < 7:
            finish()
            return nc

        pscope2 = ExitStack()
        with pscope2:
            xn_b2 = pscope2.enter_context(nc.sbuf_tensor("xn_b2", [128, DT_D * THp], BF16))
            p4 = ExitStack()
            with p4:
                ysq = p4.enter_context(nc.sbuf_tensor("ysq", [128, 2 * 512], BF16))
                vrow = p4.enter_context(nc.sbuf_tensor("vrow", [1, 512], F32))
                rb_bc = p4.enter_context(nc.sbuf_tensor("rb_bc", [128, 2 * 512], F32))
                gtile = p4.enter_context(nc.sbuf_tensor("gtile", [128, 2 * 512], BF16))
                x1r = p4.enter_context(nc.sbuf_tensor("x1r", [128, 2 * 512], F32))
                ps_v = p4.enter_context(nc.psum_tensor("ps_v", [128, 512], F32))
                ps_rb = p4.enter_context(nc.psum_tensor("ps_rb", [128, 512], F32))
                ps_o = [p4.enter_context(nc.psum_tensor(f"ps_o{j}", [128, 512], F32)) for j in range(2)]
                for si, (off, w) in enumerate([(0, 512), (512, 512)]):
                    mk = {}
                    for g in range(8):
                        for qq in range(4):
                            it = g * 4 + qq
                            @blk.scalar
                            def _(sc, it=it, off=off, w=w):
                                if it >= 2:
                                    sc.wait_ge(S_pe.h, mk[("st", it - 2)])
                                sc.activation(ysq[:, (it % 2) * 512:(it % 2) * 512 + w],
                                              xn_a[:, it * THp + off: it * THp + off + w],
                                              AF.Square).then_inc(S_ac.h, 1)
                                S_ac.n += 1
                                mk[("sq", it)] = S_ac.n
                            @blk.tensor
                            def _(t, it=it, w=w, g=g, qq=qq):
                                t.wait_ge(S_ac.h, mk[("sq", it)])
                                if qq == 0 and g > 0:
                                    t.wait_ge(S_ac.h, mk[("vev", g - 1)])
                                _li = t.matmul(ps_v[0:1, 0:w], onescolb_sb[:],
                                         ysq[:, (it % 2) * 512:(it % 2) * 512 + w],
                                         start=(qq == 0), stop=(qq == 3))
                                _li.then_inc(S_pe.h, 1)
                                S_pe.n += 1
                                mk[("st", it)] = S_pe.n
                        @blk.scalar
                        def _(sc, g=g, w=w):
                            sc.wait_ge(S_pe.h, mk[("st", g * 4 + 3)])
                            if g > 0:
                                sc.wait_ge(S_pe.h, mk[("rbmm", g - 1)])
                            sc.activation(vrow[0:1, 0:w], ps_v[0:1, 0:w], AF.Copy)
                            sc.activation(vrow[0:1, 0:w], vrow[0:1, 0:w], AF.Ln,
                                          bias=eps_sb[0:1, 0:1], scale=1.0 / 512)
                            _li = sc.activation(vrow[0:1, 0:w], vrow[0:1, 0:w], AF.Exp, scale=-0.5)
                            _li.then_inc(S_ac.h, 1)
                            S_ac.n += 1
                            mk[("vev", g)] = S_ac.n
                        @blk.tensor
                        def _(t, g=g, w=w):
                            t.wait_ge(S_ac.h, mk[("vev", g)])
                            if g >= 2:
                                t.wait_ge(S_dv.h, mk[("yn", (g - 2) * 4 + 3)])
                            _li = t.matmul(ps_rb[:, 0:w], ones1f_sb[:], vrow[0:1, 0:w],
                                     start=True, stop=True)
                            _li.then_inc(S_pe.h, 1)
                            S_pe.n += 1
                            mk[("rbmm", g)] = S_pe.n
                        @blk.scalar
                        def _(sc, g=g, w=w):
                            sc.wait_ge(S_pe.h, mk[("rbmm", g)])
                            sc.activation(rb_bc[:, (g % 2) * 512:(g % 2) * 512 + w],
                                          ps_rb[:, 0:w], AF.Copy).then_inc(S_ac.h, 1)
                            S_ac.n += 1
                            mk[("rbev", g)] = S_ac.n
                        for qq in range(4):
                            it = g * 4 + qq
                            @blk.sync
                            def _(s, it=it, off=off):
                                if it >= 2:
                                    s.wait_ge(S_dv.h, mk[("yn", it - 2)])
                                q = rsem("rgt", it)
                                s.dma_start(out=gtile[:, (it % 2) * 512:(it % 2) * 512 + 512],
                                            in_=gate_scr[it * 128:(it + 1) * 128, 3 + off: 3 + off + 512]
                                            ).then_inc(q.h, 16)
                                q.n += 16
                                mk[("gl", it)] = (q.h, q.n)
                            @blk.scalar
                            def _(sc, it=it):
                                sc.wait_ge(*mk[("gl", it)])
                                sc.activation(gtile[:, (it % 2) * 512:(it % 2) * 512 + 512],
                                              gtile[:, (it % 2) * 512:(it % 2) * 512 + 512],
                                              AF.Silu).then_inc(S_ac.h, 1)
                                S_ac.n += 1
                                mk[("gs", it)] = S_ac.n
                            @blk.vector
                            def _(v, it=it, g=g, off=off):
                                v.wait_ge(S_ac.h, mk[("gs", it)])
                                v.wait_ge(S_ac.h, mk[("rbev", g)])
                                v.tensor_tensor(t_scr[:, 0:512],
                                                xn_a[:, it * THp + off: it * THp + off + 512],
                                                rb_bc[:, (g % 2) * 512:(g % 2) * 512 + 512],
                                                op=ALU.mult)
                                v.tensor_scalar(t_scr[:, 0:512], t_scr[:, 0:512],
                                                nwcol_sb[:, it:it + 1], None, op0=ALU.mult)
                                _li = v.tensor_tensor(xn_a[:, it * THp + off: it * THp + off + 512],
                                                t_scr[:, 0:512],
                                                gtile[:, (it % 2) * 512:(it % 2) * 512 + 512],
                                                op=ALU.mult)
                                _li.then_inc(S_dv.h, 1)
                                S_dv.n += 1
                                mk[("yn", it)] = S_dv.n
                    for ct in range(32):
                        @blk.gpsimd
                        def _(g2, ct=ct):
                            if ct >= 2:
                                g2.wait_ge(S_pe.h, mk[("op", ct - 2)])
                            q = rsem("rw", ct)
                            g2.dma_start(out=wbuf[:, (ct % 2) * DM:(ct % 2 + 1) * DM],
                                         in_=wot[ct]).then_inc(q.h, 16)
                            q.n += 16
                            mk[("wo", ct)] = (q.h, q.n)
                        @blk.sync
                        def _(s, ct=ct, off=off):
                            if ct >= 2:
                                s.wait_ge(S_dv.h, mk[("x2", ct - 2)])
                                s.wait_ge(S_ac.h, mk[("x2b", ct - 2)])
                            q = rsem("rx1", ct)
                            s.dma_start(out=x1r[:, (ct % 2) * 512:(ct % 2) * 512 + 512],
                                        in_=x1_scr[ct * 128:(ct + 1) * 128, 3 + off: 3 + off + 512]
                                        ).then_inc(q.h, 16)
                            q.n += 16
                            mk[("x1l", ct)] = (q.h, q.n)
                        @blk.tensor
                        def _(t, ct=ct):
                            t.wait_ge(*mk[("wo", ct)])
                            t.wait_ge(S_dv.h, mk[("yn", 31)])
                            if ct >= 2:
                                t.wait_ge(S_dv.h, mk[("x2", ct - 2)])
                            bb = ps_o[ct % 2]
                            for it in range(32):
                                _li = t.matmul(bb[:],
                                         wbuf[:, (ct % 2) * DM + it * 128:(ct % 2) * DM + (it + 1) * 128],
                                         xn_a[:, it * THp + off: it * THp + off + 512],
                                         start=(it == 0), stop=(it == 31))
                            _li.then_inc(S_pe.h, 1)
                            S_pe.n += 1
                            mk[("op", ct)] = S_pe.n
                        @blk.vector
                        def _(v, ct=ct):
                            v.wait_ge(S_pe.h, mk[("op", ct)])
                            v.wait_ge(*mk[("x1l", ct)])
                            if ct >= 2:
                                v.wait_ge(*mk[("x2s", ct - 2)])
                                v.wait_ge(S_ac.h, mk[("x2b", ct - 2)])
                            _li = v.tensor_tensor(x1r[:, (ct % 2) * 512:(ct % 2) * 512 + 512],
                                            ps_o[ct % 2][:],
                                            x1r[:, (ct % 2) * 512:(ct % 2) * 512 + 512],
                                            op=ALU.add)
                            _li.then_inc(S_dv.h, 1)
                            S_dv.n += 1
                            mk[("x2", ct)] = S_dv.n
                        @blk.sync
                        def _(s, ct=ct, off=off):
                            s.wait_ge(S_dv.h, mk[("x2", ct)])
                            q = rsem("rx2", ct)
                            s.dma_start(out=x2_scr[ct * 128:(ct + 1) * 128, off:off + 512],
                                        in_=x1r[:, (ct % 2) * 512:(ct % 2) * 512 + 512]
                                        ).then_inc(q.h, 16)
                            q.n += 16
                            mk[("x2s", ct)] = (q.h, q.n)
                        @blk.scalar
                        def _(sc, ct=ct, off=off):
                            sc.wait_ge(S_dv.h, mk[("x2", ct)])
                            sc.activation(xn_b2[:, ct * THp + off: ct * THp + off + 512],
                                          x1r[:, (ct % 2) * 512:(ct % 2) * 512 + 512],
                                          AF.Copy).then_inc(S_ac.h, 1)
                            S_ac.n += 1
                            mk[("x2b", ct)] = S_ac.n
                    barrier()
            subw5 = [(0, 512), (512, 512)]
            if phases < 8:
                finish()
                return nc
            ln_stats(xn_b2, subw5)
            ln_fin_bcast_apply(xn_b2, T, subw5)
            barrier()
            if phases < 9:
                finish()
                return nc
            emit_ffn(xn_b2, x2_scr, outT, T, None, "f2")

        finish()

    return nc


_CACHE = {}


def _mask_idx():
    import jax
    with jax.default_device(jax.devices("cpu")[0]):
        u = np.asarray(jax.random.uniform(
            jax.random.key(42, impl="threefry2x32"), (2, 4096, 4096)))
    return ((u >= 0.25).astype(np.uint8) + (u >= 0.5) + (u >= 0.75)).astype(np.uint8)


def _tile_lhsT(Wt):
    Ktot, Mtot = Wt.shape
    Kt, Mt = Ktot // 128, Mtot // 128
    return np.ascontiguousarray(
        Wt.reshape(Kt, 128, Mt, 128).transpose(2, 1, 0, 3).reshape(Mt, 128, Kt * 128))


def prepare_inputs(L0_attn_hidden_states, L0_attn_in_proj_w, L0_attn_conv_weight,
                   L0_attn_conv_bias, L0_attn_dt_bias, L0_attn_A_log, L0_attn_D,
                   L0_attn_norm_weight, L0_attn_out_proj_w, L0_ffn_W0, L0_ffn_b0,
                   L0_ffn_A, L0_ffn_B):
    f32 = np.float32
    x0 = np.asarray(L0_attn_hidden_states, f32)
    Bsz, L, _ = x0.shape
    x_flat = np.ascontiguousarray(x0.reshape(Bsz * L, DM))
    idx = _mask_idx().reshape(Bsz * L, I)

    Bm = np.asarray(L0_ffn_B, f32)
    wbts = np.ascontiguousarray(
        Bm.transpose(2, 0, 1).reshape(16, 4, 32, 128).transpose(2, 0, 1, 3).reshape(32, 16, 512))

    shared = {
        "w0t": _tile_lhsT(np.ascontiguousarray(np.asarray(L0_ffn_W0, f32).T)),
        "wpt": _tile_lhsT(np.ascontiguousarray(
            np.pad(np.asarray(L0_attn_in_proj_w, f32), ((0, 81 * 128 - PROJ), (0, 0))).T)),
        "wot": _tile_lhsT(np.ascontiguousarray(np.asarray(L0_attn_out_proj_w, f32).T)),
        "wat": np.ascontiguousarray(
            np.asarray(L0_ffn_A, f32).transpose(2, 0, 1).reshape(32, 128, 4, 16)
            .transpose(1, 2, 0, 3).reshape(128, 2048)),
        "wbts": wbts,
        "b0c": np.ascontiguousarray(np.asarray(L0_ffn_b0, f32).reshape(32, 128).T),
        "convw": np.ascontiguousarray(
            np.asarray(L0_attn_conv_weight, f32).reshape(48, 128, 4).transpose(1, 0, 2).reshape(128, 192)),
        "convb": np.ascontiguousarray(np.asarray(L0_attn_conv_bias, f32).reshape(48, 128).T),
        "dtbias": np.ascontiguousarray(np.broadcast_to(np.asarray(L0_attn_dt_bias, f32), (128, 64))),
        "alogb": np.ascontiguousarray(np.broadcast_to(np.asarray(L0_attn_A_log, f32), (128, 64))),
        "dcolh": np.ascontiguousarray(np.repeat(np.asarray(L0_attn_D, f32), 64).reshape(32, 128).T),
        "nwcolh": np.ascontiguousarray(np.asarray(L0_attn_norm_weight, f32).reshape(32, 128).T),
        "identb": np.eye(128, dtype=ml_dtypes.bfloat16),
        "identf": np.eye(128, dtype=f32),
        "ones1f": np.ones((1, 128), f32),
        "onescolb": np.ones((128, 1), ml_dtypes.bfloat16),
        "onesrowb": np.ones((1, 512), ml_dtypes.bfloat16),
        "ctrif": np.ascontiguousarray(-np.triu(np.ones((128, 128), f32))),
        "k01f": np.ascontiguousarray(np.triu(np.ones((128, 128), f32))),
    }

    in_maps = []
    for core in range(8):
        b, r = core // 4, core % 4
        r0 = b * L + r * T
        rows = np.zeros((TH, DM), f32)
        mrows = np.zeros((TH, I), np.uint8)
        if r == 0:
            rows[3:] = x_flat[r0:r0 + T]
            mrows[3:] = idx[r0:r0 + T]
        else:
            rows[:] = x_flat[r0 - 3:r0 + T]
            mrows[:] = idx[r0 - 3:r0 + T]
        d = dict(shared)
        d["xT"] = np.ascontiguousarray(rows.T)
        d["m1T"] = np.ascontiguousarray((mrows == 1).astype(np.uint8).T)
        d["m2T"] = np.ascontiguousarray((mrows == 2).astype(np.uint8).T)
        d["m3T"] = np.ascontiguousarray((mrows == 3).astype(np.uint8).T)
        d["halom"] = np.full((128, 1), 0.0 if r == 0 else 1.0, f32)
        act = np.zeros((1, 256), f32)
        for k2 in range(4):
            act[0, k2 * 64:(k2 + 1) * 64] = 1.0 if k2 < r else 0.0
        d["actrow"] = act
        d["epsr"] = np.full((1, 1), EPS, f32)
        d["mselrow"] = act.copy()
        in_maps.append(d)
    return in_maps


def kernel(**inputs):
    in_maps = prepare_inputs(**inputs)
    Bsz = 2
    L = 4096
    f32 = np.float32
    if "nc" not in _CACHE:
        _CACHE["nc"] = build_program()
    res = run_bass_kernel_spmd(_CACHE["nc"], in_maps, core_ids=list(range(8)))

    out = np.empty((Bsz * L, DM), f32)
    for core in range(8):
        b, r = core // 4, core % 4
        r0 = b * L + r * T
        out[r0:r0 + T] = res.results[core]["outT"].T
    return out.reshape(Bsz, L, DM)


# version 03
# speedup vs baseline: 1.0036x; 1.0036x over previous
import numpy as np
import ml_dtypes
from contextlib import ExitStack

import concourse.bass as bass
import concourse.mybir as mybir
from concourse.bass_utils import run_bass_kernel_spmd

F32 = mybir.dt.float32
BF16 = mybir.dt.bfloat16
U8 = mybir.dt.uint8
AF = mybir.ActivationFunctionType
ALU = mybir.AluOpType

H, P, N, G = 64, 64, 128, 8
I = H * P
GN = G * N
CONV_DIM = I + 2 * GN
PROJ = I + CONV_DIM + H
DM = 4096
EPS = 1e-5
T = 1024
TH = T + 3
THp = 1028
Q = 128
NCH = T // Q
DT_D = DM // 128


def build_program(phases=99, debug=False):
    nc = bass.Bass("TRN2", target_bir_lowering=False)
    ap = lambda t: t.ap()
    xT = ap(nc.dram_tensor("xT", [DM, TH], F32, kind="ExternalInput"))
    m1T = ap(nc.dram_tensor("m1T", [DM, TH], U8, kind="ExternalInput"))
    m2T = ap(nc.dram_tensor("m2T", [DM, TH], U8, kind="ExternalInput"))
    m3T = ap(nc.dram_tensor("m3T", [DM, TH], U8, kind="ExternalInput"))
    w0t = ap(nc.dram_tensor("w0t", [32, 128, DM], F32, kind="ExternalInput"))
    wpt = ap(nc.dram_tensor("wpt", [81, 128, DM], F32, kind="ExternalInput"))
    wot = ap(nc.dram_tensor("wot", [32, 128, DM], F32, kind="ExternalInput"))
    wat = ap(nc.dram_tensor("wat", [128, 2048], F32, kind="ExternalInput"))
    wbts = ap(nc.dram_tensor("wbts", [32, 16, 512], F32, kind="ExternalInput"))
    b0c = ap(nc.dram_tensor("b0c", [128, 32], F32, kind="ExternalInput"))
    convw = ap(nc.dram_tensor("convw", [128, 192], F32, kind="ExternalInput"))
    convb = ap(nc.dram_tensor("convb", [128, 48], F32, kind="ExternalInput"))
    dtbias = ap(nc.dram_tensor("dtbias", [128, 64], F32, kind="ExternalInput"))
    alogb = ap(nc.dram_tensor("alogb", [128, 64], F32, kind="ExternalInput"))
    dcolh = ap(nc.dram_tensor("dcolh", [128, 32], F32, kind="ExternalInput"))
    nwcolh = ap(nc.dram_tensor("nwcolh", [128, 32], F32, kind="ExternalInput"))
    identb = ap(nc.dram_tensor("identb", [128, 128], BF16, kind="ExternalInput"))
    identf = ap(nc.dram_tensor("identf", [128, 128], F32, kind="ExternalInput"))
    ones1f = ap(nc.dram_tensor("ones1f", [1, 128], F32, kind="ExternalInput"))
    onescolb = ap(nc.dram_tensor("onescolb", [128, 1], BF16, kind="ExternalInput"))
    onesrowb = ap(nc.dram_tensor("onesrowb", [1, 512], BF16, kind="ExternalInput"))
    ctrif = ap(nc.dram_tensor("ctrif", [128, 128], F32, kind="ExternalInput"))
    k01f = ap(nc.dram_tensor("k01f", [128, 128], F32, kind="ExternalInput"))
    halom = ap(nc.dram_tensor("halom", [128, 1], F32, kind="ExternalInput"))
    actrow = ap(nc.dram_tensor("actrow", [1, 256], F32, kind="ExternalInput"))
    epsr = ap(nc.dram_tensor("epsr", [1, 1], F32, kind="ExternalInput"))
    mselrow = ap(nc.dram_tensor("mselrow", [1, 256], F32, kind="ExternalInput"))

    outT = ap(nc.dram_tensor("outT", [DM, T], F32, kind="ExternalOutput"))

    KD = "ExternalOutput" if debug else "Internal"
    x1_scr = ap(nc.dram_tensor("x1_scr", [DM, THp], F32, kind=KD))
    x2_scr = ap(nc.dram_tensor("x2_scr", [DM, T], F32, kind=KD))
    gate_scr = ap(nc.dram_tensor("gate_scr", [DM, THp], BF16, kind=KD))
    xssm_scr = ap(nc.dram_tensor("xssm_scr", [DM, T], BF16, kind=KD))
    B_scr = ap(nc.dram_tensor("B_scr", [GN, T], BF16, kind=KD))
    C_scr = ap(nc.dram_tensor("C_scr", [GN, T], BF16, kind=KD))
    ca_scr = ap(nc.dram_tensor("ca_scr", [NCH, 8192], F32, kind=KD))
    cae_scr = ap(nc.dram_tensor("cae_scr", [NCH, 64], F32, kind="Internal"))
    coll_in = nc.dram_tensor("coll_in", [129, 4096], F32, kind="Internal")
    coll_out = nc.dram_tensor("coll_out", [516, 4096], F32, kind="Internal")

    es = ExitStack()
    with es:
        sbt = lambda name, shape, dt: es.enter_context(nc.sbuf_tensor(name, shape, dt))
        xn_a = sbt("xn_a", [128, DT_D * THp], BF16)
        wat_sb = sbt("wat_sb", [128, 2048], BF16)
        wbt_r = sbt("wbt_r", [16, 2 * 512], BF16)
        b0h_sb = sbt("b0h_sb", [128, 32], F32)
        identb_sb = sbt("identb_sb", [128, 128], BF16)
        identf_sb = sbt("identf_sb", [128, 128], F32)
        ones1f_sb = sbt("ones1f_sb", [1, 128], F32)
        onescolb_sb = sbt("onescolb_sb", [128, 1], BF16)
        onesrowb_sb = sbt("onesrowb_sb", [1, 512], BF16)
        ctrif_sb = sbt("ctrif_sb", [128, 128], F32)
        k01f_sb = sbt("k01f_sb", [128, 128], F32)
        convw_sb = sbt("convw_sb", [128, 192], F32)
        convb_sb = sbt("convb_sb", [128, 48], F32)
        dtbias_sb = sbt("dtbias_sb", [128, 64], F32)
        expA_sb = sbt("expA_sb", [128, 64], F32)
        dcol_sb = sbt("dcol_sb", [128, 32], F32)
        nwcol_sb = sbt("nwcol_sb", [128, 32], F32)
        halo_sb = sbt("halo_sb", [128, 1], F32)
        actrow_sb = sbt("actrow_sb", [1, 256], F32)
        mselrow_sb = sbt("mselrow_sb", [1, 256], F32)
        eps_sb = sbt("eps_sb", [1, 1], F32)
        dtp_all = sbt("dtp_all", [128, NCH * 64], F32)
        ca_all = sbt("ca_all", [128, NCH * 64], F32)
        caendb_all = sbt("caendb_all", [128, NCH * 64], F32)
        dAtot_all = sbt("dAtot_all", [128, NCH * 64], F32)
        aprod_sb = sbt("aprod_sb", [128, 64], F32)
        rows_t = sbt("rows_t", [128, THp], F32)
        a_bc = sbt("a_bc", [128, THp], F32)
        b_bc = sbt("b_bc", [128, THp], F32)
        t_scr = sbt("t_scr", [128, THp], F32)
        xsqtl = sbt("xsqtl", [128, 2 * THp], BF16)

        r_mu = rows_t[0:1, 0:THp]
        r_v = t_scr[0:1, 0:THp]

        class Sem:
            def __init__(self, name):
                self.h = es.enter_context(nc.semaphore(name))
                self.n = 0
        S_bar = Sem("bar"); S_wld = Sem("wld"); S_xld = Sem("xld")
        S_pe = Sem("pe"); S_dv = Sem("dv"); S_ac = Sem("ac")
        S_sto = Sem("sto"); S_col = Sem("col"); S_gp = Sem("gp")
        rings = {}

        def ring(name, n=2):
            if name not in rings:
                rings[name] = [Sem(f"{name}{i}") for i in range(n)]
            return rings[name]

        def rsem(name, u, n=2):
            return ring(name, n)[u % n]


        blk = es.enter_context(nc.Block())
        nbar = [0]

        def barrier():
            nbar[0] += 1
            sems = [S_pe, S_dv, S_ac, S_col, S_xld, S_wld, S_sto, S_gp]
            for dec in (blk.sync, blk.gpsimd, blk.tensor, blk.vector, blk.scalar):
                @dec
                def _(eng):
                    for q in sems:
                        if q.n:
                            eng.wait_ge(q.h, q.n)
                    for rl in rings.values():
                        for q in rl:
                            if q.n:
                                eng.wait_ge(q.h, q.n)


        def finish():
            @blk.sync
            def _(s):
                s.wait_ge(S_sto.h, S_sto.n)
                s.wait_ge(S_xld.h, S_xld.n)
                for sems in rings.values():
                    for q in sems:
                        if q.n:
                            s.wait_ge(q.h, q.n)

        @blk.sync
        def _(s):
            for dst, src in [
                (identb_sb, identb), (identf_sb, identf), (ones1f_sb, ones1f),
                (onescolb_sb, onescolb), (onesrowb_sb, onesrowb),
                (ctrif_sb, ctrif), (k01f_sb, k01f), (convw_sb, convw),
                (convb_sb, convb), (dtbias_sb, dtbias), (dcol_sb, dcolh),
                (nwcol_sb, nwcolh), (halo_sb, halom),
                (actrow_sb, actrow), (mselrow_sb, mselrow), (eps_sb, epsr),
            ]:
                s.dma_start(out=dst[:], in_=src[:]).then_inc(S_xld.h, 16)
                S_xld.n += 16
            s.dma_start(out=expA_sb[:], in_=alogb[:]).then_inc(S_xld.h, 16)
            s.dma_start(out=b0h_sb[:], in_=b0c[:]).then_inc(S_xld.h, 16)
            S_xld.n += 32

        @blk.gpsimd
        def _(g):
            g.dma_start(out=wat_sb[:], in_=wat[:]).then_inc(S_wld.h, 16)
            S_wld.n += 16

        @blk.scalar
        def _(sc):
            sc.wait_ge(S_xld.h, S_xld.n)
            sc.wait_ge(S_wld.h, S_wld.n)
            sc.activation(expA_sb[:], expA_sb[:], AF.Exp).then_inc(S_ac.h, 1)
            S_ac.n += 1
        @blk.vector
        def _(v):
            v.wait_ge(S_ac.h, S_ac.n)
            _li = v.tensor_scalar(b0h_sb[:], b0h_sb[:], 0.5, None, op0=ALU.mult)
            _li.then_inc(S_dv.h, 1)
            S_dv.n += 1
        barrier()

        _uid = [0]

        def ln_stats(buf_bf, subw):
            _uid[0] += 1
            uid = _uid[0]
            ps = ExitStack()
            with ps:
                psl = ps.enter_context(nc.psum_tensor(f"psl_ln{uid}", [128, 512], F32))
                psl2 = ps.enter_context(nc.psum_tensor(f"psl2_ln{uid}", [128, 512], F32))
                for (off, w) in subw:
                    sq_marks = []
                    pe_base = S_pe.n
                    @blk.scalar
                    def _(sc, off=off, w=w, sq_marks=sq_marks, pe_base=pe_base):
                        for dt in range(DT_D):
                            sc.wait_ge(S_pe.h, pe_base + max(dt - 1, 0))
                            sc.activation(xsqtl[:, (dt % 2) * THp:(dt % 2) * THp + w],
                                          buf_bf[:, dt * THp + off: dt * THp + off + w],
                                          AF.Square).then_inc(S_ac.h, 1)
                            S_ac.n += 1
                            sq_marks.append(S_ac.n)
                    @blk.tensor
                    def _(t, off=off, w=w, sq_marks=sq_marks):
                        for dt in range(DT_D):
                            t.wait_ge(S_ac.h, sq_marks[dt])
                            t.matmul(psl[0:1, 0:w], onescolb_sb[:],
                                     buf_bf[:, dt * THp + off: dt * THp + off + w],
                                     start=(dt == 0), stop=(dt == DT_D - 1))
                            _li = t.matmul(psl2[0:1, 0:w], onescolb_sb[:],
                                     xsqtl[:, (dt % 2) * THp:(dt % 2) * THp + w],
                                     start=(dt == 0), stop=(dt == DT_D - 1))
                            _li.then_inc(S_pe.h, 1)
                            S_pe.n += 1
                    @blk.scalar
                    def _(sc, off=off, w=w):
                        sc.wait_ge(S_pe.h, S_pe.n)
                        sc.activation(r_mu[0:1, off:off + w], psl[0:1, 0:w], AF.Copy)
                        _li = sc.activation(r_v[0:1, off:off + w], psl2[0:1, 0:w], AF.Copy)
                        _li.then_inc(S_ac.h, 1)
                        S_ac.n += 1
                    @blk.tensor
                    def _(t):
                        t.wait_ge(S_ac.h, S_ac.n)

        def ln_fin_bcast_apply(buf_bf, W, subw):
            @blk.vector
            def _(v, W=W):
                v.wait_ge(S_ac.h, S_ac.n)
                v.tensor_scalar(r_mu[0:1, 0:W], r_mu[0:1, 0:W], -1.0 / DM, None, op0=ALU.mult)
                v.tensor_scalar(r_v[0:1, 0:W], r_v[0:1, 0:W], 1.0 / DM, None, op0=ALU.mult)
                v.tensor_tensor(a_bc[0:1, 0:W], r_mu[0:1, 0:W], r_mu[0:1, 0:W], op=ALU.mult)
                _li = v.tensor_tensor(r_v[0:1, 0:W], r_v[0:1, 0:W], a_bc[0:1, 0:W], op=ALU.subtract)
                _li.then_inc(S_dv.h, 1)
                S_dv.n += 1
            @blk.scalar
            def _(sc, W=W):
                sc.wait_ge(S_dv.h, S_dv.n)
                sc.activation(r_v[0:1, 0:W], r_v[0:1, 0:W], AF.Ln, bias=eps_sb[0:1, 0:1])
                _li = sc.activation(r_v[0:1, 0:W], r_v[0:1, 0:W], AF.Exp, scale=-0.5)
                _li.then_inc(S_ac.h, 1)
                S_ac.n += 1
            @blk.vector
            def _(v, W=W):
                v.wait_ge(S_ac.h, S_ac.n)
                _li = v.tensor_tensor(r_mu[0:1, 0:W], r_mu[0:1, 0:W], r_v[0:1, 0:W], op=ALU.mult)
                _li.then_inc(S_dv.h, 1)
                S_dv.n += 1
            ps = ExitStack()
            with ps:
                _uid[0] += 1
                uid = _uid[0]
                pba = ps.enter_context(nc.psum_tensor(f"pbca{uid}", [128, 512], F32))
                pbb = ps.enter_context(nc.psum_tensor(f"pbcb{uid}", [128, 512], F32))
                for (off, w) in subw:
                    @blk.tensor
                    def _(t, off=off, w=w):
                        t.wait_ge(S_dv.h, S_dv.n)
                        t.wait_ge(S_ac.h, S_ac.n)
                        t.matmul(pba[:, 0:w], ones1f_sb[:], r_v[0:1, off:off + w],
                                 start=True, stop=True)
                        _li = t.matmul(pbb[:, 0:w], ones1f_sb[:], r_mu[0:1, off:off + w],
                                 start=True, stop=True)
                        _li.then_inc(S_pe.h, 1)
                        S_pe.n += 1
                    @blk.scalar
                    def _(sc, off=off, w=w):
                        sc.wait_ge(S_pe.h, S_pe.n)
                        sc.activation(a_bc[:, off:off + w], pba[:, 0:w], AF.Copy)
                        _li = sc.activation(b_bc[:, off:off + w], pbb[:, 0:w], AF.Copy)
                        _li.then_inc(S_ac.h, 1)
                        S_ac.n += 1
            @blk.vector
            def _(v, W=W):
                v.wait_ge(S_ac.h, S_ac.n)
                for dt in range(16):
                    c0 = dt * THp
                    v.tensor_tensor(t_scr[:, 0:W], buf_bf[:, c0:c0 + W], a_bc[:, 0:W], op=ALU.mult)
                    _li = v.tensor_tensor(buf_bf[:, c0:c0 + W], t_scr[:, 0:W], b_bc[:, 0:W], op=ALU.add)
                _li.then_inc(S_dv.h, 1)
                S_dv.n += 1
            @blk.gpsimd
            def _(g, W=W):
                g.wait_ge(S_ac.h, S_ac.n)
                for dt in range(16, DT_D):
                    c0 = dt * THp
                    g.tensor_tensor(rows_t[:, 0:W], buf_bf[:, c0:c0 + W], a_bc[:, 0:W], op=ALU.mult)
                    _li = g.tensor_tensor(buf_bf[:, c0:c0 + W], rows_t[:, 0:W], b_bc[:, 0:W], op=ALU.add)
                _li.then_inc(S_gp.h, 1)
                S_gp.n += 1


        subw1 = [(0, 512), (512, 512), (1024, TH - 1024)]
        ld_marks = []
        ac_base1 = S_ac.n
        xring = [a_bc, b_bc]
        @blk.sync
        def _(s):
            for dt in range(DT_D):
                if dt >= 2:
                    s.wait_ge(S_ac.h, ac_base1 + dt - 1)
                q = rsem("p1x", dt)
                s.dma_start(out=xring[dt % 2][:, 0:TH],
                            in_=xT[dt * 128:(dt + 1) * 128, :]).then_inc(q.h, 16)
                q.n += 16
                ld_marks.append((q.h, q.n))
        @blk.scalar
        def _(sc):
            for dt in range(DT_D):
                sc.wait_ge(*ld_marks[dt])
                sc.activation(xn_a[:, dt * THp: dt * THp + TH], xring[dt % 2][:, 0:TH],
                              AF.Copy).then_inc(S_ac.h, 1)
                S_ac.n += 1
        barrier()
        ln_stats(xn_a, subw1)
        ln_fin_bcast_apply(xn_a, TH, subw1)
        barrier()

        def emit_ffn(inbuf, res_src, out_dst, W, nxt_buf, tag):
            subw = [(0, 512), (512, 512)] + ([(1024, W - 1024)] if W > 1024 else [])
            nsub = len(subw)
            moff0 = 3 if W == T else 0
            ps = ExitStack()
            with ps:
                psq = [ps.enter_context(nc.psum_tensor(f"psq{tag}{l}", [128, 512], F32))
                       for l in range(4)]
                psb = [ps.enter_context(nc.psum_tensor(f"psb{tag}{j}", [128, 512], F32))
                       for j in range(2)]
                psacc = ps.enter_context(nc.psum_tensor(f"psacc{tag}", [128, 512], F32))
                wbuf = ps.enter_context(nc.sbuf_tensor(f"wbuf{tag}", [128, 2 * DM], BF16))
                hbf = ps.enter_context(nc.sbuf_tensor(f"hbf{tag}", [16, 4 * THp], BF16))
                xtl = ps.enter_context(nc.sbuf_tensor(f"xtl{tag}", [128, 2 * 512], F32))
                mtl = ps.enter_context(nc.sbuf_tensor(f"mtl{tag}", [128, 6 * 512], U8))
                x1tl = ps.enter_context(nc.sbuf_tensor(f"x1tl{tag}", [128, 2 * 512], F32))
                for si, (off, w) in enumerate(subw):
                    @blk.tensor
                    def _(t, si=si, off=off, w=w):
                        if si > 0:
                            t.wait_ge(S_ac.h, S_ac.n)
                        for l in range(4):
                            for dt in range(DT_D):
                                _li = t.matmul(psq[l][0:16, 0:w],
                                         wat_sb[:, l * 512 + dt * 16: l * 512 + (dt + 1) * 16],
                                         inbuf[:, dt * THp + off: dt * THp + off + w],
                                         start=(dt == 0), stop=(dt == DT_D - 1))
                        _li.then_inc(S_pe.h, 1)
                        S_pe.n += 1
                    @blk.scalar
                    def _(sc, si=si, off=off, w=w):
                        sc.wait_ge(S_pe.h, S_pe.n)
                        for l in range(4):
                            _li = sc.activation(hbf[:, l * THp + off: l * THp + off + w],
                                          psq[l][0:16, 0:w], AF.Copy)
                        _li.then_inc(S_ac.h, 1)
                        S_ac.n += 1
                barrier()
                wldm = {}; pem = {}; dvm = {}; accm = {}; xldm = {}
                stom = {}; x1m = {}; wbm = {}
                for ct in range(32):
                    @blk.gpsimd
                    def _(g, ct=ct):
                        if ct >= 2:
                            g.wait_ge(S_pe.h, pem[(ct - 2, nsub - 1)])
                        q = rsem("rw", ct)
                        g.dma_start(out=wbuf[:, (ct % 2) * DM:(ct % 2 + 1) * DM],
                                    in_=w0t[ct]).then_inc(q.h, 16)
                        g.dma_start(out=wbt_r[:, (ct % 2) * 512:(ct % 2 + 1) * 512],
                                    in_=wbts[ct]).then_inc(q.h, 16)
                        q.n += 32
                        wldm[ct] = (q.h, q.n)
                    for si, (off, w) in enumerate(subw):
                        u = ct * nsub + si
                        slot = u % 2
                        @blk.sync
                        def _(s, ct=ct, off=off, w=w, u=u, slot=slot):
                            if u >= 2:
                                s.wait_ge(S_dv.h, dvm[u - 2])
                            q = rsem("rm", u)
                            s.dma_start(out=xtl[:, slot * 512: slot * 512 + w],
                                        in_=res_src[ct * 128:(ct + 1) * 128, off:off + w]
                                        ).then_inc(q.h, 16)
                            mo = off + moff0
                            for mi, msrc in enumerate((m1T, m2T, m3T)):
                                s.dma_start(out=mtl[:, (slot * 3 + mi) * 512:(slot * 3 + mi) * 512 + w],
                                            in_=msrc[ct * 128:(ct + 1) * 128, mo:mo + w]
                                            ).then_inc(q.h, 16)
                            q.n += 64
                            xldm[u] = (q.h, q.n)
                        @blk.tensor
                        def _(t, ct=ct, si=si, off=off, w=w, u=u):
                            t.wait_ge(*wldm[ct])
                            if u >= 2:
                                t.wait_ge(S_dv.h, dvm[u - 2])
                            bb = psb[u % 2]
                            for dt in range(DT_D):
                                t.matmul(bb[:, 0:w],
                                         wbuf[:, (ct % 2) * DM + dt * 128:(ct % 2) * DM + (dt + 1) * 128],
                                         inbuf[:, dt * THp + off: dt * THp + off + w],
                                         start=(dt == 0), stop=(dt == DT_D - 1))
                            if u >= 1:
                                t.wait_ge(S_dv.h, dvm[u - 1])
                            for l in range(4):
                                _li = t.matmul(psq[l][:, 0:w],
                                         wbt_r[:, (ct % 2) * 512 + l * 128:(ct % 2) * 512 + (l + 1) * 128],
                                         hbf[:, l * THp + off: l * THp + off + w],
                                         start=True, stop=True)
                            _li.then_inc(S_pe.h, 1)
                            S_pe.n += 1
                            pem[(ct, si)] = S_pe.n
                        @blk.vector
                        def _(v, ct=ct, si=si, w=w, u=u, slot=slot):
                            v.wait_ge(S_pe.h, pem[(ct, si)])
                            v.wait_ge(*xldm[u])
                            if u >= 2:
                                v.wait_ge(*stom[u - 2])
                                if nxt_buf is not None:
                                    v.wait_ge(S_ac.h, x1m[u - 2])
                            v.tensor_copy(psacc[:, 0:w], psq[0][:, 0:w])
                            mb = slot * 3 * 512
                            v.copy_predicated(psacc[:, 0:w], mtl[:, mb:mb + w], psq[1][:, 0:w])
                            v.copy_predicated(psacc[:, 0:w], mtl[:, mb + 512:mb + 512 + w], psq[2][:, 0:w])
                            v.copy_predicated(psacc[:, 0:w], mtl[:, mb + 1024:mb + 1024 + w], psq[3][:, 0:w])
                            v.scalar_tensor_tensor(t_scr[:, 0:w], psb[u % 2][:, 0:w], 0.5,
                                                   xtl[:, slot * 512: slot * 512 + w],
                                                   op0=ALU.mult, op1=ALU.add)
                            v.tensor_scalar(t_scr[:, 0:w], t_scr[:, 0:w], b0h_sb[:, ct:ct + 1],
                                            None, op0=ALU.add)
                            _li = v.scalar_tensor_tensor(x1tl[:, slot * 512: slot * 512 + w],
                                                   psacc[:, 0:w], 4.0, t_scr[:, 0:w],
                                                   op0=ALU.mult, op1=ALU.add)
                            _li.then_inc(S_dv.h, 1)
                            S_dv.n += 1
                            dvm[u] = S_dv.n
                        @blk.sync
                        def _(s, ct=ct, off=off, w=w, u=u, slot=slot):
                            s.wait_ge(S_dv.h, dvm[u])
                            q = rsem("rs1", u)
                            s.dma_start(out=out_dst[ct * 128:(ct + 1) * 128, off:off + w],
                                        in_=x1tl[:, slot * 512: slot * 512 + w]).then_inc(q.h, 16)
                            q.n += 16
                            stom[u] = (q.h, q.n)
                        if nxt_buf is not None:
                            @blk.scalar
                            def _(sc, ct=ct, off=off, w=w, u=u, slot=slot):
                                sc.wait_ge(S_dv.h, dvm[u])
                                sc.activation(nxt_buf[:, ct * THp + off: ct * THp + off + w],
                                              x1tl[:, slot * 512: slot * 512 + w],
                                              AF.Copy).then_inc(S_ac.h, 1)
                                S_ac.n += 1
                                x1m[u] = S_ac.n
            barrier()

        if phases < 2:
            finish()
            return nc
        pscope = ExitStack()
        with pscope:
            xn_b = pscope.enter_context(nc.sbuf_tensor("xn_b", [128, DT_D * THp], BF16))
            emit_ffn(xn_a, xT, x1_scr, TH, xn_b, "f1")
            if phases < 3:
                finish()
                return nc
            ln_stats(xn_b, subw1)
            ln_fin_bcast_apply(xn_b, TH, subw1)
            barrier()
            if phases < 4:
                finish()
                return nc

            p2 = ExitStack()
            with p2:
                psip = [p2.enter_context(nc.psum_tensor(f"psip{j}", [128, 512], F32)) for j in range(2)]
                wbuf = p2.enter_context(nc.sbuf_tensor("wbufp2", [128, 2 * DM], BF16))
                hbcT = p2.enter_context(nc.sbuf_tensor("hbcT", [128, THp], F32))
                cacc = p2.enter_context(nc.sbuf_tensor("cacc", [128, T], F32))
                gtt = p2.enter_context(nc.sbuf_tensor("gtt", [128, 2 * 512], BF16))
                xbcb = p2.enter_context(nc.sbuf_tensor("xbcb", [128, 2 * T], BF16))
                wldm = {}; pem = {}; evm = {}; cvm = {}; stom = {}; silm = {}
                for ct in range(80):
                    @blk.gpsimd
                    def _(g, ct=ct):
                        if ct >= 2:
                            g.wait_ge(S_pe.h, pem[(ct - 2, 2)])
                        q = rsem("rw", ct)
                        g.dma_start(out=wbuf[:, (ct % 2) * DM:(ct % 2 + 1) * DM],
                                    in_=wpt[ct]).then_inc(q.h, 16)
                        q.n += 16
                        wldm[ct] = (q.h, q.n)
                    for si, (off, w) in enumerate(subw1):
                        u = ct * 3 + si
                        @blk.tensor
                        def _(t, ct=ct, si=si, off=off, w=w, u=u):
                            t.wait_ge(*wldm[ct])
                            if u >= 2:
                                t.wait_ge(S_ac.h, evm[u - 2])
                            bb = psip[u % 2]
                            for dt in range(DT_D):
                                _li = t.matmul(bb[:, 0:w],
                                         wbuf[:, (ct % 2) * DM + dt * 128:(ct % 2) * DM + (dt + 1) * 128],
                                         xn_b[:, dt * THp + off: dt * THp + off + w],
                                         start=(dt == 0), stop=(dt == DT_D - 1))
                            _li.then_inc(S_pe.h, 1)
                            S_pe.n += 1
                            pem[(ct, si)] = S_pe.n
                        @blk.scalar
                        def _(sc, ct=ct, si=si, off=off, w=w, u=u):
                            sc.wait_ge(S_pe.h, pem[(ct, si)])
                            if ct < 32:
                                if u >= 2 and ("g", u - 2) in stom:
                                    sc.wait_ge(*stom[("g", u - 2)])
                                sc.activation(gtt[:, (u % 2) * 512:(u % 2) * 512 + w],
                                              psip[u % 2][:, 0:w], AF.Copy).then_inc(S_ac.h, 1)
                            else:
                                if si == 0 and ct >= 33:
                                    sc.wait_ge(S_dv.h, cvm.get(ct - 1, 0))
                                sc.activation(hbcT[:, off:off + w],
                                              psip[u % 2][:, 0:w], AF.Copy).then_inc(S_ac.h, 1)
                            S_ac.n += 1
                            evm[u] = S_ac.n
                        if ct < 32:
                            @blk.sync
                            def _(s, ct=ct, si=si, off=off, w=w, u=u):
                                s.wait_ge(S_ac.h, evm[u])
                                q = rsem("rg", u)
                                s.dma_start(out=gate_scr[ct * 128:(ct + 1) * 128, off:off + w],
                                            in_=gtt[:, (u % 2) * 512:(u % 2) * 512 + w]
                                            ).then_inc(q.h, 16)
                                q.n += 16
                                stom[("g", u)] = (q.h, q.n)
                    if ct >= 32:
                        hc = ct - 32
                        @blk.vector
                        def _(v, ct=ct, hc=hc):
                            v.wait_ge(S_ac.h, evm[ct * 3 + 2])
                            if hc >= 1:
                                v.wait_ge(S_ac.h, silm.get(hc - 1, 0))
                            o0 = (hc % 2) * T
                            v.tensor_scalar(hbcT[:, 0:3], hbcT[:, 0:3],
                                            halo_sb[:, 0:1], None, op0=ALU.mult)
                            v.tensor_scalar(cacc[:, 0:T], hbcT[:, 0:T],
                                            convw_sb[:, hc * 4: hc * 4 + 1],
                                            convb_sb[:, hc:hc + 1], op0=ALU.mult, op1=ALU.add)
                            for k in range(1, 4):
                                _li = v.scalar_tensor_tensor(cacc[:, 0:T], hbcT[:, k:k + T],
                                                       convw_sb[:, hc * 4 + k: hc * 4 + k + 1],
                                                       cacc[:, 0:T], op0=ALU.mult, op1=ALU.add)
                            _li.then_inc(S_dv.h, 1)
                            S_dv.n += 1
                            cvm[ct] = S_dv.n
                        @blk.scalar
                        def _(sc, ct=ct, hc=hc):
                            sc.wait_ge(S_dv.h, cvm[ct])
                            o0 = (hc % 2) * T
                            if hc >= 2 and ("x", hc - 2) in stom:
                                sc.wait_ge(*stom[("x", hc - 2)])
                            sc.activation(xbcb[:, o0:o0 + T], cacc[:, 0:T],
                                          AF.Silu).then_inc(S_ac.h, 1)
                            S_ac.n += 1
                            silm[hc] = S_ac.n
                        @blk.sync
                        def _(s, hc=hc):
                            s.wait_ge(S_ac.h, silm[hc])
                            if hc < 32:
                                dst = xssm_scr[hc * 128:(hc + 1) * 128, :]
                            elif hc < 40:
                                dst = B_scr[(hc - 32) * 128:(hc - 31) * 128, :]
                            else:
                                dst = C_scr[(hc - 40) * 128:(hc - 39) * 128, :]
                            q = rsem("rxs", hc)
                            s.dma_start(out=dst, in_=xbcb[:, (hc % 2) * T:(hc % 2) * T + T]
                                        ).then_inc(q.h, 16)
                            q.n += 16
                            stom[("x", hc)] = (q.h, q.n)
                @blk.gpsimd
                def _(g):
                    g.wait_ge(S_pe.h, pem[(78, 2)])
                    q = rsem("rw", 80)
                    g.dma_start(out=wbuf[:, 0:DM], in_=wpt[80]).then_inc(q.h, 16)
                    q.n += 16
                wld80 = (rsem("rw", 80).h, rsem("rw", 80).n)
                ps_dt = p2.enter_context(nc.psum_tensor("ps_dt", [128, 512], F32))
                ps_ca = p2.enter_context(nc.psum_tensor("ps_ca", [128, 512], F32))
                ps_tr_full = p2.enter_context(nc.psum_tensor("ps_tr", [128, 512], F32))
                ps_bc = p2.enter_context(nc.psum_tensor("ps_bc", [128, 512], F32))
                dt_sc = p2.enter_context(nc.sbuf_tensor("dt_sc", [128, 64], F32))
                dt_s2 = p2.enter_context(nc.sbuf_tensor("dt_s2", [128, 64], F32))
                dt_s3 = p2.enter_context(nc.sbuf_tensor("dt_s3", [128, 64], F32))
                caf_sc = p2.enter_context(nc.sbuf_tensor("caf_sc", [64, 128], F32))
                cer_sc = p2.enter_context(nc.sbuf_tensor("cer_sc", [1, 64], F32))
                for ch in range(NCH):
                    @blk.tensor
                    def _(t, ch=ch):
                        t.wait_ge(*wld80)
                        if ch > 0:
                            t.wait_ge(S_ac.h, S_ac.n)
                        for dt in range(DT_D):
                            _li = t.matmul(ps_dt[:, 0:64],
                                     xn_b[:, dt * THp + 3 + ch * Q: dt * THp + 3 + (ch + 1) * Q],
                                     wbuf[:, dt * 128: dt * 128 + 64],
                                     start=(dt == 0), stop=(dt == DT_D - 1))
                        _li.then_inc(S_pe.h, 1)
                        S_pe.n += 1
                    pe1 = S_pe.n
                    @blk.vector
                    def _(v, pe1=pe1):
                        v.wait_ge(S_pe.h, pe1)
                        _li = v.tensor_tensor(dt_sc[:], ps_dt[:, 0:64], dtbias_sb[:], op=ALU.add)
                        _li.then_inc(S_dv.h, 1)
                        S_dv.n += 1
                    dv1 = S_dv.n
                    @blk.scalar
                    def _(sc, dv1=dv1):
                        sc.wait_ge(S_dv.h, dv1)
                        sc.activation(dt_s2[:], dt_sc[:], AF.Abs)
                        sc.activation(dt_s2[:], dt_s2[:], AF.Exp, scale=-1.0)
                        _li = sc.activation(dt_s2[:], dt_s2[:], AF.Ln, bias=1.0)
                        _li.then_inc(S_ac.h, 1)
                        S_ac.n += 1
                    ac1 = S_ac.n
                    @blk.vector
                    def _(v, ch=ch, ac1=ac1):
                        v.wait_ge(S_ac.h, ac1)
                        v.tensor_scalar(dt_s3[:], dt_sc[:], 0.0, None, op0=ALU.max)
                        v.tensor_tensor(dtp_all[:, ch * 64:(ch + 1) * 64], dt_s3[:], dt_s2[:], op=ALU.add)
                        _li = v.tensor_tensor(dt_s3[:], dtp_all[:, ch * 64:(ch + 1) * 64], expA_sb[:], op=ALU.mult)
                        _li.then_inc(S_dv.h, 1)
                        S_dv.n += 1
                    dv2 = S_dv.n
                    @blk.tensor
                    def _(t, dv2=dv2):
                        t.wait_ge(S_dv.h, dv2)
                        _li = t.matmul(ps_ca[:, 0:64], ctrif_sb[:], dt_s3[:], start=True, stop=True)
                        _li.then_inc(S_pe.h, 1)
                        S_pe.n += 1
                    pe2 = S_pe.n
                    @blk.scalar
                    def _(sc, ch=ch, pe2=pe2):
                        sc.wait_ge(S_pe.h, pe2)
                        _li = sc.activation(ca_all[:, ch * 64:(ch + 1) * 64], ps_ca[:, 0:64], AF.Copy)
                        _li.then_inc(S_ac.h, 1)
                        S_ac.n += 1
                    ac2 = S_ac.n
                    @blk.tensor
                    def _(t, ch=ch, ac2=ac2):
                        t.wait_ge(S_ac.h, ac2)
                        _li = t.transpose(ps_tr_full[0:64, 0:128], ca_all[:, ch * 64:(ch + 1) * 64], identf_sb[:])
                        _li.then_inc(S_pe.h, 1)
                        S_pe.n += 1
                    pe3 = S_pe.n
                    @blk.scalar
                    def _(sc, pe3=pe3):
                        sc.wait_ge(S_pe.h, pe3)
                        sc.activation(caf_sc[:], ps_tr_full[0:64, 0:128], AF.Copy).then_inc(S_ac.h, 1)
                        S_ac.n += 1
                    ac3 = S_ac.n
                    @blk.sync
                    def _(s, ch=ch, ac3=ac3):
                        s.wait_ge(S_ac.h, ac3)
                        q = rsem("rcs", 0, n=1)
                        s.dma_start(out=ca_scr[ch:ch + 1, :].rearrange("o (h q) -> h (o q)", h=64),
                                    in_=caf_sc[:]).then_inc(q.h, 16)
                        s.dma_start(out=cae_scr[ch:ch + 1, :].rearrange("o h -> h o"),
                                    in_=caf_sc[:, 127:128]).then_inc(q.h, 16)
                        q.n += 32
                        s.wait_ge(q.h, q.n)
                        q2 = rsem("rcb", 0, n=1)
                        s.dma_start(out=cer_sc[:], in_=cae_scr[ch:ch + 1, :]).then_inc(q2.h, 16)
                        q2.n += 16
                    xl1 = (rsem("rcb", 0, n=1).h, rsem("rcb", 0, n=1).n)
                    @blk.tensor
                    def _(t, xl1=xl1):
                        t.wait_ge(*xl1)
                        _li = t.matmul(ps_bc[:, 0:64], ones1f_sb[:], cer_sc[:], start=True, stop=True)
                        _li.then_inc(S_pe.h, 1)
                        S_pe.n += 1
                    pe4 = S_pe.n
                    @blk.scalar
                    def _(sc, ch=ch, pe4=pe4):
                        sc.wait_ge(S_pe.h, pe4)
                        sc.activation(caendb_all[:, ch * 64:(ch + 1) * 64], ps_bc[:, 0:64], AF.Copy)
                        _li = sc.activation(dAtot_all[:, ch * 64:(ch + 1) * 64], ps_bc[:, 0:64], AF.Exp)
                        _li.then_inc(S_ac.h, 1)
                        S_ac.n += 1
                @blk.vector
                def _(v):
                    v.wait_ge(S_ac.h, S_ac.n)
                    v.tensor_copy(aprod_sb[:], dAtot_all[:, 0:64])
                    for ch in range(1, NCH):
                        _li = v.tensor_tensor(aprod_sb[:], aprod_sb[:],
                                        dAtot_all[:, ch * 64:(ch + 1) * 64], op=ALU.mult)
                    _li.then_inc(S_dv.h, 1)
                    S_dv.n += 1
            barrier()
            if phases < 5:
                finish()
                return nc
        barrier()

        p3 = ExitStack()
        with p3:
            S_cur = p3.enter_context(nc.sbuf_tensor("S_cur", [128, 4096], F32))
            S_bfb = p3.enter_context(nc.sbuf_tensor("S_bfb", [128, 4096], BF16))
            xsl = p3.enter_context(nc.sbuf_tensor("xsl", [128, 2 * 4096], BF16))
            dtx = p3.enter_context(nc.sbuf_tensor("dtx", [128, 2 * 4096], BF16))
            bsl = p3.enter_context(nc.sbuf_tensor("bsl", [128, 2 * 1024], BF16))
            csl = p3.enter_context(nc.sbuf_tensor("csl", [128, 2 * 1024], BF16))
            btok = p3.enter_context(nc.sbuf_tensor("btok", [128, 8 * 128], BF16))
            dtxw = p3.enter_context(nc.sbuf_tensor("dtxw", [128, 512], BF16))
            wcol = p3.enter_context(nc.sbuf_tensor("wcol", [128, 64], F32))
            ps_tx_full = p3.enter_context(nc.psum_tensor("ps_tx", [128, 2048], BF16))
            ps_S = p3.enter_context(nc.psum_tensor("ps_S", [128, 512], F32))

            def slab_rearr(scr):
                return scr.rearrange("(a p) t -> p a t", p=128)

            def emit_slabs(ch, mk, want_bc):
                @blk.sync
                def _(s, ch=ch):
                    if ch >= 2:
                        s.wait_ge(S_dv.h, mk[("xslfree", ch - 2)])
                        s.wait_ge(S_dv.h, mk[("bcfree", ch - 2)])
                    q = rsem("rsl", ch)
                    s.dma_start(
                        out=xsl[:, (ch % 2) * 4096:(ch % 2 + 1) * 4096].rearrange(
                            "p (a t) -> p a t", a=32),
                        in_=slab_rearr(xssm_scr)[:, :, ch * Q:(ch + 1) * Q]).then_inc(q.h, 16)
                    q.n += 16
                    if want_bc:
                        s.dma_start(
                            out=csl[:, (ch % 2) * 1024:(ch % 2 + 1) * 1024].rearrange(
                                "p (a t) -> p a t", a=8),
                            in_=slab_rearr(C_scr)[:, :, ch * Q:(ch + 1) * Q]).then_inc(q.h, 16)
                        q.n += 16
                    s.dma_start(
                        out=bsl[:, (ch % 2) * 1024:(ch % 2 + 1) * 1024].rearrange(
                            "p (a t) -> p a t", a=8),
                        in_=slab_rearr(B_scr)[:, :, ch * Q:(ch + 1) * Q]).then_inc(q.h, 16)
                    q.n += 16
                    mk[("xld", ch)] = (q.h, q.n)

            def emit_dtx(ch, mk):
                for m in range(32):
                    tb = (m % 2) * 1024
                    @blk.tensor
                    def _(t, ch=ch, m=m, tb=tb):
                        if m == 0:
                            t.wait_ge(*mk[("xld", ch)])
                            if ch >= 2:
                                t.wait_ge(S_pe.h, mk.get(("dtxfree", ch - 2), 0))
                            if ("btkev_last", ch - 1) in mk:
                                t.wait_ge(S_ac.h, mk[("btkev_last", ch - 1)])
                        if m > 1:
                            t.wait_ge(S_dv.h, mk[("dtx", ch, m - 2)])
                        _li = t.transpose(ps_tx_full[:, tb:tb + 128],
                                    xsl[:, (ch % 2) * 4096 + m * 128:(ch % 2) * 4096 + (m + 1) * 128],
                                    identb_sb[:])
                        _li.then_inc(S_pe.h, 1)
                        S_pe.n += 1
                        mk[("tx", ch, m)] = S_pe.n
                    @blk.vector
                    def _(v, ch=ch, m=m, tb=tb):
                        v.wait_ge(S_pe.h, mk[("tx", ch, m)])
                        d0 = (ch % 2) * 4096 + m * 128
                        for hh in range(2):
                            h = 2 * m + hh
                            _li = v.tensor_scalar(dtx[:, d0 + hh * 64: d0 + (hh + 1) * 64],
                                            ps_tx_full[:, tb + hh * 64: tb + (hh + 1) * 64],
                                            dtp_all[:, ch * 64 + h: ch * 64 + h + 1], None,
                                            op0=ALU.mult)
                        _li.then_inc(S_dv.h, 1)
                        S_dv.n += 1
                        mk[("dtx", ch, m)] = S_dv.n
                mk[("xslfree", ch)] = mk[("dtx", ch, 31)]

            def emit_contrib(ch, mk):
                @blk.vector
                def _(v, ch=ch):
                    v.wait_ge(S_dv.h, mk[("dtx", ch, 31)])
                    _li = v.tensor_tensor(wcol[:], caendb_all[:, ch * 64:(ch + 1) * 64],
                                    ca_all[:, ch * 64:(ch + 1) * 64], op=ALU.subtract)
                    _li.then_inc(S_dv.h, 1)
                    S_dv.n += 1
                dvw = S_dv.n
                @blk.scalar
                def _(sc, dvw=dvw):
                    sc.wait_ge(S_dv.h, dvw)
                    sc.activation(wcol[:], wcol[:], AF.Exp).then_inc(S_ac.h, 1)
                    S_ac.n += 1
                acw = S_ac.n
                for g in range(8):
                    @blk.tensor
                    def _(t, ch=ch, g=g, acw=acw):
                        if g == 0:
                            t.wait_ge(S_ac.h, acw)
                        else:
                            t.wait_ge(S_ac.h, mk[("btk_ev", ch, g - 1)])
                        _li = t.transpose(ps_tx_full[:, 0:128],
                                    bsl[:, (ch % 2) * 1024 + g * 128:(ch % 2) * 1024 + (g + 1) * 128],
                                    identb_sb[:])
                        _li.then_inc(S_pe.h, 1)
                        S_pe.n += 1
                        mk[("btk", ch, g)] = S_pe.n
                    @blk.scalar
                    def _(sc, ch=ch, g=g):
                        sc.wait_ge(S_pe.h, mk[("btk", ch, g)])
                        sc.activation(btok[:, g * 128:(g + 1) * 128], ps_tx_full[:, 0:128], AF.Copy).then_inc(S_ac.h, 1)
                        S_ac.n += 1
                        mk[("btk_ev", ch, g)] = S_ac.n
                    @blk.vector
                    def _(v, ch=ch, g=g, acw=acw):
                        v.wait_ge(S_ac.h, acw)
                        if g > 0:
                            v.wait_ge(S_pe.h, mk[("ctr", ch, g - 1)])
                        d0 = (ch % 2) * 4096 + g * 512
                        for hh in range(8):
                            h = g * 8 + hh
                            _li = v.tensor_scalar(dtxw[:, hh * 64:(hh + 1) * 64],
                                            dtx[:, d0 + hh * 64: d0 + (hh + 1) * 64],
                                            wcol[:, h:h + 1], None, op0=ALU.mult)
                        _li.then_inc(S_dv.h, 1)
                        S_dv.n += 1
                        mk[("dtxw", ch, g)] = S_dv.n
                    @blk.tensor
                    def _(t, ch=ch, g=g):
                        t.wait_ge(S_ac.h, mk[("btk_ev", ch, g)])
                        t.wait_ge(S_dv.h, mk[("dtxw", ch, g)])
                        prev = mk[("supd", ch, g - 1)] if g > 0 else mk.get(("supd_last", ch), 0)
                        t.wait_ge(S_dv.h, prev)
                        _li = t.matmul(ps_S[:], btok[:, g * 128:(g + 1) * 128], dtxw[:], start=True, stop=True)
                        _li.then_inc(S_pe.h, 1)
                        S_pe.n += 1
                        mk[("ctr", ch, g)] = S_pe.n
                    @blk.vector
                    def _(v, ch=ch, g=g):
                        v.wait_ge(S_pe.h, mk[("ctr", ch, g)])
                        for hh in range(8):
                            h = g * 8 + hh
                            _li = v.scalar_tensor_tensor(
                                S_cur[:, h * 64:(h + 1) * 64], S_cur[:, h * 64:(h + 1) * 64],
                                dAtot_all[:, ch * 64 + h: ch * 64 + h + 1],
                                ps_S[:, hh * 64:(hh + 1) * 64], op0=ALU.mult, op1=ALU.add)
                        _li.then_inc(S_dv.h, 1)
                        S_dv.n += 1
                        mk[("supd", ch, g)] = S_dv.n
                mk[("supd_last", ch + 1)] = mk[("supd", ch, 7)]
                mk[("bcfree", ch)] = mk[("supd", ch, 7)]

            m1 = {("supd_last", 0): 0}
            @blk.vector
            def _(v):
                _li = v.memset(S_cur[:], 0.0)
                _li.then_inc(S_dv.h, 1)
                S_dv.n += 1
            for ch in range(NCH):
                emit_slabs(ch, m1, want_bc=False)
                emit_dtx(ch, m1)
                emit_contrib(ch, m1)
            @blk.sync
            def _(s):
                s.wait_ge(S_dv.h, m1[("supd", NCH - 1, 7)])
                q = rsem("rcp", 0, n=1)
                s.dma_start(out=coll_in.ap()[0:128, :], in_=S_cur[:]).then_inc(q.h, 16)
                s.dma_start(out=coll_in.ap()[128:129, :].rearrange("p (a f) -> p a f", a=64),
                            in_=aprod_sb[0:1, :].rearrange("p (o f) -> p o f", o=1
                                                           ).broadcast_to([1, 64, 64])
                            ).then_inc(q.h, 16)
                q.n += 32
            sto_cl = (rsem("rcp", 0, n=1).h, rsem("rcp", 0, n=1).n)
            @blk.gpsimd
            def _(g):
                g.wait_ge(*sto_cl)
                g.collective_compute("AllGather", ALU.bypass,
                                     replica_groups=[[0, 1, 2, 3], [4, 5, 6, 7]],
                                     ins=[coll_in.ap().opt()],
                                     outs=[coll_out.ap().opt()]).then_inc(S_col.h, 1)
                S_col.n += 1
            pcmb = ExitStack()
            pcmb.__enter__()
            sg = pcmb.enter_context(nc.sbuf_tensor("sg", [128, 4096], F32))
            arow = pcmb.enter_context(nc.sbuf_tensor("arow", [1, 256], F32))
            wrow = pcmb.enter_context(nc.sbuf_tensor("wrow", [1, 320], F32))
            wbc = pcmb.enter_context(nc.sbuf_tensor("wbc", [128, 64], F32))
            ps_wb = pcmb.enter_context(nc.psum_tensor("ps_wb", [128, 512], F32))
            @blk.sync
            def _(s):
                s.wait_ge(S_col.h, S_col.n)
                q = rsem("rar", 0, n=1)
                for j in range(4):
                    s.dma_start(out=arow[0:1, j * 64:(j + 1) * 64],
                                in_=coll_out.ap()[129 * j + 128: 129 * j + 129, 0:64]
                                ).then_inc(q.h, 16)
                q.n += 64
            xl_ar = (rsem("rar", 0, n=1).h, rsem("rar", 0, n=1).n)
            @blk.vector
            def _(v, xl_ar=xl_ar):
                v.wait_ge(*xl_ar)
                for k2 in range(4):
                    v.tensor_scalar(wrow[0:1, k2 * 64:(k2 + 1) * 64],
                                    arow[0:1, k2 * 64:(k2 + 1) * 64], -1.0, None, op0=ALU.add)
                    v.tensor_tensor(wrow[0:1, k2 * 64:(k2 + 1) * 64],
                                    wrow[0:1, k2 * 64:(k2 + 1) * 64],
                                    actrow_sb[0:1, k2 * 64:(k2 + 1) * 64], op=ALU.mult)
                    v.tensor_scalar(wrow[0:1, k2 * 64:(k2 + 1) * 64],
                                    wrow[0:1, k2 * 64:(k2 + 1) * 64], 1.0, None, op0=ALU.add)
                v.tensor_copy(wrow[0:1, 256:320], wrow[0:1, 192:256])
                v.tensor_tensor(wrow[0:1, 192:256], mselrow_sb[0:1, 192:256],
                                mselrow_sb[0:1, 192:256], op=ALU.mult)
                v.tensor_tensor(wrow[0:1, 128:192], wrow[0:1, 256:320],
                                mselrow_sb[0:1, 128:192], op=ALU.mult)
                v.tensor_tensor(wrow[0:1, 256:320], wrow[0:1, 64:128], wrow[0:1, 256:320],
                                op=ALU.mult)
                v.tensor_tensor(wrow[0:1, 64:128], wrow[0:1, 256:320],
                                mselrow_sb[0:1, 64:128], op=ALU.mult)
                v.tensor_tensor(wrow[0:1, 256:320], wrow[0:1, 0:64], wrow[0:1, 256:320],
                                op=ALU.mult)
                _li = v.tensor_tensor(wrow[0:1, 0:64], wrow[0:1, 256:320],
                                mselrow_sb[0:1, 0:64], op=ALU.mult)
                _li.then_inc(S_dv.h, 1)
                S_dv.n += 1
            dv_w = S_dv.n
            sinm = {}
            for j in range(4):
                @blk.sync
                def _(s, j=j):
                    if j >= 1:
                        s.wait_ge(S_dv.h, sinm[j - 1])
                    q = rsem("rsg", 0, n=1)
                    s.dma_start(out=sg[:], in_=coll_out.ap()[129 * j: 129 * j + 128, :]
                                ).then_inc(q.h, 16)
                    q.n += 16
                xl_j = (rsem("rsg", 0, n=1).h, rsem("rsg", 0, n=1).n)
                @blk.tensor
                def _(t, j=j, dv_w=dv_w):
                    t.wait_ge(S_dv.h, dv_w)
                    if j > 0:
                        t.wait_ge(S_dv.h, sinm[j - 1])
                    _li = t.matmul(ps_wb[:, 0:64], ones1f_sb[:], wrow[0:1, j * 64:(j + 1) * 64],
                             start=True, stop=True)
                    _li.then_inc(S_pe.h, 1)
                    S_pe.n += 1
                pe_j = S_pe.n
                @blk.scalar
                def _(sc, pe_j=pe_j):
                    sc.wait_ge(S_pe.h, pe_j)
                    sc.activation(wbc[:], ps_wb[:, 0:64], AF.Copy).then_inc(S_ac.h, 1)
                    S_ac.n += 1
                ac_j = S_ac.n
                @blk.vector
                def _(v, j=j, ac_j=ac_j, xl_j=xl_j):
                    v.wait_ge(S_ac.h, ac_j)
                    v.wait_ge(*xl_j)
                    for h in range(64):
                        src = sg[:, h * 64:(h + 1) * 64]
                        if j == 0:
                            _li = v.tensor_scalar(S_cur[:, h * 64:(h + 1) * 64], src,
                                                  wbc[:, h:h + 1], None, op0=ALU.mult)
                        else:
                            _li = v.scalar_tensor_tensor(S_cur[:, h * 64:(h + 1) * 64], src,
                                                   wbc[:, h:h + 1],
                                                   S_cur[:, h * 64:(h + 1) * 64],
                                                   op0=ALU.mult, op1=ALU.add)
                    _li.then_inc(S_dv.h, 1)
                    S_dv.n += 1
                    sinm[j] = S_dv.n
            barrier()
            pcmb.__exit__(None, None, None)
            if phases < 6:
                finish()
                return nc

            gm = p3.enter_context(nc.sbuf_tensor("gm", [128, 128], F32))
            dcl = p3.enter_context(nc.sbuf_tensor("dcl", [128, 2 * 1024], F32))
            E_sb = p3.enter_context(nc.sbuf_tensor("E_sb", [128, 2 * 1024], F32))
            mt4 = p3.enter_context(nc.sbuf_tensor("mt4", [128, 2 * 512], BF16))
            c4 = p3.enter_context(nc.sbuf_tensor("c4", [128, 2 * 512], BF16))
            crow = p3.enter_context(nc.sbuf_tensor("crow", [1, 2 * 512], F32))
            ps_G_full = p3.enter_context(nc.psum_tensor("ps_G", [128, 512], F32))
            ps_da = p3.enter_context(nc.psum_tensor("ps_da", [128, 1024], F32))
            ps_y = [p3.enter_context(nc.psum_tensor(f"ps_y{j}", [128, 512], F32)) for j in range(2)]
            m2 = {("supd_last", 0): 0}
            for ch in range(NCH):
                @blk.scalar
                def _(sc, ch=ch):
                    if ch == 0:
                        sc.wait_ge(S_dv.h, sinm[3])
                    else:
                        sc.wait_ge(S_dv.h, m2[("supd", ch - 1, 7)])
                        sc.wait_ge(S_pe.h, m2[("y", ch - 1, 31)])
                    sc.activation(S_bfb[:], S_cur[:], AF.Copy).then_inc(S_ac.h, 1)
                    S_ac.n += 1
                    m2[("sbf", ch)] = S_ac.n
                emit_slabs(ch, m2, want_bc=True)
                emit_dtx(ch, m2)
                for gq in range(16):
                    g, half = gq // 2, gq % 2
                    hb = g * 8 + half * 4
                    @blk.sync
                    def _(s, ch=ch, gq=gq, hb=hb):
                        prev = m2.get(("E", ch, gq - 2), m2.get(("E", ch - 1, 15), 0))
                        s.wait_ge(S_ac.h, prev)
                        q = rsem("rcr", gq)
                        s.dma_start(out=crow[0:1, (gq % 2) * 512:(gq % 2) * 512 + 512],
                                    in_=ca_scr[ch:ch + 1, hb * 128:(hb + 4) * 128]
                                    ).then_inc(q.h, 16)
                        q.n += 16
                        m2[("crow", ch, gq)] = (q.h, q.n)
                    if half == 0:
                        @blk.tensor
                        def _(t, ch=ch, g=g):
                            prev = m2.get(("gm", ch, g - 1), m2.get(("gm", ch - 1, 7), 0))
                            t.wait_ge(S_dv.h, prev)
                            t.wait_ge(*m2[("xld", ch)])
                            _li = t.matmul(ps_G_full[:, 0:128],
                                     bsl[:, (ch % 2) * 1024 + g * 128:(ch % 2) * 1024 + (g + 1) * 128],
                                     csl[:, (ch % 2) * 1024 + g * 128:(ch % 2) * 1024 + (g + 1) * 128],
                                     start=True, stop=True)
                            _li.then_inc(S_pe.h, 1)
                            S_pe.n += 1
                            m2[("G", ch, g)] = S_pe.n
                        @blk.vector
                        def _(v, ch=ch, g=g):
                            v.wait_ge(S_pe.h, m2[("G", ch, g)])
                            _li = v.tensor_tensor(gm[:], ps_G_full[:, 0:128], k01f_sb[:], op=ALU.mult)
                            _li.then_inc(S_dv.h, 1)
                            S_dv.n += 1
                            m2[("gm", ch, g)] = S_dv.n
                    q2 = gq % 2
                    @blk.tensor
                    def _(t, ch=ch, gq=gq, q2=q2):
                        t.wait_ge(*m2[("crow", ch, gq)])
                        prevV = m2.get(("dclv", ch, gq - 2), m2.get(("dclv", ch - 1, 15), 0))
                        t.wait_ge(S_dv.h, prevV)
                        _li = t.matmul(ps_da[:, q2 * 512:(q2 + 1) * 512], ones1f_sb[:],
                                 crow[0:1, q2 * 512: q2 * 512 + 512],
                                 start=True, stop=True)
                        _li.then_inc(S_pe.h, 1)
                        S_pe.n += 1
                        m2[("da", ch, gq)] = S_pe.n
                    @blk.vector
                    def _(v, ch=ch, gq=gq, hb=hb, q2=q2):
                        v.wait_ge(S_pe.h, m2[("da", ch, gq)])
                        prevE = m2.get(("E", ch, gq - 2), m2.get(("E", ch - 1, 15), 0))
                        v.wait_ge(S_ac.h, prevE)
                        d0 = q2 * 1024
                        for hq in range(4):
                            h = hb + hq
                            v.tensor_scalar(dcl[:, d0 + hq * 128: d0 + (hq + 1) * 128],
                                            ps_da[:, q2 * 512 + hq * 128: q2 * 512 + (hq + 1) * 128],
                                            ca_all[:, ch * 64 + h: ch * 64 + h + 1],
                                            0.0, op0=ALU.subtract, op1=ALU.min)
                        _li = v.tensor_copy(dcl[:, d0 + 512: d0 + 1024],
                                            ps_da[:, q2 * 512:(q2 + 1) * 512])
                        _li.then_inc(S_dv.h, 1)
                        S_dv.n += 1
                        m2[("dclv", ch, gq)] = S_dv.n
                    @blk.scalar
                    def _(sc, ch=ch, gq=gq, q2=q2):
                        sc.wait_ge(S_dv.h, m2[("dclv", ch, gq)])
                        prevmt = m2.get(("mt", ch, gq - 2), m2.get(("mt", ch - 1, 15), 0))
                        sc.wait_ge(S_dv.h, prevmt)
                        _li = sc.activation(E_sb[:, q2 * 1024:(q2 + 1) * 1024],
                                            dcl[:, q2 * 1024:(q2 + 1) * 1024], AF.Exp)
                        _li.then_inc(S_ac.h, 1)
                        S_ac.n += 1
                        m2[("E", ch, gq)] = S_ac.n
                    @blk.vector
                    def _(v, ch=ch, gq=gq, g=g, half=half, q2=q2):
                        v.wait_ge(S_ac.h, m2[("E", ch, gq)])
                        v.wait_ge(S_dv.h, m2[("gm", ch, g)])
                        prevy = m2.get(("y", ch, 2 * gq - 3), m2.get(("y", ch - 1, 31), 0))
                        v.wait_ge(S_pe.h, prevy)
                        d0 = q2 * 1024
                        gmr = gm[:].rearrange("p (o q) -> p o q", o=1).broadcast_to([128, 4, 128])
                        v.tensor_tensor(mt4[:, q2 * 512:(q2 + 1) * 512].rearrange("p (a q) -> p a q", a=4),
                                        E_sb[:, d0:d0 + 512].rearrange("p (a q) -> p a q", a=4), gmr,
                                        op=ALU.mult)
                        cslr = csl[:, (ch % 2) * 1024 + g * 128:(ch % 2) * 1024 + (g + 1) * 128]
                        cfr = cslr.rearrange("p (o q) -> p o q", o=1).broadcast_to([128, 4, 128])
                        _li = v.tensor_tensor(c4[:, q2 * 512:(q2 + 1) * 512].rearrange("p (a q) -> p a q", a=4),
                                        E_sb[:, d0 + 512: d0 + 1024].rearrange("p (a q) -> p a q", a=4), cfr,
                                        op=ALU.mult)
                        _li.then_inc(S_dv.h, 1)
                        S_dv.n += 1
                        m2[("mt", ch, gq)] = S_dv.n
                    for pr in range(2):
                        h0 = hb + pr * 2
                        xm = h0 // 2
                        pyb = ps_y[xm % 2]
                        @blk.tensor
                        def _(t, ch=ch, gq=gq, hb=hb, h0=h0, xm=xm, pyb=pyb):
                            t.wait_ge(S_dv.h, m2[("mt", ch, gq)])
                            t.wait_ge(S_ac.h, m2[("sbf", ch)])
                            t.wait_ge(S_dv.h, m2[("dtx", ch, 31)])
                            prev = m2.get(("yev", ch, xm - 2), m2.get(("yev", ch - 1, 31), 0))
                            t.wait_ge(S_dv.h, prev)
                            d0 = (ch % 2) * 4096
                            q2b = (gq % 2) * 512
                            for hh in range(2):
                                h = h0 + hh
                                lq = q2b + (h - hb) * 128
                                t.matmul(pyb[hh * 64:(hh + 1) * 64, 0:128],
                                         dtx[:, d0 + h * 64: d0 + (h + 1) * 64],
                                         mt4[:, lq:lq + 128],
                                         start=True, stop=False, tile_position=(0, hh * 64))
                                _li = t.matmul(pyb[hh * 64:(hh + 1) * 64, 0:128],
                                         S_bfb[:, h * 64:(h + 1) * 64],
                                         c4[:, lq:lq + 128],
                                         start=False, stop=True, tile_position=(0, hh * 64))
                            _li.then_inc(S_pe.h, 1)
                            S_pe.n += 1
                            m2[("y", ch, xm)] = S_pe.n
                        @blk.vector
                        def _(v, ch=ch, xm=xm, pyb=pyb):
                            v.wait_ge(S_pe.h, m2[("y", ch, xm)])
                            _li = v.scalar_tensor_tensor(
                                xn_a[:, xm * THp + ch * Q: xm * THp + (ch + 1) * Q],
                                xsl[:, (ch % 2) * 4096 + xm * 128:(ch % 2) * 4096 + (xm + 1) * 128],
                                dcol_sb[:, xm:xm + 1], pyb[:, 0:128],
                                op0=ALU.mult, op1=ALU.add)
                            _li.then_inc(S_dv.h, 1)
                            S_dv.n += 1
                            m2[("yev", ch, xm)] = S_dv.n
                m2[("xslfree", ch)] = m2[("yev", ch, 31)]
                m2[("dtxfree", ch)] = m2[("y", ch, 31)]
                if ch < NCH - 1:
                    emit_contrib(ch, m2)
                else:
                    m2[("bcfree", ch)] = m2[("y", ch, 31)]
        barrier()
        if phases < 7:
            finish()
            return nc

        pscope2 = ExitStack()
        with pscope2:
            xn_b2 = pscope2.enter_context(nc.sbuf_tensor("xn_b2", [128, DT_D * THp], BF16))
            p4 = ExitStack()
            with p4:
                ysq = p4.enter_context(nc.sbuf_tensor("ysq", [128, 2 * 512], BF16))
                wbuf = p4.enter_context(nc.sbuf_tensor("wbufp4", [128, 2 * DM], BF16))
                vrow = p4.enter_context(nc.sbuf_tensor("vrow", [1, 512], F32))
                rb_bc = p4.enter_context(nc.sbuf_tensor("rb_bc", [128, 2 * 512], F32))
                gtile = p4.enter_context(nc.sbuf_tensor("gtile", [128, 2 * 512], BF16))
                x1r = p4.enter_context(nc.sbuf_tensor("x1r", [128, 2 * 512], F32))
                ps_v = p4.enter_context(nc.psum_tensor("ps_v", [128, 512], F32))
                ps_rb = p4.enter_context(nc.psum_tensor("ps_rb", [128, 512], F32))
                ps_o = [p4.enter_context(nc.psum_tensor(f"ps_o{j}", [128, 512], F32)) for j in range(2)]
                for si, (off, w) in enumerate([(0, 512), (512, 512)]):
                    mk = {}
                    for g in range(8):
                        for qq in range(4):
                            it = g * 4 + qq
                            @blk.scalar
                            def _(sc, it=it, off=off, w=w):
                                if it >= 2:
                                    sc.wait_ge(S_pe.h, mk[("st", it - 2)])
                                sc.activation(ysq[:, (it % 2) * 512:(it % 2) * 512 + w],
                                              xn_a[:, it * THp + off: it * THp + off + w],
                                              AF.Square).then_inc(S_ac.h, 1)
                                S_ac.n += 1
                                mk[("sq", it)] = S_ac.n
                            @blk.tensor
                            def _(t, it=it, w=w, g=g, qq=qq):
                                t.wait_ge(S_ac.h, mk[("sq", it)])
                                if qq == 0 and g > 0:
                                    t.wait_ge(S_ac.h, mk[("vev", g - 1)])
                                _li = t.matmul(ps_v[0:1, 0:w], onescolb_sb[:],
                                         ysq[:, (it % 2) * 512:(it % 2) * 512 + w],
                                         start=(qq == 0), stop=(qq == 3))
                                _li.then_inc(S_pe.h, 1)
                                S_pe.n += 1
                                mk[("st", it)] = S_pe.n
                        @blk.scalar
                        def _(sc, g=g, w=w):
                            sc.wait_ge(S_pe.h, mk[("st", g * 4 + 3)])
                            if g > 0:
                                sc.wait_ge(S_pe.h, mk[("rbmm", g - 1)])
                            sc.activation(vrow[0:1, 0:w], ps_v[0:1, 0:w], AF.Copy)
                            sc.activation(vrow[0:1, 0:w], vrow[0:1, 0:w], AF.Ln,
                                          bias=eps_sb[0:1, 0:1], scale=1.0 / 512)
                            _li = sc.activation(vrow[0:1, 0:w], vrow[0:1, 0:w], AF.Exp, scale=-0.5)
                            _li.then_inc(S_ac.h, 1)
                            S_ac.n += 1
                            mk[("vev", g)] = S_ac.n
                        @blk.tensor
                        def _(t, g=g, w=w):
                            t.wait_ge(S_ac.h, mk[("vev", g)])
                            if g >= 2:
                                t.wait_ge(S_dv.h, mk[("yn", (g - 2) * 4 + 3)])
                            _li = t.matmul(ps_rb[:, 0:w], ones1f_sb[:], vrow[0:1, 0:w],
                                     start=True, stop=True)
                            _li.then_inc(S_pe.h, 1)
                            S_pe.n += 1
                            mk[("rbmm", g)] = S_pe.n
                        @blk.scalar
                        def _(sc, g=g, w=w):
                            sc.wait_ge(S_pe.h, mk[("rbmm", g)])
                            sc.activation(rb_bc[:, (g % 2) * 512:(g % 2) * 512 + w],
                                          ps_rb[:, 0:w], AF.Copy).then_inc(S_ac.h, 1)
                            S_ac.n += 1
                            mk[("rbev", g)] = S_ac.n
                        for qq in range(4):
                            it = g * 4 + qq
                            @blk.sync
                            def _(s, it=it, off=off):
                                if it >= 2:
                                    s.wait_ge(S_dv.h, mk[("yn", it - 2)])
                                q = rsem("rgt", it)
                                s.dma_start(out=gtile[:, (it % 2) * 512:(it % 2) * 512 + 512],
                                            in_=gate_scr[it * 128:(it + 1) * 128, 3 + off: 3 + off + 512]
                                            ).then_inc(q.h, 16)
                                q.n += 16
                                mk[("gl", it)] = (q.h, q.n)
                            @blk.scalar
                            def _(sc, it=it):
                                sc.wait_ge(*mk[("gl", it)])
                                sc.activation(gtile[:, (it % 2) * 512:(it % 2) * 512 + 512],
                                              gtile[:, (it % 2) * 512:(it % 2) * 512 + 512],
                                              AF.Silu).then_inc(S_ac.h, 1)
                                S_ac.n += 1
                                mk[("gs", it)] = S_ac.n
                            @blk.vector
                            def _(v, it=it, g=g, off=off):
                                v.wait_ge(S_ac.h, mk[("gs", it)])
                                v.wait_ge(S_ac.h, mk[("rbev", g)])
                                v.tensor_tensor(t_scr[:, 0:512],
                                                xn_a[:, it * THp + off: it * THp + off + 512],
                                                rb_bc[:, (g % 2) * 512:(g % 2) * 512 + 512],
                                                op=ALU.mult)
                                v.tensor_scalar(t_scr[:, 0:512], t_scr[:, 0:512],
                                                nwcol_sb[:, it:it + 1], None, op0=ALU.mult)
                                _li = v.tensor_tensor(xn_a[:, it * THp + off: it * THp + off + 512],
                                                t_scr[:, 0:512],
                                                gtile[:, (it % 2) * 512:(it % 2) * 512 + 512],
                                                op=ALU.mult)
                                _li.then_inc(S_dv.h, 1)
                                S_dv.n += 1
                                mk[("yn", it)] = S_dv.n
                    for ct in range(32):
                        @blk.gpsimd
                        def _(g2, ct=ct):
                            if ct >= 2:
                                g2.wait_ge(S_pe.h, mk[("op", ct - 2)])
                            q = rsem("rw", ct)
                            g2.dma_start(out=wbuf[:, (ct % 2) * DM:(ct % 2 + 1) * DM],
                                         in_=wot[ct]).then_inc(q.h, 16)
                            q.n += 16
                            mk[("wo", ct)] = (q.h, q.n)
                        @blk.sync
                        def _(s, ct=ct, off=off):
                            if ct >= 2:
                                s.wait_ge(S_dv.h, mk[("x2", ct - 2)])
                                s.wait_ge(S_ac.h, mk[("x2b", ct - 2)])
                            q = rsem("rx1", ct)
                            s.dma_start(out=x1r[:, (ct % 2) * 512:(ct % 2) * 512 + 512],
                                        in_=x1_scr[ct * 128:(ct + 1) * 128, 3 + off: 3 + off + 512]
                                        ).then_inc(q.h, 16)
                            q.n += 16
                            mk[("x1l", ct)] = (q.h, q.n)
                        @blk.tensor
                        def _(t, ct=ct):
                            t.wait_ge(*mk[("wo", ct)])
                            t.wait_ge(S_dv.h, mk[("yn", 31)])
                            if ct >= 2:
                                t.wait_ge(S_dv.h, mk[("x2", ct - 2)])
                            bb = ps_o[ct % 2]
                            for it in range(32):
                                _li = t.matmul(bb[:],
                                         wbuf[:, (ct % 2) * DM + it * 128:(ct % 2) * DM + (it + 1) * 128],
                                         xn_a[:, it * THp + off: it * THp + off + 512],
                                         start=(it == 0), stop=(it == 31))
                            _li.then_inc(S_pe.h, 1)
                            S_pe.n += 1
                            mk[("op", ct)] = S_pe.n
                        @blk.vector
                        def _(v, ct=ct):
                            v.wait_ge(S_pe.h, mk[("op", ct)])
                            v.wait_ge(*mk[("x1l", ct)])
                            if ct >= 2:
                                v.wait_ge(*mk[("x2s", ct - 2)])
                                v.wait_ge(S_ac.h, mk[("x2b", ct - 2)])
                            _li = v.tensor_tensor(x1r[:, (ct % 2) * 512:(ct % 2) * 512 + 512],
                                            ps_o[ct % 2][:],
                                            x1r[:, (ct % 2) * 512:(ct % 2) * 512 + 512],
                                            op=ALU.add)
                            _li.then_inc(S_dv.h, 1)
                            S_dv.n += 1
                            mk[("x2", ct)] = S_dv.n
                        @blk.sync
                        def _(s, ct=ct, off=off):
                            s.wait_ge(S_dv.h, mk[("x2", ct)])
                            q = rsem("rx2", ct)
                            s.dma_start(out=x2_scr[ct * 128:(ct + 1) * 128, off:off + 512],
                                        in_=x1r[:, (ct % 2) * 512:(ct % 2) * 512 + 512]
                                        ).then_inc(q.h, 16)
                            q.n += 16
                            mk[("x2s", ct)] = (q.h, q.n)
                        @blk.scalar
                        def _(sc, ct=ct, off=off):
                            sc.wait_ge(S_dv.h, mk[("x2", ct)])
                            sc.activation(xn_b2[:, ct * THp + off: ct * THp + off + 512],
                                          x1r[:, (ct % 2) * 512:(ct % 2) * 512 + 512],
                                          AF.Copy).then_inc(S_ac.h, 1)
                            S_ac.n += 1
                            mk[("x2b", ct)] = S_ac.n
                    barrier()
            subw5 = [(0, 512), (512, 512)]
            if phases < 8:
                finish()
                return nc
            ln_stats(xn_b2, subw5)
            ln_fin_bcast_apply(xn_b2, T, subw5)
            barrier()
            if phases < 9:
                finish()
                return nc
            emit_ffn(xn_b2, x2_scr, outT, T, None, "f2")

        finish()

    return nc


_CACHE = {}


def _mask_idx():
    import jax
    with jax.default_device(jax.devices("cpu")[0]):
        u = np.asarray(jax.random.uniform(
            jax.random.key(42, impl="threefry2x32"), (2, 4096, 4096)))
    return ((u >= 0.25).astype(np.uint8) + (u >= 0.5) + (u >= 0.75)).astype(np.uint8)


def _tile_lhsT(Wt):
    Ktot, Mtot = Wt.shape
    Kt, Mt = Ktot // 128, Mtot // 128
    return np.ascontiguousarray(
        Wt.reshape(Kt, 128, Mt, 128).transpose(2, 1, 0, 3).reshape(Mt, 128, Kt * 128))


def prepare_inputs(L0_attn_hidden_states, L0_attn_in_proj_w, L0_attn_conv_weight,
                   L0_attn_conv_bias, L0_attn_dt_bias, L0_attn_A_log, L0_attn_D,
                   L0_attn_norm_weight, L0_attn_out_proj_w, L0_ffn_W0, L0_ffn_b0,
                   L0_ffn_A, L0_ffn_B):
    f32 = np.float32
    x0 = np.asarray(L0_attn_hidden_states, f32)
    Bsz, L, _ = x0.shape
    x_flat = np.ascontiguousarray(x0.reshape(Bsz * L, DM))
    idx = _mask_idx().reshape(Bsz * L, I)

    Bm = np.asarray(L0_ffn_B, f32)
    wbts = np.ascontiguousarray(
        Bm.transpose(2, 0, 1).reshape(16, 4, 32, 128).transpose(2, 0, 1, 3).reshape(32, 16, 512))

    shared = {
        "w0t": _tile_lhsT(np.ascontiguousarray(np.asarray(L0_ffn_W0, f32).T)),
        "wpt": _tile_lhsT(np.ascontiguousarray(
            np.pad(np.asarray(L0_attn_in_proj_w, f32), ((0, 81 * 128 - PROJ), (0, 0))).T)),
        "wot": _tile_lhsT(np.ascontiguousarray(np.asarray(L0_attn_out_proj_w, f32).T)),
        "wat": np.ascontiguousarray(
            np.asarray(L0_ffn_A, f32).transpose(2, 0, 1).reshape(32, 128, 4, 16)
            .transpose(1, 2, 0, 3).reshape(128, 2048)),
        "wbts": wbts,
        "b0c": np.ascontiguousarray(np.asarray(L0_ffn_b0, f32).reshape(32, 128).T),
        "convw": np.ascontiguousarray(
            np.asarray(L0_attn_conv_weight, f32).reshape(48, 128, 4).transpose(1, 0, 2).reshape(128, 192)),
        "convb": np.ascontiguousarray(np.asarray(L0_attn_conv_bias, f32).reshape(48, 128).T),
        "dtbias": np.ascontiguousarray(np.broadcast_to(np.asarray(L0_attn_dt_bias, f32), (128, 64))),
        "alogb": np.ascontiguousarray(np.broadcast_to(np.asarray(L0_attn_A_log, f32), (128, 64))),
        "dcolh": np.ascontiguousarray(np.repeat(np.asarray(L0_attn_D, f32), 64).reshape(32, 128).T),
        "nwcolh": np.ascontiguousarray(np.asarray(L0_attn_norm_weight, f32).reshape(32, 128).T),
        "identb": np.eye(128, dtype=ml_dtypes.bfloat16),
        "identf": np.eye(128, dtype=f32),
        "ones1f": np.ones((1, 128), f32),
        "onescolb": np.ones((128, 1), ml_dtypes.bfloat16),
        "onesrowb": np.ones((1, 512), ml_dtypes.bfloat16),
        "ctrif": np.ascontiguousarray(-np.triu(np.ones((128, 128), f32))),
        "k01f": np.ascontiguousarray(np.triu(np.ones((128, 128), f32))),
    }

    in_maps = []
    for core in range(8):
        b, r = core // 4, core % 4
        r0 = b * L + r * T
        rows = np.zeros((TH, DM), f32)
        mrows = np.zeros((TH, I), np.uint8)
        if r == 0:
            rows[3:] = x_flat[r0:r0 + T]
            mrows[3:] = idx[r0:r0 + T]
        else:
            rows[:] = x_flat[r0 - 3:r0 + T]
            mrows[:] = idx[r0 - 3:r0 + T]
        d = dict(shared)
        d["xT"] = np.ascontiguousarray(rows.T)
        d["m1T"] = np.ascontiguousarray((mrows == 1).astype(np.uint8).T)
        d["m2T"] = np.ascontiguousarray((mrows == 2).astype(np.uint8).T)
        d["m3T"] = np.ascontiguousarray((mrows == 3).astype(np.uint8).T)
        d["halom"] = np.full((128, 1), 0.0 if r == 0 else 1.0, f32)
        act = np.zeros((1, 256), f32)
        for k2 in range(4):
            act[0, k2 * 64:(k2 + 1) * 64] = 1.0 if k2 < r else 0.0
        d["actrow"] = act
        d["epsr"] = np.full((1, 1), EPS, f32)
        d["mselrow"] = act.copy()
        in_maps.append(d)
    return in_maps


def kernel(**inputs):
    in_maps = prepare_inputs(**inputs)
    Bsz = 2
    L = 4096
    f32 = np.float32
    if "nc" not in _CACHE:
        _CACHE["nc"] = build_program()
    res = run_bass_kernel_spmd(_CACHE["nc"], in_maps, core_ids=list(range(8)))

    out = np.empty((Bsz * L, DM), f32)
    for core in range(8):
        b, r = core // 4, core % 4
        r0 = b * L + r * T
        out[r0:r0 + T] = res.results[core]["outT"].T
    return out.reshape(Bsz, L, DM)


# version 06
# speedup vs baseline: 1.1483x; 1.1442x over previous
import numpy as np
import ml_dtypes
from contextlib import ExitStack

import concourse.bass as bass
import concourse.mybir as mybir
from concourse.bass_utils import run_bass_kernel_spmd

F32 = mybir.dt.float32
BF16 = mybir.dt.bfloat16
U8 = mybir.dt.uint8
AF = mybir.ActivationFunctionType
ALU = mybir.AluOpType

H, P, N, G = 64, 64, 128, 8
I = H * P
GN = G * N
CONV_DIM = I + 2 * GN
PROJ = I + CONV_DIM + H
DM = 4096
EPS = 1e-5
T = 1024
TH = T + 3
THp = 1028
Q = 128
NCH = T // Q
DT_D = DM // 128


def build_program(phases=99, debug=False):
    nc = bass.Bass("TRN2", target_bir_lowering=False)
    ap = lambda t: t.ap()
    xT = ap(nc.dram_tensor("xT", [DM, TH], F32, kind="ExternalInput"))
    m1T = ap(nc.dram_tensor("m1T", [DM, TH], U8, kind="ExternalInput"))
    m2T = ap(nc.dram_tensor("m2T", [DM, TH], U8, kind="ExternalInput"))
    m3T = ap(nc.dram_tensor("m3T", [DM, TH], U8, kind="ExternalInput"))
    w0t = ap(nc.dram_tensor("w0t", [32, 128, DM], F32, kind="ExternalInput"))
    wpt = ap(nc.dram_tensor("wpt", [81, 128, DM], F32, kind="ExternalInput"))
    wot = ap(nc.dram_tensor("wot", [32, 128, DM], F32, kind="ExternalInput"))
    wat = ap(nc.dram_tensor("wat", [128, 2048], F32, kind="ExternalInput"))
    wbts = ap(nc.dram_tensor("wbts", [32, 16, 512], F32, kind="ExternalInput"))
    b0c = ap(nc.dram_tensor("b0c", [128, 32], F32, kind="ExternalInput"))
    convw = ap(nc.dram_tensor("convw", [128, 192], F32, kind="ExternalInput"))
    convb = ap(nc.dram_tensor("convb", [128, 48], F32, kind="ExternalInput"))
    dtbias = ap(nc.dram_tensor("dtbias", [128, 64], F32, kind="ExternalInput"))
    alogb = ap(nc.dram_tensor("alogb", [128, 64], F32, kind="ExternalInput"))
    dcolh = ap(nc.dram_tensor("dcolh", [128, 32], F32, kind="ExternalInput"))
    nwcolh = ap(nc.dram_tensor("nwcolh", [128, 32], F32, kind="ExternalInput"))
    identb = ap(nc.dram_tensor("identb", [128, 128], BF16, kind="ExternalInput"))
    identf = ap(nc.dram_tensor("identf", [128, 128], F32, kind="ExternalInput"))
    ones1f = ap(nc.dram_tensor("ones1f", [1, 128], F32, kind="ExternalInput"))
    onescolb = ap(nc.dram_tensor("onescolb", [128, 1], BF16, kind="ExternalInput"))
    onesrowb = ap(nc.dram_tensor("onesrowb", [1, 512], BF16, kind="ExternalInput"))
    ctrif = ap(nc.dram_tensor("ctrif", [128, 128], F32, kind="ExternalInput"))
    k01f = ap(nc.dram_tensor("k01f", [128, 128], F32, kind="ExternalInput"))
    halom = ap(nc.dram_tensor("halom", [128, 1], F32, kind="ExternalInput"))
    actrow = ap(nc.dram_tensor("actrow", [1, 256], F32, kind="ExternalInput"))
    epsr = ap(nc.dram_tensor("epsr", [1, 1], F32, kind="ExternalInput"))
    mselrow = ap(nc.dram_tensor("mselrow", [1, 256], F32, kind="ExternalInput"))

    outT = ap(nc.dram_tensor("outT", [DM, T], F32, kind="ExternalOutput"))

    KD = "ExternalOutput" if debug else "Internal"
    x1_scr = ap(nc.dram_tensor("x1_scr", [DM, THp], F32, kind=KD))
    x2_scr = ap(nc.dram_tensor("x2_scr", [DM, T], F32, kind=KD))
    gate_scr = ap(nc.dram_tensor("gate_scr", [DM, THp], BF16, kind=KD))
    xssm_scr = ap(nc.dram_tensor("xssm_scr", [DM, T], BF16, kind=KD))
    B_scr = ap(nc.dram_tensor("B_scr", [GN, T], BF16, kind=KD))
    C_scr = ap(nc.dram_tensor("C_scr", [GN, T], BF16, kind=KD))
    ca_scr = ap(nc.dram_tensor("ca_scr", [NCH, 8192], F32, kind=KD))
    cae_scr = ap(nc.dram_tensor("cae_scr", [NCH, 64], F32, kind="Internal"))
    caer_scr = ap(nc.dram_tensor("caer_scr", [NCH, 8192], BF16, kind="Internal"))
    coll_in = nc.dram_tensor("coll_in", [129, 4096], F32, kind="Internal")
    coll_out = nc.dram_tensor("coll_out", [516, 4096], F32, kind="Internal")

    es = ExitStack()
    with es:
        sbt = lambda name, shape, dt: es.enter_context(nc.sbuf_tensor(name, shape, dt))
        xn_a = sbt("xn_a", [128, DT_D * THp], BF16)
        wat_sb = sbt("wat_sb", [128, 2048], BF16)
        wbt_r = sbt("wbt_r", [16, 2 * 512], BF16)
        b0h_sb = sbt("b0h_sb", [128, 32], F32)
        identb_sb = sbt("identb_sb", [128, 128], BF16)
        identf_sb = sbt("identf_sb", [128, 128], F32)
        ones1f_sb = sbt("ones1f_sb", [1, 128], F32)
        onescolb_sb = sbt("onescolb_sb", [128, 1], BF16)
        onesrowb_sb = sbt("onesrowb_sb", [1, 512], BF16)
        ctrif_sb = sbt("ctrif_sb", [128, 128], F32)
        k01f_sb = sbt("k01f_sb", [128, 128], F32)
        convw_sb = sbt("convw_sb", [128, 192], F32)
        convb_sb = sbt("convb_sb", [128, 48], F32)
        dtbias_sb = sbt("dtbias_sb", [128, 64], F32)
        expA_sb = sbt("expA_sb", [128, 64], F32)
        dcol_sb = sbt("dcol_sb", [128, 32], F32)
        nwcol_sb = sbt("nwcol_sb", [128, 32], F32)
        halo_sb = sbt("halo_sb", [128, 1], F32)
        actrow_sb = sbt("actrow_sb", [1, 256], F32)
        mselrow_sb = sbt("mselrow_sb", [1, 256], F32)
        eps_sb = sbt("eps_sb", [1, 1], F32)
        dtp_all = sbt("dtp_all", [128, NCH * 64], F32)
        ca_all = sbt("ca_all", [128, NCH * 64], F32)
        caendb_all = sbt("caendb_all", [128, NCH * 64], F32)
        dAtot_all = sbt("dAtot_all", [128, NCH * 64], F32)
        aprod_sb = sbt("aprod_sb", [128, 64], F32)
        rows_t = sbt("rows_t", [128, THp], F32)
        a_bc = sbt("a_bc", [128, THp], F32)
        b_bc = sbt("b_bc", [128, THp], F32)
        t_scr = sbt("t_scr", [128, THp], F32)
        xsqtl = sbt("xsqtl", [128, 2 * THp], BF16)

        r_mu = rows_t[0:1, 0:THp]
        r_v = t_scr[0:1, 0:THp]

        class Sem:
            def __init__(self, name):
                self.h = es.enter_context(nc.semaphore(name))
                self.n = 0
        S_bar = Sem("bar"); S_wld = Sem("wld"); S_xld = Sem("xld")
        S_pe = Sem("pe"); S_dv = Sem("dv"); S_ac = Sem("ac")
        S_sto = Sem("sto"); S_col = Sem("col"); S_gp = Sem("gp")
        rings = {}

        def ring(name, n=2):
            if name not in rings:
                rings[name] = [Sem(f"{name}{i}") for i in range(n)]
            return rings[name]

        def rsem(name, u, n=2):
            return ring(name, n)[u % n]

        cur_set = [None]

        def need_set(sc, sid):
            if cur_set[0] != sid:
                sc.add_instruction(mybir.InstLoadActFuncSet(
                    name=nc.get_next_instruction_name(),
                    act_func_set_id=sid, ins=[], outs=[]))
                cur_set[0] = sid


        blk = es.enter_context(nc.Block())
        nbar = [0]

        def barrier():
            nbar[0] += 1
            sems = [S_pe, S_dv, S_ac, S_col, S_xld, S_wld, S_sto, S_gp]
            for dec in (blk.sync, blk.gpsimd, blk.tensor, blk.vector, blk.scalar):
                @dec
                def _(eng):
                    for q in sems:
                        if q.n:
                            eng.wait_ge(q.h, q.n)
                    for rl in rings.values():
                        for q in rl:
                            if q.n:
                                eng.wait_ge(q.h, q.n)


        def finish():
            @blk.sync
            def _(s):
                s.wait_ge(S_sto.h, S_sto.n)
                s.wait_ge(S_xld.h, S_xld.n)
                for sems in rings.values():
                    for q in sems:
                        if q.n:
                            s.wait_ge(q.h, q.n)

        @blk.sync
        def _(s):
            for dst, src in [
                (identb_sb, identb), (identf_sb, identf), (ones1f_sb, ones1f),
                (onescolb_sb, onescolb), (onesrowb_sb, onesrowb),
                (ctrif_sb, ctrif), (k01f_sb, k01f), (convw_sb, convw),
                (convb_sb, convb), (dtbias_sb, dtbias), (dcol_sb, dcolh),
                (nwcol_sb, nwcolh), (halo_sb, halom),
                (actrow_sb, actrow), (mselrow_sb, mselrow), (eps_sb, epsr),
            ]:
                s.dma_start(out=dst[:], in_=src[:]).then_inc(S_xld.h, 16)
                S_xld.n += 16
            s.dma_start(out=expA_sb[:], in_=alogb[:]).then_inc(S_xld.h, 16)
            s.dma_start(out=b0h_sb[:], in_=b0c[:]).then_inc(S_xld.h, 16)
            S_xld.n += 32

        @blk.gpsimd
        def _(g):
            g.dma_start(out=wat_sb[:], in_=wat[:]).then_inc(S_wld.h, 16)
            S_wld.n += 16

        @blk.scalar
        def _(sc):
            need_set(sc, 6)
            sc.wait_ge(S_xld.h, S_xld.n)
            sc.wait_ge(S_wld.h, S_wld.n)
            sc.activation(expA_sb[:], expA_sb[:], AF.Exp).then_inc(S_ac.h, 1)
            S_ac.n += 1
        @blk.vector
        def _(v):
            v.wait_ge(S_ac.h, S_ac.n)
            _li = v.tensor_scalar(b0h_sb[:], b0h_sb[:], 0.5, None, op0=ALU.mult)
            _li.then_inc(S_dv.h, 1)
            S_dv.n += 1
        barrier()

        _uid = [0]

        def ln_stats(buf_bf, subw):
            _uid[0] += 1
            uid = _uid[0]
            ps = ExitStack()
            with ps:
                psl = ps.enter_context(nc.psum_tensor(f"psl_ln{uid}", [128, 512], F32))
                psl2 = ps.enter_context(nc.psum_tensor(f"psl2_ln{uid}", [128, 512], F32))
                for (off, w) in subw:
                    sq_marks = []
                    pe_base = S_pe.n
                    @blk.scalar
                    def _(sc, off=off, w=w, sq_marks=sq_marks, pe_base=pe_base):
                        for dt in range(DT_D):
                            sc.wait_ge(S_pe.h, pe_base + max(dt - 1, 0))
                            sc.activation(xsqtl[:, (dt % 2) * THp:(dt % 2) * THp + w],
                                          buf_bf[:, dt * THp + off: dt * THp + off + w],
                                          AF.Square).then_inc(S_ac.h, 1)
                            S_ac.n += 1
                            sq_marks.append(S_ac.n)
                    @blk.tensor
                    def _(t, off=off, w=w, sq_marks=sq_marks):
                        for dt in range(DT_D):
                            t.wait_ge(S_ac.h, sq_marks[dt])
                            t.matmul(psl[0:1, 0:w], onescolb_sb[:],
                                     buf_bf[:, dt * THp + off: dt * THp + off + w],
                                     start=(dt == 0), stop=(dt == DT_D - 1))
                            _li = t.matmul(psl2[0:1, 0:w], onescolb_sb[:],
                                     xsqtl[:, (dt % 2) * THp:(dt % 2) * THp + w],
                                     start=(dt == 0), stop=(dt == DT_D - 1))
                            _li.then_inc(S_pe.h, 1)
                            S_pe.n += 1
                    @blk.scalar
                    def _(sc, off=off, w=w):
                        sc.wait_ge(S_pe.h, S_pe.n)
                        sc.activation(r_mu[0:1, off:off + w], psl[0:1, 0:w], AF.Copy)
                        _li = sc.activation(r_v[0:1, off:off + w], psl2[0:1, 0:w], AF.Copy)
                        _li.then_inc(S_ac.h, 1)
                        S_ac.n += 1
                    @blk.tensor
                    def _(t):
                        t.wait_ge(S_ac.h, S_ac.n)

        def ln_fin_bcast_apply(buf_bf, W, subw):
            @blk.vector
            def _(v, W=W):
                v.wait_ge(S_ac.h, S_ac.n)
                v.tensor_scalar(r_mu[0:1, 0:W], r_mu[0:1, 0:W], -1.0 / DM, None, op0=ALU.mult)
                v.tensor_scalar(r_v[0:1, 0:W], r_v[0:1, 0:W], 1.0 / DM, None, op0=ALU.mult)
                v.tensor_tensor(a_bc[0:1, 0:W], r_mu[0:1, 0:W], r_mu[0:1, 0:W], op=ALU.mult)
                _li = v.tensor_tensor(r_v[0:1, 0:W], r_v[0:1, 0:W], a_bc[0:1, 0:W], op=ALU.subtract)
                _li.then_inc(S_dv.h, 1)
                S_dv.n += 1
            @blk.scalar
            def _(sc, W=W):
                sc.wait_ge(S_dv.h, S_dv.n)
                need_set(sc, 6)
                sc.activation(r_v[0:1, 0:W], r_v[0:1, 0:W], AF.Ln, bias=eps_sb[0:1, 0:1])
                _li = sc.activation(r_v[0:1, 0:W], r_v[0:1, 0:W], AF.Exp, scale=-0.5)
                _li.then_inc(S_ac.h, 1)
                S_ac.n += 1
            @blk.vector
            def _(v, W=W):
                v.wait_ge(S_ac.h, S_ac.n)
                _li = v.tensor_tensor(r_mu[0:1, 0:W], r_mu[0:1, 0:W], r_v[0:1, 0:W], op=ALU.mult)
                _li.then_inc(S_dv.h, 1)
                S_dv.n += 1
            ps = ExitStack()
            with ps:
                _uid[0] += 1
                uid = _uid[0]
                pba = ps.enter_context(nc.psum_tensor(f"pbca{uid}", [128, 512], F32))
                pbb = ps.enter_context(nc.psum_tensor(f"pbcb{uid}", [128, 512], F32))
                for (off, w) in subw:
                    @blk.tensor
                    def _(t, off=off, w=w):
                        t.wait_ge(S_dv.h, S_dv.n)
                        t.wait_ge(S_ac.h, S_ac.n)
                        t.matmul(pba[:, 0:w], ones1f_sb[:], r_v[0:1, off:off + w],
                                 start=True, stop=True)
                        _li = t.matmul(pbb[:, 0:w], ones1f_sb[:], r_mu[0:1, off:off + w],
                                 start=True, stop=True)
                        _li.then_inc(S_pe.h, 1)
                        S_pe.n += 1
                    @blk.scalar
                    def _(sc, off=off, w=w):
                        sc.wait_ge(S_pe.h, S_pe.n)
                        sc.activation(a_bc[:, off:off + w], pba[:, 0:w], AF.Copy)
                        _li = sc.activation(b_bc[:, off:off + w], pbb[:, 0:w], AF.Copy)
                        _li.then_inc(S_ac.h, 1)
                        S_ac.n += 1
            @blk.vector
            def _(v, W=W):
                v.wait_ge(S_ac.h, S_ac.n)
                for dt in range(16):
                    c0 = dt * THp
                    v.tensor_tensor(t_scr[:, 0:W], buf_bf[:, c0:c0 + W], a_bc[:, 0:W], op=ALU.mult)
                    _li = v.tensor_tensor(buf_bf[:, c0:c0 + W], t_scr[:, 0:W], b_bc[:, 0:W], op=ALU.add)
                _li.then_inc(S_dv.h, 1)
                S_dv.n += 1
            @blk.gpsimd
            def _(g, W=W):
                g.wait_ge(S_ac.h, S_ac.n)
                for dt in range(16, DT_D):
                    c0 = dt * THp
                    g.tensor_tensor(rows_t[:, 0:W], buf_bf[:, c0:c0 + W], a_bc[:, 0:W], op=ALU.mult)
                    _li = g.tensor_tensor(buf_bf[:, c0:c0 + W], rows_t[:, 0:W], b_bc[:, 0:W], op=ALU.add)
                _li.then_inc(S_gp.h, 1)
                S_gp.n += 1


        subw1 = [(0, 512), (512, 512), (1024, TH - 1024)]
        ld_marks = []
        ac_base1 = S_ac.n
        xring = [a_bc, b_bc]
        @blk.sync
        def _(s):
            for dt in range(DT_D):
                if dt >= 2:
                    s.wait_ge(S_ac.h, ac_base1 + dt - 1)
                q = rsem("p1x", dt)
                s.dma_start(out=xring[dt % 2][:, 0:TH],
                            in_=xT[dt * 128:(dt + 1) * 128, :]).then_inc(q.h, 16)
                q.n += 16
                ld_marks.append((q.h, q.n))
        @blk.scalar
        def _(sc):
            for dt in range(DT_D):
                sc.wait_ge(*ld_marks[dt])
                sc.activation(xn_a[:, dt * THp: dt * THp + TH], xring[dt % 2][:, 0:TH],
                              AF.Copy).then_inc(S_ac.h, 1)
                S_ac.n += 1
        barrier()
        ln_stats(xn_a, subw1)
        ln_fin_bcast_apply(xn_a, TH, subw1)
        barrier()

        def emit_ffn(inbuf, res_src, out_dst, W, nxt_buf, tag):
            subw = [(0, 512), (512, 512)] + ([(1024, W - 1024)] if W > 1024 else [])
            nsub = len(subw)
            moff0 = 3 if W == T else 0
            ps = ExitStack()
            with ps:
                psq = [ps.enter_context(nc.psum_tensor(f"psq{tag}{l}", [128, 512], F32))
                       for l in range(4)]
                psb = [ps.enter_context(nc.psum_tensor(f"psb{tag}{j}", [128, 512], F32))
                       for j in range(2)]
                psacc = ps.enter_context(nc.psum_tensor(f"psacc{tag}", [128, 512], F32))
                wbuf = ps.enter_context(nc.sbuf_tensor(f"wbuf{tag}", [128, 2 * DM], BF16))
                hbf = ps.enter_context(nc.sbuf_tensor(f"hbf{tag}", [16, 4 * THp], BF16))
                xtl = ps.enter_context(nc.sbuf_tensor(f"xtl{tag}", [128, 2 * 512], F32))
                mtl = ps.enter_context(nc.sbuf_tensor(f"mtl{tag}", [128, 6 * 512], U8))
                x1tl = ps.enter_context(nc.sbuf_tensor(f"x1tl{tag}", [128, 2 * 512], F32))
                for si, (off, w) in enumerate(subw):
                    @blk.tensor
                    def _(t, si=si, off=off, w=w):
                        if si > 0:
                            t.wait_ge(S_ac.h, S_ac.n)
                        for l in range(4):
                            for dt in range(DT_D):
                                _li = t.matmul(psq[l][0:16, 0:w],
                                         wat_sb[:, l * 512 + dt * 16: l * 512 + (dt + 1) * 16],
                                         inbuf[:, dt * THp + off: dt * THp + off + w],
                                         start=(dt == 0), stop=(dt == DT_D - 1))
                        _li.then_inc(S_pe.h, 1)
                        S_pe.n += 1
                    @blk.scalar
                    def _(sc, si=si, off=off, w=w):
                        sc.wait_ge(S_pe.h, S_pe.n)
                        for l in range(4):
                            _li = sc.activation(hbf[:, l * THp + off: l * THp + off + w],
                                          psq[l][0:16, 0:w], AF.Copy)
                        _li.then_inc(S_ac.h, 1)
                        S_ac.n += 1
                barrier()
                wldm = {}; pem = {}; dvm = {}; accm = {}; xldm = {}
                stom = {}; x1m = {}; wbm = {}
                for ct in range(32):
                    @blk.gpsimd
                    def _(g, ct=ct):
                        if ct >= 2:
                            g.wait_ge(S_pe.h, pem[(ct - 2, nsub - 1)])
                        q = rsem("rw", ct)
                        g.dma_start(out=wbuf[:, (ct % 2) * DM:(ct % 2 + 1) * DM],
                                    in_=w0t[ct]).then_inc(q.h, 16)
                        g.dma_start(out=wbt_r[:, (ct % 2) * 512:(ct % 2 + 1) * 512],
                                    in_=wbts[ct]).then_inc(q.h, 16)
                        q.n += 32
                        wldm[ct] = (q.h, q.n)
                    for si, (off, w) in enumerate(subw):
                        u = ct * nsub + si
                        slot = u % 2
                        @blk.sync
                        def _(s, ct=ct, off=off, w=w, u=u, slot=slot):
                            if u >= 2:
                                s.wait_ge(S_dv.h, dvm[u - 2])
                            q = rsem("rm", u)
                            s.dma_start(out=xtl[:, slot * 512: slot * 512 + w],
                                        in_=res_src[ct * 128:(ct + 1) * 128, off:off + w]
                                        ).then_inc(q.h, 16)
                            mo = off + moff0
                            for mi, msrc in enumerate((m1T, m2T, m3T)):
                                s.dma_start(out=mtl[:, (slot * 3 + mi) * 512:(slot * 3 + mi) * 512 + w],
                                            in_=msrc[ct * 128:(ct + 1) * 128, mo:mo + w]
                                            ).then_inc(q.h, 16)
                            q.n += 64
                            xldm[u] = (q.h, q.n)
                        @blk.tensor
                        def _(t, ct=ct, si=si, off=off, w=w, u=u):
                            t.wait_ge(*wldm[ct])
                            if u >= 2:
                                t.wait_ge(S_dv.h, dvm[u - 2])
                            bb = psb[u % 2]
                            for dt in range(DT_D):
                                t.matmul(bb[:, 0:w],
                                         wbuf[:, (ct % 2) * DM + dt * 128:(ct % 2) * DM + (dt + 1) * 128],
                                         inbuf[:, dt * THp + off: dt * THp + off + w],
                                         start=(dt == 0), stop=(dt == DT_D - 1))
                            if u >= 1:
                                t.wait_ge(S_dv.h, dvm[u - 1])
                            for l in range(4):
                                _li = t.matmul(psq[l][:, 0:w],
                                         wbt_r[:, (ct % 2) * 512 + l * 128:(ct % 2) * 512 + (l + 1) * 128],
                                         hbf[:, l * THp + off: l * THp + off + w],
                                         start=True, stop=True)
                            _li.then_inc(S_pe.h, 1)
                            S_pe.n += 1
                            pem[(ct, si)] = S_pe.n
                        @blk.vector
                        def _(v, ct=ct, si=si, w=w, u=u, slot=slot):
                            v.wait_ge(S_pe.h, pem[(ct, si)])
                            v.wait_ge(*xldm[u])
                            if u >= 2:
                                v.wait_ge(*stom[u - 2])
                                if nxt_buf is not None:
                                    v.wait_ge(S_ac.h, x1m[u - 2])
                            v.tensor_copy(psacc[:, 0:w], psq[0][:, 0:w])
                            mb = slot * 3 * 512
                            v.copy_predicated(psacc[:, 0:w], mtl[:, mb:mb + w], psq[1][:, 0:w])
                            v.copy_predicated(psacc[:, 0:w], mtl[:, mb + 512:mb + 512 + w], psq[2][:, 0:w])
                            v.copy_predicated(psacc[:, 0:w], mtl[:, mb + 1024:mb + 1024 + w], psq[3][:, 0:w])
                            v.scalar_tensor_tensor(t_scr[:, 0:w], psb[u % 2][:, 0:w], 0.5,
                                                   xtl[:, slot * 512: slot * 512 + w],
                                                   op0=ALU.mult, op1=ALU.add)
                            v.tensor_scalar(t_scr[:, 0:w], t_scr[:, 0:w], b0h_sb[:, ct:ct + 1],
                                            None, op0=ALU.add)
                            _li = v.scalar_tensor_tensor(x1tl[:, slot * 512: slot * 512 + w],
                                                   psacc[:, 0:w], 4.0, t_scr[:, 0:w],
                                                   op0=ALU.mult, op1=ALU.add)
                            _li.then_inc(S_dv.h, 1)
                            S_dv.n += 1
                            dvm[u] = S_dv.n
                        @blk.sync
                        def _(s, ct=ct, off=off, w=w, u=u, slot=slot):
                            s.wait_ge(S_dv.h, dvm[u])
                            q = rsem("rs1", u)
                            s.dma_start(out=out_dst[ct * 128:(ct + 1) * 128, off:off + w],
                                        in_=x1tl[:, slot * 512: slot * 512 + w]).then_inc(q.h, 16)
                            q.n += 16
                            stom[u] = (q.h, q.n)
                        if nxt_buf is not None:
                            @blk.scalar
                            def _(sc, ct=ct, off=off, w=w, u=u, slot=slot):
                                sc.wait_ge(S_dv.h, dvm[u])
                                sc.activation(nxt_buf[:, ct * THp + off: ct * THp + off + w],
                                              x1tl[:, slot * 512: slot * 512 + w],
                                              AF.Copy).then_inc(S_ac.h, 1)
                                S_ac.n += 1
                                x1m[u] = S_ac.n
            barrier()

        if phases < 2:
            finish()
            return nc
        pscope = ExitStack()
        with pscope:
            xn_b = pscope.enter_context(nc.sbuf_tensor("xn_b", [128, DT_D * THp], BF16))
            emit_ffn(xn_a, xT, x1_scr, TH, xn_b, "f1")
            if phases < 3:
                finish()
                return nc
            ln_stats(xn_b, subw1)
            ln_fin_bcast_apply(xn_b, TH, subw1)
            barrier()
            if phases < 4:
                finish()
                return nc

            p2 = ExitStack()
            with p2:
                psip = [p2.enter_context(nc.psum_tensor(f"psip{j}", [128, 512], F32)) for j in range(2)]
                wbuf = p2.enter_context(nc.sbuf_tensor("wbufp2", [128, 2 * DM], BF16))
                hbcT = p2.enter_context(nc.sbuf_tensor("hbcT", [128, THp], F32))
                cacc = p2.enter_context(nc.sbuf_tensor("cacc", [128, T], F32))
                sigt = p2.enter_context(nc.sbuf_tensor("sigt", [128, T], F32))
                gtt = p2.enter_context(nc.sbuf_tensor("gtt", [128, 2 * 512], BF16))
                xbcb = p2.enter_context(nc.sbuf_tensor("xbcb", [128, 2 * T], BF16))
                wldm = {}; pem = {}; evm = {}; cvm = {}; stom = {}; silm = {}; sigm = {}
                for ct in range(80):
                    @blk.gpsimd
                    def _(g, ct=ct):
                        if ct >= 2:
                            g.wait_ge(S_pe.h, pem[(ct - 2, 2)])
                        q = rsem("rw", ct)
                        g.dma_start(out=wbuf[:, (ct % 2) * DM:(ct % 2 + 1) * DM],
                                    in_=wpt[ct]).then_inc(q.h, 16)
                        q.n += 16
                        wldm[ct] = (q.h, q.n)
                    for si, (off, w) in enumerate(subw1):
                        u = ct * 3 + si
                        @blk.tensor
                        def _(t, ct=ct, si=si, off=off, w=w, u=u):
                            t.wait_ge(*wldm[ct])
                            if u >= 2:
                                t.wait_ge(S_ac.h, evm[u - 2])
                            bb = psip[u % 2]
                            for dt in range(DT_D):
                                _li = t.matmul(bb[:, 0:w],
                                         wbuf[:, (ct % 2) * DM + dt * 128:(ct % 2) * DM + (dt + 1) * 128],
                                         xn_b[:, dt * THp + off: dt * THp + off + w],
                                         start=(dt == 0), stop=(dt == DT_D - 1))
                            _li.then_inc(S_pe.h, 1)
                            S_pe.n += 1
                            pem[(ct, si)] = S_pe.n
                        @blk.scalar
                        def _(sc, ct=ct, si=si, off=off, w=w, u=u):
                            sc.wait_ge(S_pe.h, pem[(ct, si)])
                            if ct < 32:
                                if u >= 2 and ("g", u - 2) in stom:
                                    sc.wait_ge(*stom[("g", u - 2)])
                                sc.activation(gtt[:, (u % 2) * 512:(u % 2) * 512 + w],
                                              psip[u % 2][:, 0:w], AF.Copy).then_inc(S_ac.h, 1)
                            else:
                                if si == 0 and ct >= 33:
                                    sc.wait_ge(S_dv.h, cvm.get(ct - 1, 0))
                                sc.activation(hbcT[:, off:off + w],
                                              psip[u % 2][:, 0:w], AF.Copy).then_inc(S_ac.h, 1)
                            S_ac.n += 1
                            evm[u] = S_ac.n
                        if ct < 32:
                            @blk.sync
                            def _(s, ct=ct, si=si, off=off, w=w, u=u):
                                s.wait_ge(S_ac.h, evm[u])
                                q = rsem("rg", u)
                                s.dma_start(out=gate_scr[ct * 128:(ct + 1) * 128, off:off + w],
                                            in_=gtt[:, (u % 2) * 512:(u % 2) * 512 + w]
                                            ).then_inc(q.h, 16)
                                q.n += 16
                                stom[("g", u)] = (q.h, q.n)
                    if ct >= 32:
                        hc = ct - 32
                        @blk.vector
                        def _(v, ct=ct, hc=hc):
                            v.wait_ge(S_ac.h, evm[ct * 3 + 2])
                            if hc >= 1:
                                v.wait_ge(S_dv.h, silm.get(hc - 1, 0))
                            o0 = (hc % 2) * T
                            v.tensor_scalar(hbcT[:, 0:3], hbcT[:, 0:3],
                                            halo_sb[:, 0:1], None, op0=ALU.mult)
                            v.tensor_scalar(cacc[:, 0:T], hbcT[:, 0:T],
                                            convw_sb[:, hc * 4: hc * 4 + 1],
                                            convb_sb[:, hc:hc + 1], op0=ALU.mult, op1=ALU.add)
                            for k in range(1, 4):
                                _li = v.scalar_tensor_tensor(cacc[:, 0:T], hbcT[:, k:k + T],
                                                       convw_sb[:, hc * 4 + k: hc * 4 + k + 1],
                                                       cacc[:, 0:T], op0=ALU.mult, op1=ALU.add)
                            _li.then_inc(S_dv.h, 1)
                            S_dv.n += 1
                            cvm[ct] = S_dv.n
                        @blk.scalar
                        def _(sc, ct=ct, hc=hc):
                            sc.wait_ge(S_dv.h, cvm[ct])
                            need_set(sc, 2)
                            sc.activation(sigt[:], cacc[:, 0:T],
                                          AF.Sigmoid).then_inc(S_ac.h, 1)
                            S_ac.n += 1
                            sigm[hc] = S_ac.n
                        @blk.vector
                        def _(v, hc=hc):
                            v.wait_ge(S_ac.h, sigm[hc])
                            if hc >= 2 and ("x", hc - 2) in stom:
                                v.wait_ge(*stom[("x", hc - 2)])
                            o0 = (hc % 2) * T
                            _li = v.tensor_tensor(xbcb[:, o0:o0 + T], sigt[:],
                                                  cacc[:, 0:T], op=ALU.mult)
                            _li.then_inc(S_dv.h, 1)
                            S_dv.n += 1
                            silm[hc] = S_dv.n
                        @blk.sync
                        def _(s, hc=hc):
                            s.wait_ge(S_dv.h, silm[hc])
                            if hc < 32:
                                dst = xssm_scr[hc * 128:(hc + 1) * 128, :]
                            elif hc < 40:
                                dst = B_scr[(hc - 32) * 128:(hc - 31) * 128, :]
                            else:
                                dst = C_scr[(hc - 40) * 128:(hc - 39) * 128, :]
                            q = rsem("rxs", hc)
                            s.dma_start(out=dst, in_=xbcb[:, (hc % 2) * T:(hc % 2) * T + T]
                                        ).then_inc(q.h, 16)
                            q.n += 16
                            stom[("x", hc)] = (q.h, q.n)
                @blk.gpsimd
                def _(g):
                    g.wait_ge(S_pe.h, pem[(78, 2)])
                    q = rsem("rw", 80)
                    g.dma_start(out=wbuf[:, 0:DM], in_=wpt[80]).then_inc(q.h, 16)
                    q.n += 16
                wld80 = (rsem("rw", 80).h, rsem("rw", 80).n)
                ps_dt = p2.enter_context(nc.psum_tensor("ps_dt", [128, 512], F32))
                ps_ca = p2.enter_context(nc.psum_tensor("ps_ca", [128, 512], F32))
                ps_tr_full = p2.enter_context(nc.psum_tensor("ps_tr", [128, 512], F32))
                ps_bc = p2.enter_context(nc.psum_tensor("ps_bc", [128, 512], F32))
                dt_sc = p2.enter_context(nc.sbuf_tensor("dt_sc", [128, 64], F32))
                dt_s2 = p2.enter_context(nc.sbuf_tensor("dt_s2", [128, 64], F32))
                dt_s3 = p2.enter_context(nc.sbuf_tensor("dt_s3", [128, 64], F32))
                caf_sc = p2.enter_context(nc.sbuf_tensor("caf_sc", [64, 128], F32))
                cer_sc = p2.enter_context(nc.sbuf_tensor("cer_sc", [1, 64], F32))
                for ch in range(NCH):
                    @blk.tensor
                    def _(t, ch=ch):
                        t.wait_ge(*wld80)
                        if ch > 0:
                            t.wait_ge(S_ac.h, S_ac.n)
                        for dt in range(DT_D):
                            _li = t.matmul(ps_dt[:, 0:64],
                                     xn_b[:, dt * THp + 3 + ch * Q: dt * THp + 3 + (ch + 1) * Q],
                                     wbuf[:, dt * 128: dt * 128 + 64],
                                     start=(dt == 0), stop=(dt == DT_D - 1))
                        _li.then_inc(S_pe.h, 1)
                        S_pe.n += 1
                    pe1 = S_pe.n
                    @blk.vector
                    def _(v, pe1=pe1):
                        v.wait_ge(S_pe.h, pe1)
                        _li = v.tensor_tensor(dt_sc[:], ps_dt[:, 0:64], dtbias_sb[:], op=ALU.add)
                        _li.then_inc(S_dv.h, 1)
                        S_dv.n += 1
                    dv1 = S_dv.n
                    @blk.scalar
                    def _(sc, dv1=dv1):
                        sc.wait_ge(S_dv.h, dv1)
                        need_set(sc, 6)
                        sc.activation(dt_s2[:], dt_sc[:], AF.Abs)
                        sc.activation(dt_s2[:], dt_s2[:], AF.Exp, scale=-1.0)
                        _li = sc.activation(dt_s2[:], dt_s2[:], AF.Ln, bias=1.0)
                        _li.then_inc(S_ac.h, 1)
                        S_ac.n += 1
                    ac1 = S_ac.n
                    @blk.vector
                    def _(v, ch=ch, ac1=ac1):
                        v.wait_ge(S_ac.h, ac1)
                        v.tensor_scalar(dt_s3[:], dt_sc[:], 0.0, None, op0=ALU.max)
                        v.tensor_tensor(dtp_all[:, ch * 64:(ch + 1) * 64], dt_s3[:], dt_s2[:], op=ALU.add)
                        _li = v.tensor_tensor(dt_s3[:], dtp_all[:, ch * 64:(ch + 1) * 64], expA_sb[:], op=ALU.mult)
                        _li.then_inc(S_dv.h, 1)
                        S_dv.n += 1
                    dv2 = S_dv.n
                    @blk.tensor
                    def _(t, dv2=dv2):
                        t.wait_ge(S_dv.h, dv2)
                        _li = t.matmul(ps_ca[:, 0:64], ctrif_sb[:], dt_s3[:], start=True, stop=True)
                        _li.then_inc(S_pe.h, 1)
                        S_pe.n += 1
                    pe2 = S_pe.n
                    @blk.scalar
                    def _(sc, ch=ch, pe2=pe2):
                        sc.wait_ge(S_pe.h, pe2)
                        _li = sc.activation(ca_all[:, ch * 64:(ch + 1) * 64], ps_ca[:, 0:64], AF.Copy)
                        _li.then_inc(S_ac.h, 1)
                        S_ac.n += 1
                    ac2 = S_ac.n
                    @blk.tensor
                    def _(t, ch=ch, ac2=ac2):
                        t.wait_ge(S_ac.h, ac2)
                        _li = t.transpose(ps_tr_full[0:64, 0:128], ca_all[:, ch * 64:(ch + 1) * 64], identf_sb[:])
                        _li.then_inc(S_pe.h, 1)
                        S_pe.n += 1
                    pe3 = S_pe.n
                    @blk.scalar
                    def _(sc, pe3=pe3):
                        sc.wait_ge(S_pe.h, pe3)
                        sc.activation(caf_sc[:], ps_tr_full[0:64, 0:128], AF.Copy).then_inc(S_ac.h, 1)
                        S_ac.n += 1
                    ac3 = S_ac.n
                    @blk.sync
                    def _(s, ch=ch, ac3=ac3):
                        s.wait_ge(S_ac.h, ac3)
                        q = rsem("rcs", 0, n=1)
                        s.dma_start(out=ca_scr[ch:ch + 1, :].rearrange("o (h q) -> h (o q)", h=64),
                                    in_=caf_sc[:]).then_inc(q.h, 16)
                        s.dma_start(out=cae_scr[ch:ch + 1, :].rearrange("o h -> h o"),
                                    in_=caf_sc[:, 127:128]).then_inc(q.h, 16)
                        q.n += 32
                        s.wait_ge(q.h, q.n)
                        q2 = rsem("rcb", 0, n=1)
                        s.dma_start(out=cer_sc[:], in_=cae_scr[ch:ch + 1, :]).then_inc(q2.h, 16)
                        q2.n += 16
                    xl1 = (rsem("rcb", 0, n=1).h, rsem("rcb", 0, n=1).n)
                    @blk.tensor
                    def _(t, xl1=xl1):
                        t.wait_ge(*xl1)
                        _li = t.matmul(ps_bc[:, 0:64], ones1f_sb[:], cer_sc[:], start=True, stop=True)
                        _li.then_inc(S_pe.h, 1)
                        S_pe.n += 1
                    pe4 = S_pe.n
                    @blk.scalar
                    def _(sc, ch=ch, pe4=pe4):
                        sc.wait_ge(S_pe.h, pe4)
                        sc.activation(caendb_all[:, ch * 64:(ch + 1) * 64], ps_bc[:, 0:64], AF.Copy)
                        _li = sc.activation(dAtot_all[:, ch * 64:(ch + 1) * 64], ps_bc[:, 0:64], AF.Exp)
                        _li.then_inc(S_ac.h, 1)
                        S_ac.n += 1
                @blk.vector
                def _(v):
                    v.wait_ge(S_ac.h, S_ac.n)
                    v.tensor_copy(aprod_sb[:], dAtot_all[:, 0:64])
                    for ch in range(1, NCH):
                        _li = v.tensor_tensor(aprod_sb[:], aprod_sb[:],
                                        dAtot_all[:, ch * 64:(ch + 1) * 64], op=ALU.mult)
                    _li.then_inc(S_dv.h, 1)
                    S_dv.n += 1
            barrier()
            if phases < 5:
                finish()
                return nc
        barrier()

        p3 = ExitStack()
        with p3:
            S_cur = p3.enter_context(nc.sbuf_tensor("S_cur", [128, 4096], F32))
            S_bfb = p3.enter_context(nc.sbuf_tensor("S_bfb", [128, 4096], BF16))
            xsl = p3.enter_context(nc.sbuf_tensor("xsl", [128, 2 * 4096], BF16))
            dtx = p3.enter_context(nc.sbuf_tensor("dtx", [128, 2 * 4096], BF16))
            bsl = p3.enter_context(nc.sbuf_tensor("bsl", [128, 2 * 1024], BF16))
            csl = p3.enter_context(nc.sbuf_tensor("csl", [128, 2 * 1024], BF16))
            btok = p3.enter_context(nc.sbuf_tensor("btok", [128, 8 * 128], BF16))
            dtxw = p3.enter_context(nc.sbuf_tensor("dtxw", [128, 512], BF16))
            wcol = p3.enter_context(nc.sbuf_tensor("wcol", [128, 64], F32))
            ps_tx_full = p3.enter_context(nc.psum_tensor("ps_tx", [128, 2048], BF16))
            ps_S = p3.enter_context(nc.psum_tensor("ps_S", [128, 512], F32))

            def slab_rearr(scr):
                return scr.rearrange("(a p) t -> p a t", p=128)

            def emit_slabs(ch, mk, want_bc):
                @blk.sync
                def _(s, ch=ch):
                    if ch >= 2:
                        s.wait_ge(S_dv.h, mk[("xslfree", ch - 2)])
                        s.wait_ge(S_dv.h, mk[("bcfree", ch - 2)])
                    q = rsem("rsl", ch)
                    s.dma_start(
                        out=xsl[:, (ch % 2) * 4096:(ch % 2 + 1) * 4096].rearrange(
                            "p (a t) -> p a t", a=32),
                        in_=slab_rearr(xssm_scr)[:, :, ch * Q:(ch + 1) * Q]).then_inc(q.h, 16)
                    q.n += 16
                    if want_bc:
                        s.dma_start(
                            out=csl[:, (ch % 2) * 1024:(ch % 2 + 1) * 1024].rearrange(
                                "p (a t) -> p a t", a=8),
                            in_=slab_rearr(C_scr)[:, :, ch * Q:(ch + 1) * Q]).then_inc(q.h, 16)
                        q.n += 16
                    s.dma_start(
                        out=bsl[:, (ch % 2) * 1024:(ch % 2 + 1) * 1024].rearrange(
                            "p (a t) -> p a t", a=8),
                        in_=slab_rearr(B_scr)[:, :, ch * Q:(ch + 1) * Q]).then_inc(q.h, 16)
                    q.n += 16
                    mk[("xld", ch)] = (q.h, q.n)

            def emit_dtx(ch, mk):
                for m in range(32):
                    tb = (m % 2) * 1024
                    @blk.tensor
                    def _(t, ch=ch, m=m, tb=tb):
                        if m == 0:
                            t.wait_ge(*mk[("xld", ch)])
                            if ch >= 2:
                                t.wait_ge(S_pe.h, mk.get(("dtxfree", ch - 2), 0))
                            if ("btkev_last", ch - 1) in mk:
                                t.wait_ge(S_ac.h, mk[("btkev_last", ch - 1)])
                        if m > 1:
                            t.wait_ge(S_dv.h, mk[("dtx", ch, m - 2)])
                        _li = t.transpose(ps_tx_full[:, tb:tb + 128],
                                    xsl[:, (ch % 2) * 4096 + m * 128:(ch % 2) * 4096 + (m + 1) * 128],
                                    identb_sb[:])
                        _li.then_inc(S_pe.h, 1)
                        S_pe.n += 1
                        mk[("tx", ch, m)] = S_pe.n
                    @blk.vector
                    def _(v, ch=ch, m=m, tb=tb):
                        v.wait_ge(S_pe.h, mk[("tx", ch, m)])
                        d0 = (ch % 2) * 4096 + m * 128
                        for hh in range(2):
                            h = 2 * m + hh
                            _li = v.tensor_scalar(dtx[:, d0 + hh * 64: d0 + (hh + 1) * 64],
                                            ps_tx_full[:, tb + hh * 64: tb + (hh + 1) * 64],
                                            dtp_all[:, ch * 64 + h: ch * 64 + h + 1], None,
                                            op0=ALU.mult)
                        _li.then_inc(S_dv.h, 1)
                        S_dv.n += 1
                        mk[("dtx", ch, m)] = S_dv.n
                mk[("xslfree", ch)] = mk[("dtx", ch, 31)]

            def emit_contrib(ch, mk):
                @blk.vector
                def _(v, ch=ch):
                    v.wait_ge(S_dv.h, mk[("dtx", ch, 31)])
                    _li = v.tensor_tensor(wcol[:], caendb_all[:, ch * 64:(ch + 1) * 64],
                                    ca_all[:, ch * 64:(ch + 1) * 64], op=ALU.subtract)
                    _li.then_inc(S_dv.h, 1)
                    S_dv.n += 1
                dvw = S_dv.n
                @blk.scalar
                def _(sc, dvw=dvw):
                    sc.wait_ge(S_dv.h, dvw)
                    sc.activation(wcol[:], wcol[:], AF.Exp).then_inc(S_ac.h, 1)
                    S_ac.n += 1
                acw = S_ac.n
                for g in range(8):
                    @blk.tensor
                    def _(t, ch=ch, g=g, acw=acw):
                        if g == 0:
                            t.wait_ge(S_ac.h, acw)
                        else:
                            t.wait_ge(S_ac.h, mk[("btk_ev", ch, g - 1)])
                        _li = t.transpose(ps_tx_full[:, 0:128],
                                    bsl[:, (ch % 2) * 1024 + g * 128:(ch % 2) * 1024 + (g + 1) * 128],
                                    identb_sb[:])
                        _li.then_inc(S_pe.h, 1)
                        S_pe.n += 1
                        mk[("btk", ch, g)] = S_pe.n
                    @blk.scalar
                    def _(sc, ch=ch, g=g):
                        sc.wait_ge(S_pe.h, mk[("btk", ch, g)])
                        sc.activation(btok[:, g * 128:(g + 1) * 128], ps_tx_full[:, 0:128], AF.Copy).then_inc(S_ac.h, 1)
                        S_ac.n += 1
                        mk[("btk_ev", ch, g)] = S_ac.n
                    @blk.vector
                    def _(v, ch=ch, g=g, acw=acw):
                        v.wait_ge(S_ac.h, acw)
                        if g > 0:
                            v.wait_ge(S_pe.h, mk[("ctr", ch, g - 1)])
                        d0 = (ch % 2) * 4096 + g * 512
                        for hh in range(8):
                            h = g * 8 + hh
                            _li = v.tensor_scalar(dtxw[:, hh * 64:(hh + 1) * 64],
                                            dtx[:, d0 + hh * 64: d0 + (hh + 1) * 64],
                                            wcol[:, h:h + 1], None, op0=ALU.mult)
                        _li.then_inc(S_dv.h, 1)
                        S_dv.n += 1
                        mk[("dtxw", ch, g)] = S_dv.n
                    @blk.tensor
                    def _(t, ch=ch, g=g):
                        t.wait_ge(S_ac.h, mk[("btk_ev", ch, g)])
                        t.wait_ge(S_dv.h, mk[("dtxw", ch, g)])
                        prev = mk[("supd", ch, g - 1)] if g > 0 else mk.get(("supd_last", ch), 0)
                        t.wait_ge(S_dv.h, prev)
                        _li = t.matmul(ps_S[:], btok[:, g * 128:(g + 1) * 128], dtxw[:], start=True, stop=True)
                        _li.then_inc(S_pe.h, 1)
                        S_pe.n += 1
                        mk[("ctr", ch, g)] = S_pe.n
                    @blk.vector
                    def _(v, ch=ch, g=g):
                        v.wait_ge(S_pe.h, mk[("ctr", ch, g)])
                        for hh in range(8):
                            h = g * 8 + hh
                            _li = v.scalar_tensor_tensor(
                                S_cur[:, h * 64:(h + 1) * 64], S_cur[:, h * 64:(h + 1) * 64],
                                dAtot_all[:, ch * 64 + h: ch * 64 + h + 1],
                                ps_S[:, hh * 64:(hh + 1) * 64], op0=ALU.mult, op1=ALU.add)
                        _li.then_inc(S_dv.h, 1)
                        S_dv.n += 1
                        mk[("supd", ch, g)] = S_dv.n
                mk[("supd_last", ch + 1)] = mk[("supd", ch, 7)]
                mk[("bcfree", ch)] = mk[("supd", ch, 7)]

            m1 = {("supd_last", 0): 0}
            @blk.vector
            def _(v):
                _li = v.memset(S_cur[:], 0.0)
                _li.then_inc(S_dv.h, 1)
                S_dv.n += 1
            for ch in range(NCH):
                emit_slabs(ch, m1, want_bc=False)
                emit_dtx(ch, m1)
                emit_contrib(ch, m1)
            @blk.sync
            def _(s):
                s.wait_ge(S_dv.h, m1[("supd", NCH - 1, 7)])
                q = rsem("rcp", 0, n=1)
                s.dma_start(out=coll_in.ap()[0:128, :], in_=S_cur[:]).then_inc(q.h, 16)
                s.dma_start(out=coll_in.ap()[128:129, :].rearrange("p (a f) -> p a f", a=64),
                            in_=aprod_sb[0:1, :].rearrange("p (o f) -> p o f", o=1
                                                           ).broadcast_to([1, 64, 64])
                            ).then_inc(q.h, 16)
                q.n += 32
            sto_cl = (rsem("rcp", 0, n=1).h, rsem("rcp", 0, n=1).n)
            @blk.gpsimd
            def _(g):
                g.wait_ge(*sto_cl)
                g.wait_ge(S_ac.h, S_ac.n)
                g.collective_compute("AllGather", ALU.bypass,
                                     replica_groups=[[0, 1, 2, 3], [4, 5, 6, 7]],
                                     ins=[coll_in.ap().opt()],
                                     outs=[coll_out.ap().opt()]).then_inc(S_col.h, 1)
                S_col.n += 1
            cur_set[0] = None
            pcmb = ExitStack()
            pcmb.__enter__()
            sg = pcmb.enter_context(nc.sbuf_tensor("sg", [128, 4096], F32))
            arow = pcmb.enter_context(nc.sbuf_tensor("arow", [1, 256], F32))
            wrow = pcmb.enter_context(nc.sbuf_tensor("wrow", [1, 320], F32))
            wbc = pcmb.enter_context(nc.sbuf_tensor("wbc", [128, 64], F32))
            ps_wb = pcmb.enter_context(nc.psum_tensor("ps_wb", [128, 512], F32))
            @blk.sync
            def _(s):
                s.wait_ge(S_col.h, S_col.n)
                q = rsem("rar", 0, n=1)
                for j in range(4):
                    s.dma_start(out=arow[0:1, j * 64:(j + 1) * 64],
                                in_=coll_out.ap()[129 * j + 128: 129 * j + 129, 0:64]
                                ).then_inc(q.h, 16)
                q.n += 64
            xl_ar = (rsem("rar", 0, n=1).h, rsem("rar", 0, n=1).n)
            @blk.vector
            def _(v, xl_ar=xl_ar):
                v.wait_ge(*xl_ar)
                for k2 in range(4):
                    v.tensor_scalar(wrow[0:1, k2 * 64:(k2 + 1) * 64],
                                    arow[0:1, k2 * 64:(k2 + 1) * 64], -1.0, None, op0=ALU.add)
                    v.tensor_tensor(wrow[0:1, k2 * 64:(k2 + 1) * 64],
                                    wrow[0:1, k2 * 64:(k2 + 1) * 64],
                                    actrow_sb[0:1, k2 * 64:(k2 + 1) * 64], op=ALU.mult)
                    v.tensor_scalar(wrow[0:1, k2 * 64:(k2 + 1) * 64],
                                    wrow[0:1, k2 * 64:(k2 + 1) * 64], 1.0, None, op0=ALU.add)
                v.tensor_copy(wrow[0:1, 256:320], wrow[0:1, 192:256])
                v.tensor_tensor(wrow[0:1, 192:256], mselrow_sb[0:1, 192:256],
                                mselrow_sb[0:1, 192:256], op=ALU.mult)
                v.tensor_tensor(wrow[0:1, 128:192], wrow[0:1, 256:320],
                                mselrow_sb[0:1, 128:192], op=ALU.mult)
                v.tensor_tensor(wrow[0:1, 256:320], wrow[0:1, 64:128], wrow[0:1, 256:320],
                                op=ALU.mult)
                v.tensor_tensor(wrow[0:1, 64:128], wrow[0:1, 256:320],
                                mselrow_sb[0:1, 64:128], op=ALU.mult)
                v.tensor_tensor(wrow[0:1, 256:320], wrow[0:1, 0:64], wrow[0:1, 256:320],
                                op=ALU.mult)
                _li = v.tensor_tensor(wrow[0:1, 0:64], wrow[0:1, 256:320],
                                mselrow_sb[0:1, 0:64], op=ALU.mult)
                _li.then_inc(S_dv.h, 1)
                S_dv.n += 1
            dv_w = S_dv.n
            sinm = {}
            for j in range(4):
                @blk.sync
                def _(s, j=j):
                    if j >= 1:
                        s.wait_ge(S_dv.h, sinm[j - 1])
                    q = rsem("rsg", 0, n=1)
                    s.dma_start(out=sg[:], in_=coll_out.ap()[129 * j: 129 * j + 128, :]
                                ).then_inc(q.h, 16)
                    q.n += 16
                xl_j = (rsem("rsg", 0, n=1).h, rsem("rsg", 0, n=1).n)
                @blk.tensor
                def _(t, j=j, dv_w=dv_w):
                    t.wait_ge(S_dv.h, dv_w)
                    if j > 0:
                        t.wait_ge(S_dv.h, sinm[j - 1])
                    _li = t.matmul(ps_wb[:, 0:64], ones1f_sb[:], wrow[0:1, j * 64:(j + 1) * 64],
                             start=True, stop=True)
                    _li.then_inc(S_pe.h, 1)
                    S_pe.n += 1
                pe_j = S_pe.n
                @blk.scalar
                def _(sc, pe_j=pe_j):
                    sc.wait_ge(S_pe.h, pe_j)
                    need_set(sc, 6)
                    sc.activation(wbc[:], ps_wb[:, 0:64], AF.Copy).then_inc(S_ac.h, 1)
                    S_ac.n += 1
                ac_j = S_ac.n
                @blk.vector
                def _(v, j=j, ac_j=ac_j, xl_j=xl_j):
                    v.wait_ge(S_ac.h, ac_j)
                    v.wait_ge(*xl_j)
                    for h in range(64):
                        src = sg[:, h * 64:(h + 1) * 64]
                        if j == 0:
                            _li = v.tensor_scalar(S_cur[:, h * 64:(h + 1) * 64], src,
                                                  wbc[:, h:h + 1], None, op0=ALU.mult)
                        else:
                            _li = v.scalar_tensor_tensor(S_cur[:, h * 64:(h + 1) * 64], src,
                                                   wbc[:, h:h + 1],
                                                   S_cur[:, h * 64:(h + 1) * 64],
                                                   op0=ALU.mult, op1=ALU.add)
                    _li.then_inc(S_dv.h, 1)
                    S_dv.n += 1
                    sinm[j] = S_dv.n
            barrier()
            pcmb.__exit__(None, None, None)
            if phases < 6:
                finish()
                return nc

            gm = p3.enter_context(nc.sbuf_tensor("gm", [128, 128], F32))
            dcl = p3.enter_context(nc.sbuf_tensor("dcl", [128, 2 * 512], F32))
            E_sb = p3.enter_context(nc.sbuf_tensor("E_sb", [128, 2 * 512], F32))
            mt4 = p3.enter_context(nc.sbuf_tensor("mt4", [128, 2 * 512], BF16))
            c4 = p3.enter_context(nc.sbuf_tensor("c4", [128, 2 * 512], BF16))
            bcb = p3.enter_context(nc.sbuf_tensor("bcb", [128, 4 * 512], F32))
            caeb = p3.enter_context(nc.sbuf_tensor("caeb", [128, 4 * 512], BF16))
            car_sb = p3.enter_context(nc.sbuf_tensor("car_sb", [64, 128], F32))
            ps_G_full = p3.enter_context(nc.psum_tensor("ps_G", [128, 512], F32))
            ps_y = [p3.enter_context(nc.psum_tensor(f"ps_y{j}", [128, 512], F32)) for j in range(2)]
            m2 = {("supd_last", 0): 0}
            for ch in range(NCH):
                @blk.scalar
                def _(sc, ch=ch):
                    if ch == 0:
                        sc.wait_ge(S_dv.h, sinm[3])
                    else:
                        sc.wait_ge(S_dv.h, m2[("supd", ch - 1, 7)])
                        sc.wait_ge(S_pe.h, m2[("y", ch - 1, 31)])
                    sc.activation(S_bfb[:], S_cur[:], AF.Copy).then_inc(S_ac.h, 1)
                    S_ac.n += 1
                    m2[("sbf", ch)] = S_ac.n
                @blk.sync
                def _(s, ch=ch):
                    if ch >= 1:
                        s.wait_ge(*m2[("rowst", ch - 1)])
                    q = rsem("rca2", ch)
                    s.dma_start(out=car_sb[:],
                                in_=ca_scr[ch:ch + 1, :].rearrange("o (p f) -> (o p) f", p=64)
                                ).then_inc(q.h, 16)
                    q.n += 16
                    m2[("carld", ch)] = (q.h, q.n)
                @blk.scalar
                def _(sc, ch=ch):
                    sc.wait_ge(*m2[("carld", ch)])
                    sc.activation(car_sb[:], car_sb[:], AF.Exp).then_inc(S_ac.h, 1)
                    S_ac.n += 1
                    m2[("caexp", ch)] = S_ac.n
                @blk.gpsimd
                def _(g, ch=ch):
                    g.wait_ge(S_ac.h, m2[("caexp", ch)])
                    q = rsem("rca3", ch)
                    g.dma_start(out=caer_scr[ch:ch + 1, :].rearrange("o (p f) -> (o p) f", p=64),
                                in_=car_sb[:]).then_inc(q.h, 16)
                    q.n += 16
                    m2[("rowst", ch)] = (q.h, q.n)
                emit_slabs(ch, m2, want_bc=True)
                emit_dtx(ch, m2)
                for gq in range(16):
                    g, half = gq // 2, gq % 2
                    hb = g * 8 + half * 4
                    @blk.sync
                    def _(s, ch=ch, gq=gq, hb=hb):
                        sl = gq % 4
                        prevd = m2.get(("dclv", ch, gq - 4), m2.get(("dclv", ch - 1, 15), 0))
                        s.wait_ge(S_dv.h, prevd)
                        q = rsem("rbc", gq, n=4)
                        s.dma_start(out=bcb[:, sl * 512:(sl + 1) * 512],
                                    in_=ca_scr[ch:ch + 1, hb * 128:(hb + 4) * 128]
                                    .broadcast_to([128, 512])).then_inc(q.h, 16)
                        q.n += 16
                        m2[("bcb", ch, gq)] = (q.h, q.n)
                        prevm = m2.get(("mt", ch, gq - 4), m2.get(("mt", ch - 1, 15), 0))
                        s.wait_ge(S_dv.h, prevm)
                        s.wait_ge(*m2[("rowst", ch)])
                        q2r = rsem("rce", gq, n=4)
                        s.dma_start(out=caeb[:, sl * 512:(sl + 1) * 512],
                                    in_=caer_scr[ch:ch + 1, hb * 128:(hb + 4) * 128]
                                    .broadcast_to([128, 512])).then_inc(q2r.h, 16)
                        q2r.n += 16
                        m2[("ceb", ch, gq)] = (q2r.h, q2r.n)
                    if half == 0:
                        @blk.tensor
                        def _(t, ch=ch, g=g):
                            prev = m2.get(("gm", ch, g - 1), m2.get(("gm", ch - 1, 7), 0))
                            t.wait_ge(S_dv.h, prev)
                            t.wait_ge(*m2[("xld", ch)])
                            _li = t.matmul(ps_G_full[:, 0:128],
                                     bsl[:, (ch % 2) * 1024 + g * 128:(ch % 2) * 1024 + (g + 1) * 128],
                                     csl[:, (ch % 2) * 1024 + g * 128:(ch % 2) * 1024 + (g + 1) * 128],
                                     start=True, stop=True)
                            _li.then_inc(S_pe.h, 1)
                            S_pe.n += 1
                            m2[("G", ch, g)] = S_pe.n
                        @blk.vector
                        def _(v, ch=ch, g=g):
                            v.wait_ge(S_pe.h, m2[("G", ch, g)])
                            _li = v.tensor_tensor(gm[:], ps_G_full[:, 0:128], k01f_sb[:], op=ALU.mult)
                            _li.then_inc(S_dv.h, 1)
                            S_dv.n += 1
                            m2[("gm", ch, g)] = S_dv.n
                    q2 = gq % 2
                    @blk.vector
                    def _(v, ch=ch, gq=gq, hb=hb, q2=q2):
                        v.wait_ge(*m2[("bcb", ch, gq)])
                        prevE = m2.get(("E", ch, gq - 2), m2.get(("E", ch - 1, 15), 0))
                        v.wait_ge(S_ac.h, prevE)
                        d0 = q2 * 512
                        sl = gq % 4
                        for hq in range(4):
                            h = hb + hq
                            _li = v.tensor_scalar(dcl[:, d0 + hq * 128: d0 + (hq + 1) * 128],
                                            bcb[:, sl * 512 + hq * 128: sl * 512 + (hq + 1) * 128],
                                            ca_all[:, ch * 64 + h: ch * 64 + h + 1],
                                            0.0, op0=ALU.subtract, op1=ALU.min)
                        _li.then_inc(S_dv.h, 1)
                        S_dv.n += 1
                        m2[("dclv", ch, gq)] = S_dv.n
                    @blk.scalar
                    def _(sc, ch=ch, gq=gq, q2=q2):
                        sc.wait_ge(S_dv.h, m2[("dclv", ch, gq)])
                        prevmt = m2.get(("mt", ch, gq - 2), m2.get(("mt", ch - 1, 15), 0))
                        sc.wait_ge(S_dv.h, prevmt)
                        _li = sc.activation(E_sb[:, q2 * 512:(q2 + 1) * 512],
                                            dcl[:, q2 * 512:(q2 + 1) * 512], AF.Exp)
                        _li.then_inc(S_ac.h, 1)
                        S_ac.n += 1
                        m2[("E", ch, gq)] = S_ac.n
                    @blk.vector
                    def _(v, ch=ch, gq=gq, g=g, half=half, q2=q2):
                        v.wait_ge(S_ac.h, m2[("E", ch, gq)])
                        v.wait_ge(S_dv.h, m2[("gm", ch, g)])
                        v.wait_ge(*m2[("ceb", ch, gq)])
                        prevy = m2.get(("y", ch, 2 * gq - 3), m2.get(("y", ch - 1, 31), 0))
                        v.wait_ge(S_pe.h, prevy)
                        d0 = q2 * 512
                        sl = gq % 4
                        gmr = gm[:].rearrange("p (o q) -> p o q", o=1).broadcast_to([128, 4, 128])
                        v.tensor_tensor(mt4[:, q2 * 512:(q2 + 1) * 512].rearrange("p (a q) -> p a q", a=4),
                                        E_sb[:, d0:d0 + 512].rearrange("p (a q) -> p a q", a=4), gmr,
                                        op=ALU.mult)
                        cslr = csl[:, (ch % 2) * 1024 + g * 128:(ch % 2) * 1024 + (g + 1) * 128]
                        cfr = cslr.rearrange("p (o q) -> p o q", o=1).broadcast_to([128, 4, 128])
                        _li = v.tensor_tensor(c4[:, q2 * 512:(q2 + 1) * 512].rearrange("p (a q) -> p a q", a=4),
                                        caeb[:, sl * 512:(sl + 1) * 512].rearrange("p (a q) -> p a q", a=4), cfr,
                                        op=ALU.mult)
                        _li.then_inc(S_dv.h, 1)
                        S_dv.n += 1
                        m2[("mt", ch, gq)] = S_dv.n
                    for pr in range(2):
                        h0 = hb + pr * 2
                        xm = h0 // 2
                        pyb = ps_y[xm % 2]
                        @blk.tensor
                        def _(t, ch=ch, gq=gq, hb=hb, h0=h0, xm=xm, pyb=pyb):
                            t.wait_ge(S_dv.h, m2[("mt", ch, gq)])
                            t.wait_ge(S_ac.h, m2[("sbf", ch)])
                            t.wait_ge(S_dv.h, m2[("dtx", ch, 31)])
                            prev = m2.get(("yev", ch, xm - 2), m2.get(("yev", ch - 1, 31), 0))
                            t.wait_ge(S_dv.h, prev)
                            d0 = (ch % 2) * 4096
                            q2b = (gq % 2) * 512
                            for hh in range(2):
                                h = h0 + hh
                                lq = q2b + (h - hb) * 128
                                t.matmul(pyb[hh * 64:(hh + 1) * 64, 0:128],
                                         dtx[:, d0 + h * 64: d0 + (h + 1) * 64],
                                         mt4[:, lq:lq + 128],
                                         start=True, stop=False, tile_position=(0, hh * 64))
                                _li = t.matmul(pyb[hh * 64:(hh + 1) * 64, 0:128],
                                         S_bfb[:, h * 64:(h + 1) * 64],
                                         c4[:, lq:lq + 128],
                                         start=False, stop=True, tile_position=(0, hh * 64))
                            _li.then_inc(S_pe.h, 1)
                            S_pe.n += 1
                            m2[("y", ch, xm)] = S_pe.n
                        @blk.vector
                        def _(v, ch=ch, xm=xm, pyb=pyb):
                            v.wait_ge(S_pe.h, m2[("y", ch, xm)])
                            _li = v.scalar_tensor_tensor(
                                xn_a[:, xm * THp + ch * Q: xm * THp + (ch + 1) * Q],
                                xsl[:, (ch % 2) * 4096 + xm * 128:(ch % 2) * 4096 + (xm + 1) * 128],
                                dcol_sb[:, xm:xm + 1], pyb[:, 0:128],
                                op0=ALU.mult, op1=ALU.add)
                            _li.then_inc(S_dv.h, 1)
                            S_dv.n += 1
                            m2[("yev", ch, xm)] = S_dv.n
                m2[("xslfree", ch)] = m2[("yev", ch, 31)]
                m2[("dtxfree", ch)] = m2[("y", ch, 31)]
                if ch < NCH - 1:
                    emit_contrib(ch, m2)
                else:
                    m2[("bcfree", ch)] = m2[("y", ch, 31)]
        barrier()
        if phases < 7:
            finish()
            return nc

        pscope2 = ExitStack()
        with pscope2:
            xn_b2 = pscope2.enter_context(nc.sbuf_tensor("xn_b2", [128, DT_D * THp], BF16))
            p4 = ExitStack()
            with p4:
                ysq = p4.enter_context(nc.sbuf_tensor("ysq", [128, 2 * 512], BF16))
                wbuf = p4.enter_context(nc.sbuf_tensor("wbufp4", [128, 2 * DM], BF16))
                vrow = p4.enter_context(nc.sbuf_tensor("vrow", [1, 512], F32))
                rb_bc = p4.enter_context(nc.sbuf_tensor("rb_bc", [128, 2 * 512], F32))
                gtile = p4.enter_context(nc.sbuf_tensor("gtile", [128, 2 * 512], BF16))
                gsig = p4.enter_context(nc.sbuf_tensor("gsig", [128, 2 * 512], BF16))
                x1r = p4.enter_context(nc.sbuf_tensor("x1r", [128, 2 * 512], F32))
                ps_v = p4.enter_context(nc.psum_tensor("ps_v", [128, 512], F32))
                ps_rb = p4.enter_context(nc.psum_tensor("ps_rb", [128, 512], F32))
                ps_o = [p4.enter_context(nc.psum_tensor(f"ps_o{j}", [128, 512], F32)) for j in range(2)]
                for si, (off, w) in enumerate([(0, 512), (512, 512)]):
                    mk = {}
                    for g in range(8):
                        for qq in range(4):
                            it = g * 4 + qq
                            @blk.scalar
                            def _(sc, it=it, off=off, w=w):
                                if it >= 2:
                                    sc.wait_ge(S_pe.h, mk[("st", it - 2)])
                                sc.activation(ysq[:, (it % 2) * 512:(it % 2) * 512 + w],
                                              xn_a[:, it * THp + off: it * THp + off + w],
                                              AF.Square).then_inc(S_ac.h, 1)
                                S_ac.n += 1
                                mk[("sq", it)] = S_ac.n
                            @blk.tensor
                            def _(t, it=it, w=w, g=g, qq=qq):
                                t.wait_ge(S_ac.h, mk[("sq", it)])
                                if qq == 0 and g > 0:
                                    t.wait_ge(S_ac.h, mk[("vev", g - 1)])
                                _li = t.matmul(ps_v[0:1, 0:w], onescolb_sb[:],
                                         ysq[:, (it % 2) * 512:(it % 2) * 512 + w],
                                         start=(qq == 0), stop=(qq == 3))
                                _li.then_inc(S_pe.h, 1)
                                S_pe.n += 1
                                mk[("st", it)] = S_pe.n
                        @blk.scalar
                        def _(sc, g=g, w=w):
                            sc.wait_ge(S_pe.h, mk[("st", g * 4 + 3)])
                            if g > 0:
                                sc.wait_ge(S_pe.h, mk[("rbmm", g - 1)])
                            sc.activation(vrow[0:1, 0:w], ps_v[0:1, 0:w], AF.Copy)
                            need_set(sc, 6)
                            sc.activation(vrow[0:1, 0:w], vrow[0:1, 0:w], AF.Ln,
                                          bias=eps_sb[0:1, 0:1], scale=1.0 / 512)
                            _li = sc.activation(vrow[0:1, 0:w], vrow[0:1, 0:w], AF.Exp, scale=-0.5)
                            _li.then_inc(S_ac.h, 1)
                            S_ac.n += 1
                            mk[("vev", g)] = S_ac.n
                        @blk.tensor
                        def _(t, g=g, w=w):
                            t.wait_ge(S_ac.h, mk[("vev", g)])
                            if g >= 2:
                                t.wait_ge(S_dv.h, mk[("yn", (g - 2) * 4 + 3)])
                            _li = t.matmul(ps_rb[:, 0:w], ones1f_sb[:], vrow[0:1, 0:w],
                                     start=True, stop=True)
                            _li.then_inc(S_pe.h, 1)
                            S_pe.n += 1
                            mk[("rbmm", g)] = S_pe.n
                        @blk.scalar
                        def _(sc, g=g, w=w):
                            sc.wait_ge(S_pe.h, mk[("rbmm", g)])
                            sc.activation(rb_bc[:, (g % 2) * 512:(g % 2) * 512 + w],
                                          ps_rb[:, 0:w], AF.Copy).then_inc(S_ac.h, 1)
                            S_ac.n += 1
                            mk[("rbev", g)] = S_ac.n
                        for qq in range(4):
                            it = g * 4 + qq
                            @blk.sync
                            def _(s, it=it, off=off):
                                if it >= 2:
                                    s.wait_ge(S_dv.h, mk[("yn", it - 2)])
                                q = rsem("rgt", it)
                                s.dma_start(out=gtile[:, (it % 2) * 512:(it % 2) * 512 + 512],
                                            in_=gate_scr[it * 128:(it + 1) * 128, 3 + off: 3 + off + 512]
                                            ).then_inc(q.h, 16)
                                q.n += 16
                                mk[("gl", it)] = (q.h, q.n)
                            @blk.scalar
                            def _(sc, it=it):
                                sc.wait_ge(*mk[("gl", it)])
                                need_set(sc, 2)
                                sc.activation(gsig[:, (it % 2) * 512:(it % 2) * 512 + 512],
                                              gtile[:, (it % 2) * 512:(it % 2) * 512 + 512],
                                              AF.Sigmoid).then_inc(S_ac.h, 1)
                                S_ac.n += 1
                                mk[("gs", it)] = S_ac.n
                            @blk.vector
                            def _(v, it=it, g=g, off=off):
                                v.wait_ge(S_ac.h, mk[("gs", it)])
                                v.wait_ge(S_ac.h, mk[("rbev", g)])
                                v.tensor_tensor(t_scr[:, 0:512],
                                                xn_a[:, it * THp + off: it * THp + off + 512],
                                                rb_bc[:, (g % 2) * 512:(g % 2) * 512 + 512],
                                                op=ALU.mult)
                                v.tensor_scalar(t_scr[:, 0:512], t_scr[:, 0:512],
                                                nwcol_sb[:, it:it + 1], None, op0=ALU.mult)
                                v.tensor_tensor(t_scr[:, 0:512], t_scr[:, 0:512],
                                                gtile[:, (it % 2) * 512:(it % 2) * 512 + 512],
                                                op=ALU.mult)
                                _li = v.tensor_tensor(xn_a[:, it * THp + off: it * THp + off + 512],
                                                t_scr[:, 0:512],
                                                gsig[:, (it % 2) * 512:(it % 2) * 512 + 512],
                                                op=ALU.mult)
                                _li.then_inc(S_dv.h, 1)
                                S_dv.n += 1
                                mk[("yn", it)] = S_dv.n
                    for ct in range(32):
                        @blk.gpsimd
                        def _(g2, ct=ct):
                            if ct >= 2:
                                g2.wait_ge(S_pe.h, mk[("op", ct - 2)])
                            q = rsem("rw", ct)
                            g2.dma_start(out=wbuf[:, (ct % 2) * DM:(ct % 2 + 1) * DM],
                                         in_=wot[ct]).then_inc(q.h, 16)
                            q.n += 16
                            mk[("wo", ct)] = (q.h, q.n)
                        @blk.sync
                        def _(s, ct=ct, off=off):
                            if ct >= 2:
                                s.wait_ge(S_dv.h, mk[("x2", ct - 2)])
                                s.wait_ge(S_ac.h, mk[("x2b", ct - 2)])
                            q = rsem("rx1", ct)
                            s.dma_start(out=x1r[:, (ct % 2) * 512:(ct % 2) * 512 + 512],
                                        in_=x1_scr[ct * 128:(ct + 1) * 128, 3 + off: 3 + off + 512]
                                        ).then_inc(q.h, 16)
                            q.n += 16
                            mk[("x1l", ct)] = (q.h, q.n)
                        @blk.tensor
                        def _(t, ct=ct):
                            t.wait_ge(*mk[("wo", ct)])
                            t.wait_ge(S_dv.h, mk[("yn", 31)])
                            if ct >= 2:
                                t.wait_ge(S_dv.h, mk[("x2", ct - 2)])
                            bb = ps_o[ct % 2]
                            for it in range(32):
                                _li = t.matmul(bb[:],
                                         wbuf[:, (ct % 2) * DM + it * 128:(ct % 2) * DM + (it + 1) * 128],
                                         xn_a[:, it * THp + off: it * THp + off + 512],
                                         start=(it == 0), stop=(it == 31))
                            _li.then_inc(S_pe.h, 1)
                            S_pe.n += 1
                            mk[("op", ct)] = S_pe.n
                        @blk.vector
                        def _(v, ct=ct):
                            v.wait_ge(S_pe.h, mk[("op", ct)])
                            v.wait_ge(*mk[("x1l", ct)])
                            if ct >= 2:
                                v.wait_ge(*mk[("x2s", ct - 2)])
                                v.wait_ge(S_ac.h, mk[("x2b", ct - 2)])
                            _li = v.tensor_tensor(x1r[:, (ct % 2) * 512:(ct % 2) * 512 + 512],
                                            ps_o[ct % 2][:],
                                            x1r[:, (ct % 2) * 512:(ct % 2) * 512 + 512],
                                            op=ALU.add)
                            _li.then_inc(S_dv.h, 1)
                            S_dv.n += 1
                            mk[("x2", ct)] = S_dv.n
                        @blk.sync
                        def _(s, ct=ct, off=off):
                            s.wait_ge(S_dv.h, mk[("x2", ct)])
                            q = rsem("rx2", ct)
                            s.dma_start(out=x2_scr[ct * 128:(ct + 1) * 128, off:off + 512],
                                        in_=x1r[:, (ct % 2) * 512:(ct % 2) * 512 + 512]
                                        ).then_inc(q.h, 16)
                            q.n += 16
                            mk[("x2s", ct)] = (q.h, q.n)
                        @blk.scalar
                        def _(sc, ct=ct, off=off):
                            sc.wait_ge(S_dv.h, mk[("x2", ct)])
                            sc.activation(xn_b2[:, ct * THp + off: ct * THp + off + 512],
                                          x1r[:, (ct % 2) * 512:(ct % 2) * 512 + 512],
                                          AF.Copy).then_inc(S_ac.h, 1)
                            S_ac.n += 1
                            mk[("x2b", ct)] = S_ac.n
                    barrier()
            subw5 = [(0, 512), (512, 512)]
            if phases < 8:
                finish()
                return nc
            ln_stats(xn_b2, subw5)
            ln_fin_bcast_apply(xn_b2, T, subw5)
            barrier()
            if phases < 9:
                finish()
                return nc
            emit_ffn(xn_b2, x2_scr, outT, T, None, "f2")

        finish()

    return nc


_CACHE = {}


def _mask_idx():
    import jax
    with jax.default_device(jax.devices("cpu")[0]):
        u = np.asarray(jax.random.uniform(
            jax.random.key(42, impl="threefry2x32"), (2, 4096, 4096)))
    return ((u >= 0.25).astype(np.uint8) + (u >= 0.5) + (u >= 0.75)).astype(np.uint8)


def _tile_lhsT(Wt):
    Ktot, Mtot = Wt.shape
    Kt, Mt = Ktot // 128, Mtot // 128
    return np.ascontiguousarray(
        Wt.reshape(Kt, 128, Mt, 128).transpose(2, 1, 0, 3).reshape(Mt, 128, Kt * 128))


def prepare_inputs(L0_attn_hidden_states, L0_attn_in_proj_w, L0_attn_conv_weight,
                   L0_attn_conv_bias, L0_attn_dt_bias, L0_attn_A_log, L0_attn_D,
                   L0_attn_norm_weight, L0_attn_out_proj_w, L0_ffn_W0, L0_ffn_b0,
                   L0_ffn_A, L0_ffn_B):
    f32 = np.float32
    x0 = np.asarray(L0_attn_hidden_states, f32)
    Bsz, L, _ = x0.shape
    x_flat = np.ascontiguousarray(x0.reshape(Bsz * L, DM))
    idx = _mask_idx().reshape(Bsz * L, I)

    Bm = np.asarray(L0_ffn_B, f32)
    wbts = np.ascontiguousarray(
        Bm.transpose(2, 0, 1).reshape(16, 4, 32, 128).transpose(2, 0, 1, 3).reshape(32, 16, 512))

    shared = {
        "w0t": _tile_lhsT(np.ascontiguousarray(np.asarray(L0_ffn_W0, f32).T)),
        "wpt": _tile_lhsT(np.ascontiguousarray(
            np.pad(np.asarray(L0_attn_in_proj_w, f32), ((0, 81 * 128 - PROJ), (0, 0))).T)),
        "wot": _tile_lhsT(np.ascontiguousarray(np.asarray(L0_attn_out_proj_w, f32).T)),
        "wat": np.ascontiguousarray(
            np.asarray(L0_ffn_A, f32).transpose(2, 0, 1).reshape(32, 128, 4, 16)
            .transpose(1, 2, 0, 3).reshape(128, 2048)),
        "wbts": wbts,
        "b0c": np.ascontiguousarray(np.asarray(L0_ffn_b0, f32).reshape(32, 128).T),
        "convw": np.ascontiguousarray(
            np.asarray(L0_attn_conv_weight, f32).reshape(48, 128, 4).transpose(1, 0, 2).reshape(128, 192)),
        "convb": np.ascontiguousarray(np.asarray(L0_attn_conv_bias, f32).reshape(48, 128).T),
        "dtbias": np.ascontiguousarray(np.broadcast_to(np.asarray(L0_attn_dt_bias, f32), (128, 64))),
        "alogb": np.ascontiguousarray(np.broadcast_to(np.asarray(L0_attn_A_log, f32), (128, 64))),
        "dcolh": np.ascontiguousarray(np.repeat(np.asarray(L0_attn_D, f32), 64).reshape(32, 128).T),
        "nwcolh": np.ascontiguousarray(np.asarray(L0_attn_norm_weight, f32).reshape(32, 128).T),
        "identb": np.eye(128, dtype=ml_dtypes.bfloat16),
        "identf": np.eye(128, dtype=f32),
        "ones1f": np.ones((1, 128), f32),
        "onescolb": np.ones((128, 1), ml_dtypes.bfloat16),
        "onesrowb": np.ones((1, 512), ml_dtypes.bfloat16),
        "ctrif": np.ascontiguousarray(-np.triu(np.ones((128, 128), f32))),
        "k01f": np.ascontiguousarray(np.triu(np.ones((128, 128), f32))),
    }

    in_maps = []
    for core in range(8):
        b, r = core // 4, core % 4
        r0 = b * L + r * T
        rows = np.zeros((TH, DM), f32)
        mrows = np.zeros((TH, I), np.uint8)
        if r == 0:
            rows[3:] = x_flat[r0:r0 + T]
            mrows[3:] = idx[r0:r0 + T]
        else:
            rows[:] = x_flat[r0 - 3:r0 + T]
            mrows[:] = idx[r0 - 3:r0 + T]
        d = dict(shared)
        d["xT"] = np.ascontiguousarray(rows.T)
        d["m1T"] = np.ascontiguousarray((mrows == 1).astype(np.uint8).T)
        d["m2T"] = np.ascontiguousarray((mrows == 2).astype(np.uint8).T)
        d["m3T"] = np.ascontiguousarray((mrows == 3).astype(np.uint8).T)
        d["halom"] = np.full((128, 1), 0.0 if r == 0 else 1.0, f32)
        act = np.zeros((1, 256), f32)
        for k2 in range(4):
            act[0, k2 * 64:(k2 + 1) * 64] = 1.0 if k2 < r else 0.0
        d["actrow"] = act
        d["epsr"] = np.full((1, 1), EPS, f32)
        d["mselrow"] = act.copy()
        in_maps.append(d)
    return in_maps


def kernel(**inputs):
    in_maps = prepare_inputs(**inputs)
    Bsz = 2
    L = 4096
    f32 = np.float32
    if "nc" not in _CACHE:
        _CACHE["nc"] = build_program()
    res = run_bass_kernel_spmd(_CACHE["nc"], in_maps, core_ids=list(range(8)))

    out = np.empty((Bsz * L, DM), f32)
    for core in range(8):
        b, r = core // 4, core % 4
        r0 = b * L + r * T
        out[r0:r0 + T] = res.results[core]["outT"].T
    return out.reshape(Bsz, L, DM)
